# version 40
# baseline (speedup 1.0000x reference)
import contextlib
import numpy as np
import concourse.bass as bass
import concourse.mybir as mybir
from concourse.bass_utils import run_bass_kernel_spmd

F32 = mybir.dt.float32
BF16 = mybir.dt.bfloat16
I32 = mybir.dt.int32
AF = mybir.ActivationFunctionType
ALU = mybir.AluOpType
AX = mybir.AxisListType

FULL_CFG = dict(NCORES=8, T=4096, S=8192, D=1024, F=4096, SWQ=2048, KBG=512)


class Sem:
    def __init__(self, st, nc, name):
        self.h = st.enter_context(nc.semaphore(name))
        self.n = 0

    def inc(self, inst, k=1):
        inst.then_inc(self.h, k)
        self.n += k
        return self.n


def build_nc(cfg):
    NCORES = cfg["NCORES"]
    T, S, D, F = cfg["T"], cfg["S"], cfg["D"], cfg["F"]
    SWQ, KBG = cfg["SWQ"], cfg["KBG"]
    DC = D // 128
    FC = F // 128
    NTIL = S // 128
    NSW = T // SWQ
    NQT = SWQ // 128
    NQG = SWQ // 512
    NKBG = S // KBG
    KC = KBG // 128
    NDV = (D + 511) // 512
    DVW = min(D, 512)
    NTT = T // 512
    TCH = 4
    assert SWQ % 512 == 0 and T % 512 == 0 and D % 128 == 0 and F % 128 == 0

    nc = bass.Bass("TRN2", target_bir_lowering=False)

    xb = nc.dram_tensor("xb", [S, D], F32, kind="ExternalInput").ap()
    w_in = nc.dram_tensor("w_in", [D, F], F32, kind="ExternalInput").ap()
    b_in = nc.dram_tensor("b_in", [1, F], F32, kind="ExternalInput").ap()
    w_out = nc.dram_tensor("w_out", [F, D], F32, kind="ExternalInput").ap()
    b_out = nc.dram_tensor("b_out", [1, D], F32, kind="ExternalInput").ap()
    out = nc.dram_tensor("out", [T, D], F32, kind="ExternalOutput").ap()

    vtm = nc.dram_tensor("vtm", [S, D], BF16).ap()
    kfm = nc.dram_tensor("kfm", [D, S], BF16).ap()
    x2h = nc.dram_tensor("x2h", [T, D], F32).ap()
    t_h = nc.dram_tensor("t_h", [T, D], BF16).ap()
    h_h = nc.dram_tensor("h_h", [F, T], BF16).ap()
    scr = nc.dram_tensor("scr", [1, F], F32).ap()
    cc1i = nc.dram_tensor("cc1i", [1, 128], F32).ap()
    cc1o = nc.dram_tensor("cc1o", [1, 128], F32).ap()
    cc2i = nc.dram_tensor("cc2i", [1, 128], F32).ap()
    cc2o = nc.dram_tensor("cc2o", [1, 128], F32).ap()

    vtm3 = vtm.rearrange("(c p) d -> p c d", p=128)
    kfm3 = kfm.rearrange("(c p) s -> p c s", p=128)
    hh3 = h_h.rearrange("(c p) t -> p c t", p=128)
    binv = b_in.rearrange("o (c p) -> (o c) p", p=128)
    scrv = scr.rearrange("o (c p) -> (o c) p", p=128)

    st = contextlib.ExitStack()
    E = st.enter_context
    with st:
        c_eps = E(nc.sbuf_tensor("c_eps", [128, 1], F32))
        c_nshift = E(nc.sbuf_tensor("c_nshift", [128, 1], F32))
        c_zero = E(nc.sbuf_tensor("c_zero", [128, 1], F32))
        ones_row = E(nc.sbuf_tensor("ones_row", [1, 128], F32))
        ones_col = E(nc.sbuf_tensor("ones_col", [128, 1], BF16))
        iota_j = E(nc.sbuf_tensor("iota_j", [128, 128], F32))
        iota_p = E(nc.sbuf_tensor("iota_p", [128, 128], F32))
        ident = E(nc.sbuf_tensor("ident", [128, 128], F32))
        ident_bf = E(nc.sbuf_tensor("ident_bf", [128, 128], BF16))
        amax1a = E(nc.sbuf_tensor("amax1a", [128, 1], F32))
        amax2a = E(nc.sbuf_tensor("amax2a", [128, 1], F32))
        mrow = E(nc.sbuf_tensor("mrow", [1, 128], F32))
        mtop = E(nc.sbuf_tensor("mtop", [1, 1], F32))
        screp = E(nc.sbuf_tensor("screp", [1, 128], F32))
        gmax = E(nc.sbuf_tensor("gmax", [1, 1], F32))
        s1_bc = E(nc.sbuf_tensor("s1_bc", [128, 1], F32))
        inv1_bc = E(nc.sbuf_tensor("inv1_bc", [128, 1], F32))
        s2_bc = E(nc.sbuf_tensor("s2_bc", [128, 1], F32))
        inv2_bc = E(nc.sbuf_tensor("inv2_bc", [128, 1], F32))
        ab_bc = E(nc.sbuf_tensor("ab_bc", [128, 1], F32))
        colmax_pc = E(nc.sbuf_tensor("colmax_pc", [128, FC], F32))
        ginvw_pc = E(nc.sbuf_tensor("ginvw_pc", [128, FC], F32))
        binpc = E(nc.sbuf_tensor("binpc", [128, FC], F32))
        bout_bc = E(nc.sbuf_tensor("bout_bc", [128, D], F32))
        l_sb = E(nc.sbuf_tensor("l_sb", [128, max(NQT, 1)], F32))
        linv = E(nc.sbuf_tensor("linv", [128, max(NQT, 1)], F32))

        ps_st = []
        ps_o = []
        HP = []

        sem_names = ["ms", "ss", "vs", "gs", "bar", "ccs",
                     "d_vtm0", "d_vtm1", "d_vtm2", "d_vtm3",
                     "d_kfm0", "d_kfm1", "d_kfm2", "d_kfm3",
                     "d_xin0", "d_xin1", "d_xin2", "d_xin3",
                     "d_xin4", "d_xin5", "d_xin6", "d_xin7",
                     "d_q", "d_k0", "d_k1", "d_v0", "d_v1",
                     "d_xq0", "d_xq1", "d_xq2", "d_xq3", "d_xq4", "d_xq5",
                     "d_x2w0", "d_tw0", "d_tw1", "d_tw2", "d_tw3",
                     "d_w0", "d_w1", "d_w2", "d_w3", "d_scr", "d_bio", "d_tfm0", "d_tfm1",
                     "d_hw0", "d_hw1", "d_hr0", "d_hr1", "d_xr0", "d_xr1", "d_cc",
                     "d_out0", "d_out1"]
        SM = {name: Sem(st, nc, name) for name in sem_names}
        ms, ss, vs, gs, bar = SM["ms"], SM["ss"], SM["vs"], SM["gs"], SM["bar"]

        blk = E(nc.Block())
        nbar = [0]

        def barrier(extra=()):
            @blk.sync
            def _(y):
                for sm, val in extra:
                    y.wait_ge(sm.h, val)
                bar.inc(y.nop())
                nbar[0] += 1
                y.wait_ge(bar.h, 5 * nbar[0])

            for engname, own in [("vector", vs), ("scalar", ss),
                                 ("tensor", ms), ("gpsimd", gs)]:
                @getattr(blk, engname)
                def _(e, own=own):
                    e.wait_ge(own.h, own.n)
                    bar.inc(e.nop())
                    e.wait_ge(bar.h, 5 * nbar[0])

        NCH = (D + 511) // 512

        def make_stats_bufs(Ep, prefix, n=2):
            d = {k: [Ep(nc.sbuf_tensor(f"{prefix}_{k}{j}", [128, 1], F32))
                     for j in range(n)]
                 for k in ["lnv", "rstd", "nb"]}
            d["s12"] = [Ep(nc.sbuf_tensor(f"{prefix}_s12{j}", [128, 6 * NCH],
                                          F32)) for j in range(n)]
            d["mv"] = [Ep(nc.sbuf_tensor(f"{prefix}_mv{j}", [128, 2], F32))
                       for j in range(n)]
            return d

        def emit_ln_stats(src_ap, sq_scr, bufs, j, W, wait_v=(), wait_s=(),
                          sq_on=None, prev_use=None):
            nch = (W + 511) // 512
            s12 = bufs["s12"][j]
            mv = bufs["mv"][j]

            @blk.vector
            def _(v):
                for smh, val in wait_v:
                    v.wait_ge(smh, val)
                if prev_use is not None:
                    v.wait_ge(prev_use[0].h, prev_use[1])
                for ch in range(nch):
                    w0 = ch * 512
                    w1 = min(W, w0 + 512)
                    vs.inc(v.bn_stats(s12[:, 6 * ch:6 * ch + 6],
                                      src_ap[:, w0:w1]))
                v.wait_ge(vs.h, vs.n)
                vs.inc(v.bn_aggr(mv[:], s12[:, 0:6 * nch]))
            mv_c = vs.n

            @blk.scalar
            def _(s):
                s.wait_ge(vs.h, mv_c)
                ss.inc(s.activation(bufs["lnv"][j][:], mv[:, 1:2], AF.Ln,
                                    scale=1.0, bias=c_eps[:]))
                s.wait_ge(ss.h, ss.n)
                ss.inc(s.activation(bufs["rstd"][j][:], bufs["lnv"][j][:],
                                    AF.Exp, scale=-0.5, bias=c_zero[:]))
            rstd_c = ss.n

            @blk.vector
            def _(v):
                v.wait_ge(ss.h, rstd_c)
                vs.inc(v.tensor_scalar(bufs["nb"][j][:], mv[:, 0:1],
                                       bufs["rstd"][j][:], -1.0,
                                       op0=ALU.mult, op1=ALU.mult))
            nb_c = vs.n
            return rstd_c, nb_c

        def emit_ar_part1(acc_tile, cci, cco, last_vs_for_acc, pbank):
            @blk.tensor
            def _(t):
                t.wait_ge(vs.h, last_vs_for_acc)
                ms.inc(t.transpose(pbank[0:1, 0:128], acc_tile[:], ident[:]))
            tr_c = ms.n

            @blk.vector
            def _(v):
                v.wait_ge(ms.h, tr_c)
                vs.inc(v.tensor_copy(mrow[:], pbank[0:1, 0:128]))
                v.wait_ge(vs.h, vs.n)
                vs.inc(v.reduce_max(mtop[:], mrow[:], axis=AX.X))
                v.wait_ge(vs.h, vs.n)
                vs.inc(v.tensor_scalar(screp[:], mrow[:], 0.0, mtop[0:1, 0:1],
                                       op0=ALU.mult, op1=ALU.add))
            rep_c = vs.n

            @blk.gpsimd
            def _(g):
                g.wait_ge(vs.h, rep_c)
                SM["d_cc"].inc(g.dma_start(out=cci, in_=screp[:]), 16)
                g.wait_ge(SM["d_cc"].h, SM["d_cc"].n)
                SM["ccs"].inc(g.collective_compute(
                    "AllReduce", ALU.max, replica_groups=[list(range(NCORES))],
                    ins=[cci], outs=[cco]))
                g.wait_ge(SM["ccs"].h, SM["ccs"].n)
                SM["d_cc"].inc(g.dma_start(out=gmax[:], in_=cco[0:1, 0:1]), 16)
            return (SM["d_cc"].n, rep_c)

        def emit_ar_part2(s_out, inv_out, qconst, cc_in, pbank):
            cc_c, vsafe = cc_in

            @blk.tensor
            def _(t):
                t.wait_ge(SM["d_cc"].h, cc_c)
                t.wait_ge(vs.h, vsafe)
                ms.inc(t.matmul(pbank[:, 0:1], ones_row[0:1, :], gmax[0:1, :],
                                start=True, stop=True))
            bc_c = ms.n

            @blk.vector
            def _(v):
                v.wait_ge(ms.h, bc_c)
                vs.inc(v.tensor_copy(ab_bc[:], pbank[:, 0:1]))
                v.wait_ge(vs.h, vs.n)
                vs.inc(v.reciprocal(s_out[:], ab_bc[:]))
                v.wait_ge(vs.h, vs.n)
                vs.inc(v.tensor_scalar(s_out[:], s_out[:], float(qconst), None,
                                       op0=ALU.mult))
                vs.inc(v.tensor_scalar(inv_out[:], ab_bc[:], 1.0 / float(qconst),
                                       None, op0=ALU.mult))
            return vs.n

        from concourse.hw_specs import get_activation_tables
        _tab_names = list(get_activation_tables(nc.m.arch).keys())
        _SET_MAIN = _tab_names.index("natural_log_exp_and_others")
        _SET_GELU = _tab_names.index("gelu_and_others")

        def load_act_set(s, set_id):
            s.add_instruction(mybir.InstLoadActFuncSet(
                name=nc.get_next_instruction_name(),
                act_func_set_id=set_id, ins=[], outs=[]))

        @blk.scalar
        def _(s):
            load_act_set(s, _SET_MAIN)

        @blk.gpsimd
        def _(g):
            g.memset(c_eps[:], 1e-5)
            g.memset(c_nshift[:], -float(D))
            g.memset(c_zero[:], 0.0)
            g.memset(ones_row[:], 1.0)
            g.memset(ones_col[:], 1.0)
            g.memset(amax1a[:], 1e-8)
            g.memset(amax2a[:], 1e-8)
            g.iota(iota_j[:], [[1, 128]], channel_multiplier=0,
                   allow_small_or_imprecise_dtypes=True)
            gs.inc(g.iota(iota_p[:], [[0, 128]], channel_multiplier=1,
                          allow_small_or_imprecise_dtypes=True))
        init_g = gs.n

        @blk.vector
        def _(v):
            v.wait_ge(gs.h, init_g)
            vs.inc(v.tensor_tensor(ident[:], iota_j[:], iota_p[:], op=ALU.is_equal))
            v.wait_ge(vs.h, vs.n)
            vs.inc(v.tensor_copy(ident_bf[:], ident[:]))

        barrier()

        with contextlib.ExitStack() as p0:
            Ep = p0.enter_context
            xin2 = [Ep(nc.sbuf_tensor(f"xin2_{j}", [128, D], F32))
                    for j in range(8)]
            vt2 = [Ep(nc.sbuf_tensor(f"vt2_{j}", [128, D], BF16))
                   for j in range(4)]
            kst2 = [Ep(nc.sbuf_tensor(f"kst2_{j}", [128, D], BF16))
                    for j in range(4)]
            p0b = make_stats_bufs(Ep, "p0", n=16)

            ps_tp = [Ep(nc.psum_tensor(f"ps_tp{j}", [128, D], BF16))
                     for j in range(2)]
            norm_done = {}
            rstd_done = {}
            nb_done = {}
            mv_done = {}
            ld_done = {}
            tpose_done = {}
            kcopy_done = {}

            G0 = 8
            for g0 in range(0, NTIL, G0):
                grp = range(g0, min(NTIL, g0 + G0))

                for i in grp:
                    dxin = SM[f"d_xin{i % 8}"]

                    @blk.sync
                    def _(y, i=i, dxin=dxin, nd=norm_done.get(i - 8)):
                        if nd:
                            y.wait_ge(nd[0].h, nd[1])
                        dxin.inc(y.dma_start(
                            out=xin2[i % 8][:],
                            in_=xb[i * 128:(i + 1) * 128, :]), 16)
                    ld_done[i] = (dxin, dxin.n)

                for i in grp:
                    e = i % 16

                    @blk.vector
                    def _(v, i=i, e=e, ld=ld_done[i],
                          pr=rstd_done.get(i - 16)):
                        v.wait_ge(ld[0].h, ld[1])
                        if pr:
                            v.wait_ge(ss.h, pr)
                        for ch in range(NCH):
                            vs.inc(v.bn_stats(
                                p0b["s12"][e][:, 6 * ch:6 * ch + 6],
                                xin2[i % 8][:, ch * 512:min(D, (ch + 1) * 512)]))
                        v.wait_ge(vs.h, vs.n)
                        vs.inc(v.bn_aggr(p0b["mv"][e][:],
                                         p0b["s12"][e][:, 0:6 * NCH]))
                    mv_done[i] = vs.n

                for i in grp:
                    e = i % 16

                    @blk.scalar
                    def _(s, i=i, e=e):
                        s.wait_ge(vs.h, mv_done[i])
                        ss.inc(s.activation(p0b["lnv"][e][:],
                                            p0b["mv"][e][:, 1:2], AF.Ln,
                                            scale=1.0, bias=c_eps[:]))
                        s.wait_ge(ss.h, ss.n)
                        ss.inc(s.activation(p0b["rstd"][e][:],
                                            p0b["lnv"][e][:], AF.Exp,
                                            scale=-0.5, bias=c_zero[:]))
                    rstd_done[i] = ss.n

                for i in grp:
                    e = i % 16

                    @blk.vector
                    def _(v, i=i, e=e):
                        v.wait_ge(ss.h, rstd_done[i])
                        vs.inc(v.tensor_scalar(p0b["nb"][e][:],
                                               p0b["mv"][e][:, 0:1],
                                               p0b["rstd"][e][:], -1.0,
                                               op0=ALU.mult, op1=ALU.mult))
                    nb_done[i] = vs.n

                for i in grp:
                    e = i % 16
                    dvtm = SM[f"d_vtm{i % 4}"]
                    dkf = SM[f"d_kfm{i % 4}"]

                    if i % 2 == 0:
                        @blk.scalar
                        def _(s, i=i, e=e, dvtm=dvtm, dv_c=dvtm.n,
                              td=tpose_done.get(i - 4)):
                            s.wait_ge(vs.h, nb_done[i])
                            if td:
                                s.wait_ge(dvtm.h, dv_c)
                                s.wait_ge(ms.h, td)
                            ss.inc(s.activation(vt2[i % 4][:], xin2[i % 8][:],
                                                AF.Identity,
                                                scale=p0b["rstd"][e][:],
                                                bias=p0b["nb"][e][:]))
                        norm_done[i] = (ss, ss.n)
                    else:
                        @blk.vector
                        def _(v, i=i, e=e, dvtm=dvtm, dv_c=dvtm.n,
                              td=tpose_done.get(i - 4)):
                            v.wait_ge(vs.h, nb_done[i])
                            if td:
                                v.wait_ge(dvtm.h, dv_c)
                                v.wait_ge(ms.h, td)
                            vs.inc(v.tensor_scalar(vt2[i % 4][:],
                                                   xin2[i % 8][:],
                                                   p0b["rstd"][e][:],
                                                   p0b["nb"][e][:],
                                                   op0=ALU.mult, op1=ALU.add))
                        norm_done[i] = (vs, vs.n)

                    @blk.gpsimd
                    def _(g, i=i, dvtm=dvtm, nd=norm_done[i]):
                        g.wait_ge(nd[0].h, nd[1])
                        dvtm.inc(g.dma_start(out=vtm3[:, i, :],
                                             in_=vt2[i % 4][:]), 16)

                    @blk.tensor
                    def _(t, i=i, nd=norm_done[i], kd=kcopy_done.get(i - 2)):
                        t.wait_ge(nd[0].h, nd[1])
                        if kd:
                            t.wait_ge(kd[0].h, kd[1])
                        for dc in range(DC):
                            mm = t.transpose(
                                ps_tp[i % 2][:, dc * 128:(dc + 1) * 128],
                                vt2[i % 4][:, dc * 128:(dc + 1) * 128],
                                ident_bf[:])
                        ms.inc(mm)
                    tpose_done[i] = ms.n

                    if i % 2 == 1:
                        @blk.scalar
                        def _(s, i=i, dkf=dkf, dk_c=dkf.n):
                            s.wait_ge(ms.h, tpose_done[i])
                            if i >= 4:
                                s.wait_ge(dkf.h, dk_c)
                            ss.inc(s.activation(kst2[i % 4][:],
                                                ps_tp[i % 2][:], AF.Copy,
                                                bias=0.0, scale=1.0))
                        kcopy_done[i] = (ss, ss.n)
                    else:
                        @blk.vector
                        def _(v, i=i, dkf=dkf, dk_c=dkf.n):
                            v.wait_ge(ms.h, tpose_done[i])
                            if i >= 4:
                                v.wait_ge(dkf.h, dk_c)
                            vs.inc(v.tensor_copy(kst2[i % 4][:],
                                                 ps_tp[i % 2][:]))
                        kcopy_done[i] = (vs, vs.n)

                    @blk.gpsimd
                    def _(g, i=i, dkf=dkf, kd=kcopy_done[i]):
                        g.wait_ge(kd[0].h, kd[1])
                        dkf.inc(g.dma_start(
                            out=kfm3[:, :, i * 128:(i + 1) * 128],
                            in_=kst2[i % 4][:].rearrange("p (c k) -> p c k",
                                                         c=DC)), 16)

            barrier(extra=[(SM[f"d_kfm{j}"], SM[f"d_kfm{j}"].n)
                           for j in range(4)] +
                          [(SM[f"d_vtm{j}"], SM[f"d_vtm{j}"].n)
                           for j in range(4)])

        ps_st.extend(E(nc.psum_tensor(f"ps_st{i}", [128, 512], F32))
                     for i in range(4))
        ps_o.extend(E(nc.psum_tensor(f"ps_o{i}", [128, 1024], F32))
                    for i in range(2))
        HP.extend([ps_st[0][:], ps_st[1][:], ps_st[2][:], ps_st[3][:],
                   ps_o[0][:, 0:512], ps_o[0][:, 512:1024],
                   ps_o[1][:, 0:512], ps_o[1][:, 512:1024]])

        with contextlib.ExitStack() as p2:
            Ep = p2.enter_context
            Qt = Ep(nc.sbuf_tensor("Qt", [128, DC, SWQ], BF16))
            K2 = [Ep(nc.sbuf_tensor(f"K2_{j}", [128, DC, KBG], BF16)) for j in range(2)]
            V2 = [Ep(nc.sbuf_tensor(f"V2_{j}", [128, KC, D], BF16)) for j in range(2)]
            PT = Ep(nc.sbuf_tensor("PT", [128, KC, NQG, 512], BF16))
            O_sb = [Ep(nc.sbuf_tensor(f"O_sb{q}", [128, D], F32)) for q in range(NQT)]
            xr = [Ep(nc.sbuf_tensor(f"xr_{j}", [128, D], F32)) for j in range(6)]
            ln12 = [Ep(nc.sbuf_tensor(f"ln12_{j}", [128, D], F32)) for j in range(4)]
            tt2 = [Ep(nc.sbuf_tensor(f"tt2_{j}", [128, D], BF16)) for j in range(4)]
            tmax2 = [Ep(nc.sbuf_tensor(f"tmax2_{j}", [128, 1], F32)) for j in range(16)]
            linv2 = Ep(nc.sbuf_tensor("linv2", [128, max(NQT, 1)], F32))
            eb1 = make_stats_bufs(Ep, "eb1", n=16)
            eb2 = make_stats_bufs(Ep, "eb2", n=16)

            npp = (NQG + 1) // 2
            st_seq = []
            for pp in range(npp):
                qgs = [g for g in (2 * pp, 2 * pp + 1) if g < NQG]
                for kc in range(KC):
                    st_seq.append((pp, kc, qgs,
                                   [2 * gi + kc % 2 for gi in range(len(qgs))]))
            exps_per_kbg = sum(len(e[2]) for e in st_seq)

            bank_exp = {0: 0, 1: 0, 2: 0, 3: 0}
            bank_oev = {0: 0, 1: 0}
            l_evac_c = 0
            kv_used = {}
            att_prev = {}
            att_prev_dma = 0
            ep_counts = {}
            xq_map = {}
            amax_upd = 0

            for sw in range(NSW):
                q0 = sw * SWQ

                @blk.sync
                def _(y, sw=sw, q0=q0):
                    if sw > 0:
                        y.wait_ge(ms.h, kv_used[(sw - 1, NKBG - 1)])
                    SM["d_q"].inc(y.dma_start(out=Qt[:],
                                              in_=kfm3[:, :, q0:q0 + SWQ]), 16)
                q_c = SM["d_q"].n

                oev = {}
                for kbg in range(NKBG):
                    j = kbg % 2
                    k0 = kbg * KBG
                    dk, dv = SM[f"d_k{j}"], SM[f"d_v{j}"]

                    @blk.sync
                    def _(y, sw=sw, kbg=kbg, j=j, k0=k0, dk=dk, dv=dv):
                        key = (sw, kbg - 2) if kbg >= 2 else \
                              (sw - 1, NKBG - 2 + kbg)
                        if key in kv_used:
                            y.wait_ge(ms.h, kv_used[key])
                        dk.inc(y.dma_start(out=K2[j][:],
                                           in_=kfm3[:, :, k0:k0 + KBG]), 16)
                        dv.inc(y.dma_start(
                            out=V2[j][:],
                            in_=vtm[k0:k0 + KBG, :].rearrange(
                                "(c p) d -> p c d", p=128)), 16)
                    k_c, v_c = dk.n, dv.n

                    ss_base = ss.n
                    ms_base = ms.n
                    exp_cnt = {}
                    st_done = {}
                    run = 0
                    for idx, (pp, kc, qgs, banks) in enumerate(st_seq):
                        st_done[(pp, kc)] = ms_base + idx + 1
                        for gi in range(len(qgs)):
                            run += 1
                            exp_cnt[(pp, kc, gi)] = ss_base + run
                    pv_base = ms_base + len(st_seq)
                    vs_pv_base = vs.n

                    bank_last = dict(bank_exp)

                    @blk.tensor
                    def _(t, j=j, k_c=k_c, q_c=q_c, bank_last=bank_last,
                          lec=l_evac_c):
                        t.wait_ge(SM["d_q"].h, q_c)
                        t.wait_ge(SM[f"d_k{j}"].h, k_c)
                        if lec:
                            t.wait_ge(vs.h, lec)
                        for (pp, kc, qgs, banks) in st_seq:
                            for bk in banks:
                                if bank_last[bk]:
                                    t.wait_ge(ss.h, bank_last[bk])
                            for dc in range(DC):
                                lhs = K2[j][:, dc, kc * 128:(kc + 1) * 128]
                                for gi, qg in enumerate(qgs):
                                    mm = t.matmul(
                                        ps_st[banks[gi]][:], lhs,
                                        Qt[:, dc, qg * 512:(qg + 1) * 512],
                                        start=(dc == 0), stop=(dc == DC - 1))
                            ms.inc(mm)
                            for gi, bk in enumerate(banks):
                                bank_last[bk] = exp_cnt[(pp, kc, gi)]
                    assert ms.n == pv_base

                    @blk.scalar
                    def _(s):
                        for (pp, kc, qgs, banks) in st_seq:
                            s.wait_ge(ms.h, st_done[(pp, kc)])
                            for gi, qg in enumerate(qgs):
                                ss.inc(s.activation(
                                    PT[:, kc, qg, :], ps_st[banks[gi]][:],
                                    AF.Exp, bias=c_nshift[:], scale=1.0))
                    assert ss.n == ss_base + exps_per_kbg
                    exp_all = ss.n
                    for (pp, kc, qgs, banks) in st_seq:
                        for gi, bk in enumerate(banks):
                            bank_exp[bk] = exp_cnt[(pp, kc, gi)]

                    @blk.tensor
                    def _(t, sw=sw, kbg=kbg, j=j, v_c=v_c, exp_all=exp_all,
                          vs_pv_base=vs_pv_base, boev=dict(bank_oev)):
                        t.wait_ge(ss.h, exp_all)
                        t.wait_ge(SM[f"d_v{j}"].h, v_c)
                        for qt in range(NQT):
                            qg, qc = divmod(qt, 4)
                            ob = qt % 2
                            if qt >= 2:
                                t.wait_ge(vs.h, vs_pv_base + (qt - 2) + 1)
                            elif boev[ob]:
                                t.wait_ge(vs.h, boev[ob])
                            for kc in range(KC):
                                lhs = PT[:, kc, qg, qc * 128:(qc + 1) * 128]
                                for dvt in range(NDV):
                                    t.matmul(
                                        ps_o[ob][:, dvt * 512:dvt * 512 + DVW],
                                        lhs,
                                        V2[j][:, kc, dvt * 512:dvt * 512 + DVW],
                                        start=(kc == 0), stop=(kc == KC - 1))
                                mm = t.matmul(ps_st[0][:, qt:qt + 1], lhs,
                                              ones_col[:], start=(kc == 0),
                                              stop=(kc == KC - 1))
                            ms.inc(mm)
                    assert ms.n == pv_base + NQT
                    kv_used[(sw, kbg)] = ms.n

                    @blk.vector
                    def _(v, sw=sw, kbg=kbg, pv_base=pv_base, lec=l_evac_c):
                        for qt in range(NQT):
                            ob = qt % 2
                            v.wait_ge(ms.h, pv_base + qt + 1)
                            if kbg > 0:
                                v.wait_ge(vs.h, oev[(kbg - 1, qt)])
                            elif sw > 0:
                                apm, apc = att_prev[qt]
                                v.wait_ge(apm.h, apc)
                                v.wait_ge(SM["d_x2w0"].h, att_prev_dma)
                            if kbg == 0:
                                vs.inc(v.tensor_copy(O_sb[qt][:],
                                                     ps_o[ob][:, 0:D]))
                            else:
                                vs.inc(v.tensor_tensor(O_sb[qt][:], O_sb[qt][:],
                                                       ps_o[ob][:, 0:D],
                                                       op=ALU.add))
                            oev[(kbg, qt)] = vs.n
                            bank_oev[ob] = vs.n
                        if lec:
                            v.wait_ge(vs.h, lec)
                        if kbg == 0:
                            vs.inc(v.tensor_copy(l_sb[:, 0:NQT],
                                                 ps_st[0][:, 0:NQT]))
                        else:
                            vs.inc(v.tensor_tensor(l_sb[:, 0:NQT], l_sb[:, 0:NQT],
                                                   ps_st[0][:, 0:NQT],
                                                   op=ALU.add))
                    l_evac_c = vs.n
                    assert vs.n == vs_pv_base + NQT + 1

                @blk.vector
                def _(v, lec=l_evac_c):
                    v.wait_ge(vs.h, lec)
                    vs.inc(v.reciprocal(linv[:, 0:NQT], l_sb[:, 0:NQT]))
                    v.wait_ge(vs.h, vs.n)
                    vs.inc(v.tensor_tensor(linv2[:, 0:NQT], linv[:, 0:NQT],
                                           linv[:, 0:NQT], op=ALU.mult))
                linv2_c = vs.n

                gqs = [sw * NQT + qt for qt in range(NQT)]
                mv1_c = {}; rstd_c = {}; bq_c = {}; ln1_c = {}
                x2_c = {}; nb2_c = {}; tn_c = {}

                def emit_xq(qt_l):
                    gql = gqs[qt_l]
                    dxql = SM[f"d_xq{gql % 6}"]

                    @blk.gpsimd
                    def _(g, qt_l=qt_l, gql=gql, dxql=dxql):
                        if (gql - 6, "x2add") in ep_counts:
                            sm2, c2 = ep_counts[(gql - 6, "x2add")]
                            g.wait_ge(sm2.h, c2)
                        dxql.inc(g.dma_start(
                            out=xr[gql % 6][:],
                            in_=xb[q0 + qt_l * 128:
                                   q0 + (qt_l + 1) * 128, :]), 16)
                    xq_map[gql] = (dxql, dxql.n)

                for qt in range(min(6, NQT)):
                    emit_xq(qt)

                for qt in range(NQT):
                    gq = gqs[qt]
                    e = gq % 16

                    @blk.vector
                    def _(v, qt=qt, gq=gq, e=e):
                        if (gq - 16, "rstd1") in ep_counts:
                            sm2, c2 = ep_counts[(gq - 16, "rstd1")]
                            v.wait_ge(sm2.h, c2)
                        for ch in range(NCH):
                            vs.inc(v.bn_stats(
                                eb1["s12"][e][:, 6 * ch:6 * ch + 6],
                                O_sb[qt][:, ch * 512:min(D, (ch + 1) * 512)]))
                        v.wait_ge(vs.h, vs.n)
                        vs.inc(v.bn_aggr(eb1["mv"][e][:],
                                         eb1["s12"][e][:, 0:6 * NCH]))
                    mv1_c[gq] = vs.n

                for qt in range(NQT):
                    gq = gqs[qt]
                    e = gq % 16

                    @blk.scalar
                    def _(s, qt=qt, gq=gq, e=e, l2c=linv2_c):
                        s.wait_ge(vs.h, max(mv1_c[gq], l2c))
                        ss.inc(s.activation(eb1["lnv"][e][:],
                                            eb1["mv"][e][:, 1:2],
                                            AF.Ln, scale=linv2[:, qt:qt + 1],
                                            bias=c_eps[:]))
                        s.wait_ge(ss.h, ss.n)
                        ss.inc(s.activation(eb1["rstd"][e][:],
                                            eb1["lnv"][e][:], AF.Exp,
                                            scale=-0.5, bias=c_zero[:]))
                    rstd_c[gq] = ss.n
                    ep_counts[(gq, "rstd1")] = (ss, ss.n)

                for qt in range(NQT):
                    gq = gqs[qt]
                    e = gq % 16

                    @blk.vector
                    def _(v, qt=qt, gq=gq, e=e):
                        v.wait_ge(ss.h, rstd_c[gq])
                        vs.inc(v.tensor_scalar(eb1["rstd"][e][:],
                                               eb1["rstd"][e][:],
                                               linv[:, qt:qt + 1], 1.0,
                                               op0=ALU.mult, op1=ALU.mult))
                        v.wait_ge(vs.h, vs.n)
                        vs.inc(v.tensor_scalar(eb1["nb"][e][:],
                                               eb1["mv"][e][:, 0:1],
                                               eb1["rstd"][e][:], -1.0,
                                               op0=ALU.mult, op1=ALU.mult))
                    bq_c[gq] = vs.n

                for qt in range(NQT):
                    gq = gqs[qt]
                    e = gq % 16

                    @blk.scalar
                    def _(s, qt=qt, gq=gq, e=e):
                        s.wait_ge(vs.h, bq_c[gq])
                        if (gq - 4, "x2add") in ep_counts:
                            sm2, c2 = ep_counts[(gq - 4, "x2add")]
                            s.wait_ge(sm2.h, c2)
                        ss.inc(s.activation(ln12[gq % 4][:], O_sb[qt][:],
                                            AF.Identity,
                                            scale=eb1["rstd"][e][:],
                                            bias=eb1["nb"][e][:]))
                    ln1_c[gq] = ss.n

                    dxq, xq_cc = xq_map[gq]

                    @blk.vector
                    def _(v, qt=qt, gq=gq, dxq=dxq, xq_cc=xq_cc):
                        v.wait_ge(dxq.h, xq_cc)
                        v.wait_ge(ss.h, ln1_c[gq])
                        vs.inc(v.tensor_tensor(O_sb[qt][:], ln12[gq % 4][:],
                                               xr[gq % 6][:], op=ALU.add))
                    ep_counts[(gq, "x2add")] = (vs, vs.n)
                    x2_c[gq] = vs.n

                    if qt + 6 < NQT:
                        emit_xq(qt + 6)

                for qt in range(NQT):
                    gq = gqs[qt]

                    @blk.gpsimd
                    def _(g, qt=qt, gq=gq):
                        g.wait_ge(vs.h, x2_c[gq])
                        SM["d_x2w0"].inc(g.dma_start(
                            out=x2h[q0 + qt * 128:q0 + (qt + 1) * 128, :],
                            in_=O_sb[qt][:]), 16)

                for qt in range(NQT):
                    gq = gqs[qt]
                    e = gq % 16

                    @blk.vector
                    def _(v, qt=qt, gq=gq, e=e):
                        v.wait_ge(vs.h, x2_c[gq])
                        if (gq - 16, "rstd2") in ep_counts:
                            sm2, c2 = ep_counts[(gq - 16, "rstd2")]
                            v.wait_ge(sm2.h, c2)
                        for ch in range(NCH):
                            vs.inc(v.bn_stats(
                                eb2["s12"][e][:, 6 * ch:6 * ch + 6],
                                O_sb[qt][:, ch * 512:min(D, (ch + 1) * 512)]))
                        v.wait_ge(vs.h, vs.n)
                        vs.inc(v.bn_aggr(eb2["mv"][e][:],
                                         eb2["s12"][e][:, 0:6 * NCH]))
                    mv1_c[(gq, 2)] = vs.n

                for qt in range(NQT):
                    gq = gqs[qt]
                    e = gq % 16

                    @blk.scalar
                    def _(s, qt=qt, gq=gq, e=e):
                        s.wait_ge(vs.h, mv1_c[(gq, 2)])
                        ss.inc(s.activation(eb2["lnv"][e][:],
                                            eb2["mv"][e][:, 1:2],
                                            AF.Ln, scale=1.0, bias=c_eps[:]))
                        s.wait_ge(ss.h, ss.n)
                        ss.inc(s.activation(eb2["rstd"][e][:],
                                            eb2["lnv"][e][:], AF.Exp,
                                            scale=-0.5, bias=c_zero[:]))
                    rstd_c[(gq, 2)] = ss.n
                    ep_counts[(gq, "rstd2")] = (ss, ss.n)

                for qt in range(NQT):
                    gq = gqs[qt]
                    e = gq % 16

                    @blk.vector
                    def _(v, qt=qt, gq=gq, e=e):
                        v.wait_ge(ss.h, rstd_c[(gq, 2)])
                        vs.inc(v.tensor_scalar(eb2["nb"][e][:],
                                               eb2["mv"][e][:, 0:1],
                                               eb2["rstd"][e][:], -1.0,
                                               op0=ALU.mult, op1=ALU.mult))
                    nb2_c[gq] = vs.n

                for qt in range(NQT):
                    gq = gqs[qt]
                    e = gq % 16
                    dtw = SM[f"d_tw{gq % 4}"]

                    @blk.scalar
                    def _(s, qt=qt, gq=gq, e=e, dtw=dtw, dtw_c=dtw.n):
                        s.wait_ge(vs.h, nb2_c[gq])
                        if (gq - 4, "amax") in ep_counts:
                            s.wait_ge(dtw.h, dtw_c)
                            sm2, c2 = ep_counts[(gq - 4, "amax")]
                            s.wait_ge(sm2.h, c2)
                        ss.inc(s.activation(tt2[gq % 4][:], O_sb[qt][:],
                                            AF.Identity,
                                            scale=eb2["rstd"][e][:],
                                            bias=eb2["nb"][e][:]))
                    tn_c[gq] = ss.n
                    ep_counts[(gq, "tnorm")] = (ss, ss.n)

                    @blk.vector
                    def _(v, qt=qt, gq=gq, au=amax_upd):
                        v.wait_ge(ss.h, tn_c[gq])
                        if au:
                            v.wait_ge(vs.h, au)
                        vs.inc(v.reduce_max(tmax2[gq % 16][:], tt2[gq % 4][:],
                                            axis=AX.X,
                                            apply_absolute_value=True))
                        v.wait_ge(vs.h, vs.n)
                        vs.inc(v.tensor_tensor(amax1a[:], amax1a[:],
                                               tmax2[gq % 16][:], op=ALU.max))
                    amax_upd = vs.n
                    ep_counts[(gq, "amax")] = (vs, amax_upd)

                    @blk.gpsimd
                    def _(g, qt=qt, gq=gq):
                        g.wait_ge(ss.h, tn_c[gq])
                        SM[f"d_tw{gq % 4}"].inc(g.dma_start(
                            out=t_h[q0 + qt * 128:q0 + (qt + 1) * 128, :],
                            in_=tt2[gq % 4][:]), 16)

                att_prev = {qt: (ss, tn_c[gqs[qt]]) for qt in range(NQT)}
                att_prev_dma = SM["d_x2w0"].n

            barrier(extra=[(SM["d_x2w0"], SM["d_x2w0"].n)] +
                          [(SM[f"d_tw{j}"], SM[f"d_tw{j}"].n)
                           for j in range(4)])

        with contextlib.ExitStack() as pf:
            Ef = pf.enter_context
            W_sb = Ef(nc.sbuf_tensor("W_sb", [128, DC * F], BF16))
            Wi_sb = W_sb[:].rearrange("p (c f) -> p c f", c=DC)
            Wo_sb = W_sb[:].rearrange("p (c d) -> p c d", c=FC)

            tfm_st = contextlib.ExitStack()
            tfm2 = [tfm_st.enter_context(nc.sbuf_tensor(f"tfm2_{j}", [128, T],
                                                        BF16))
                    for j in range(2)]
            ar1_cc = emit_ar_part1(amax1a, cc1i, cc1o, amax_upd, ps_o[1])

            with contextlib.ExitStack() as ar1:
                Ea = ar1.enter_context
                wt2 = [Ea(nc.sbuf_tensor(f"wt2_{j}", [128, F], F32))
                       for j in range(2)]

                @blk.sync
                def _(y):
                    for dc0 in range(min(2, DC)):
                        SM[f"d_tfm{dc0 % 2}"].inc(y.dma_start_transpose(
                            tfm2[dc0 % 2][:],
                            t_h[:, dc0 * 128:(dc0 + 1) * 128]), 16)
                macc = Ea(nc.sbuf_tensor("macc", [128, F], F32))
                prodw = Ea(nc.sbuf_tensor("prodw", [128, F], F32))
                qi_w = Ea(nc.sbuf_tensor("qi_w", [128, F], I32))
                s_pc = Ea(nc.sbuf_tensor("s_pc", [128, FC], F32))
                s32 = Ea(nc.sbuf_tensor("s32", [FC, 128], F32))
                srow = Ea(nc.sbuf_tensor("srow", [1, F], F32))
                s_bcast = Ea(nc.sbuf_tensor("s_bcast", [128, F], F32))
                binfc = Ea(nc.sbuf_tensor("binfc", [FC, 128], F32))
                boutrow = Ea(nc.sbuf_tensor("boutrow", [1, D], F32))

                @blk.vector
                def _(v):
                    vs.inc(v.memset(macc[:], 1e-8))
                macc_c = vs.n

                w_rd = {}
                for dc in range(DC):
                    j = dc % 2
                    dw = SM[f"d_w{j}"]

                    @blk.sync
                    def _(y, dc=dc, j=j, dw=dw):
                        if dc >= 2:
                            y.wait_ge(ss.h, w_rd[dc - 2])
                        dw.inc(y.dma_start(
                            out=wt2[j][:],
                            in_=w_in[dc * 128:(dc + 1) * 128, :]), 16)
                    wl_c = dw.n

                    @blk.scalar
                    def _(s, j=j, dw=dw, wl_c=wl_c, mc=macc_c):
                        s.wait_ge(dw.h, wl_c)
                        if isinstance(mc, tuple):
                            s.wait_ge(mc[0].h, mc[1])
                        else:
                            s.wait_ge(vs.h, mc)
                        ss.inc(s.activation(prodw[:], wt2[j][:], AF.Abs,
                                            bias=c_zero[:], scale=1.0))
                    abs_c = ss.n
                    w_rd[dc] = abs_c

                    @blk.vector
                    def _(v, abs_c=abs_c, mc=macc_c):
                        v.wait_ge(ss.h, abs_c)
                        if isinstance(mc, tuple):
                            v.wait_ge(mc[0].h, mc[1])
                        else:
                            v.wait_ge(vs.h, mc)
                        vs.inc(v.tensor_tensor(macc[:], prodw[:], macc[:],
                                               op=ALU.max))
                    macc_c = (vs, vs.n)

                cm_c = macc_c
                for c in range(FC):
                    pb = 2 + (c % 2)

                    @blk.tensor
                    def _(t, c=c, pb=pb, cm_c=cm_c):
                        if isinstance(cm_c, tuple):
                            t.wait_ge(cm_c[0].h, cm_c[1])
                        else:
                            t.wait_ge(vs.h, cm_c)
                        ms.inc(t.transpose(ps_st[pb][0:128, 0:128],
                                           macc[:, c * 128:(c + 1) * 128],
                                           ident[:]))
                    tr_c = ms.n

                    @blk.vector
                    def _(v, c=c, pb=pb, tr_c=tr_c):
                        v.wait_ge(ms.h, tr_c)
                        vs.inc(v.reduce_max(colmax_pc[:, c:c + 1],
                                            ps_st[pb][0:128, 0:128], axis=AX.X))
                    cm_c = vs.n

                @blk.vector
                def _(v, cm_c=cm_c):
                    v.wait_ge(vs.h, cm_c)
                    vs.inc(v.tensor_scalar(ginvw_pc[:], colmax_pc[:],
                                           1.0 / 127.0, None, op0=ALU.mult))
                    vs.inc(v.reciprocal(s_pc[:], colmax_pc[:]))
                    v.wait_ge(vs.h, vs.n)
                    vs.inc(v.tensor_scalar(s_pc[:], s_pc[:], 127.0, None,
                                           op0=ALU.mult))
                spc_c = vs.n

                @blk.tensor
                def _(t, spc_c=spc_c):
                    t.wait_ge(vs.h, spc_c)
                    ms.inc(t.transpose(ps_st[2][0:FC, 0:128], s_pc[:], ident[:]))
                trs_c = ms.n

                @blk.vector
                def _(v, trs_c=trs_c):
                    v.wait_ge(ms.h, trs_c)
                    vs.inc(v.tensor_copy(s32[:], ps_st[2][0:FC, 0:128]))
                s32_c = vs.n

                @blk.sync
                def _(y, s32_c=s32_c):
                    y.wait_ge(vs.h, s32_c)
                    SM["d_scr"].inc(y.dma_start(out=scrv, in_=s32[:]), 16)
                    y.wait_ge(SM["d_scr"].h, SM["d_scr"].n)
                    SM["d_scr"].inc(y.dma_start(out=srow[:], in_=scr), 16)
                    SM["d_bio"].inc(y.dma_start(out=binfc[:], in_=binv), 16)
                    SM["d_bio"].inc(y.dma_start(out=boutrow[:], in_=b_out), 16)
                scr_c = SM["d_scr"].n
                bio_c = SM["d_bio"].n

                sb_c = 0
                for jj in range((F + 511) // 512):
                    w = min(512, F - jj * 512)
                    pb = 2 + (jj % 2)

                    @blk.tensor
                    def _(t, jj=jj, w=w, pb=pb, scr_c=scr_c, sb_c=sb_c):
                        t.wait_ge(SM["d_scr"].h, scr_c)
                        if sb_c:
                            t.wait_ge(vs.h, sb_c)
                        ms.inc(t.matmul(ps_st[pb][:, 0:w], ones_row[0:1, :],
                                        srow[0:1, jj * 512:jj * 512 + w],
                                        start=True, stop=True))
                    mmc = ms.n

                    @blk.vector
                    def _(v, jj=jj, w=w, pb=pb, mmc=mmc):
                        v.wait_ge(ms.h, mmc)
                        vs.inc(v.tensor_copy(s_bcast[:, jj * 512:jj * 512 + w],
                                             ps_st[pb][:, 0:w]))
                    sb_c = vs.n

                @blk.tensor
                def _(t, bio_c=bio_c, sb_c=sb_c):
                    t.wait_ge(SM["d_bio"].h, bio_c)
                    t.wait_ge(vs.h, sb_c)
                    ms.inc(t.transpose(ps_st[0][0:128, 0:FC], binfc[0:FC, :],
                                       ident[0:FC, 0:FC]))
                    for jj in range((D + 511) // 512):
                        w = min(512, D - jj * 512)
                        tgt = ps_st[1][:, 0:w] if jj == 0 else ps_o[0][:, 0:w]
                        ms.inc(t.matmul(tgt, ones_row[0:1, :],
                                        boutrow[0:1, jj * 512:jj * 512 + w],
                                        start=True, stop=True))
                bio_mm = ms.n

                @blk.vector
                def _(v, bio_mm=bio_mm):
                    v.wait_ge(ms.h, bio_mm)
                    vs.inc(v.tensor_copy(binpc[:, 0:FC], ps_st[0][0:128, 0:FC]))
                    vs.inc(v.tensor_copy(bout_bc[:, 0:min(512, D)],
                                         ps_st[1][:, 0:min(512, D)]))
                    if D > 512:
                        vs.inc(v.tensor_copy(bout_bc[:, 512:D],
                                             ps_o[0][:, 0:D - 512]))
                bio_v = vs.n

                wq_rd = {}
                for dc in range(DC):
                    j = dc % 2
                    dw = SM[f"d_w{j}"]

                    @blk.sync
                    def _(y, dc=dc, j=j, dw=dw):
                        if dc - 2 in wq_rd:
                            y.wait_ge(ss.h, wq_rd[dc - 2])
                        else:
                            y.wait_ge(ss.h, w_rd.get(DC - 2 + dc, w_rd[DC - 1]))
                        dw.inc(y.dma_start(
                            out=wt2[j][:],
                            in_=w_in[dc * 128:(dc + 1) * 128, :]), 16)
                    wl_c = dw.n

                    @blk.gpsimd
                    def _(g, j=j, dw=dw, wl_c=wl_c, sb_c=sb_c):
                        g.wait_ge(dw.h, wl_c)
                        g.wait_ge(vs.h, sb_c)
                        gs.inc(g.tensor_tensor(wt2[j][:], wt2[j][:],
                                               s_bcast[:], op=ALU.mult))
                    gw_c = gs.n

                    @blk.vector
                    def _(v, dc=dc, gw_c=gw_c, pw=wq_rd.get(dc - 1, 0)):
                        v.wait_ge(gs.h, gw_c)
                        if pw:
                            v.wait_ge(ss.h, pw)
                        vs.inc(v.tensor_copy(qi_w[:], wt2[j][:]))
                    cp1_c = vs.n

                    @blk.scalar
                    def _(s, dc=dc, cp1_c=cp1_c):
                        s.wait_ge(vs.h, cp1_c)
                        ss.inc(s.activation(Wi_sb[:, dc, :], qi_w[:],
                                            AF.Copy, bias=0.0, scale=1.0))
                    wq_rd[dc] = ss.n

                scales1_c = emit_ar_part2(s1_bc, inv1_bc, 32767.0, ar1_cc,
                                          ps_o[1])
                barrier(extra=[(SM["d_scr"], SM["d_scr"].n)])

            with contextlib.ExitStack() as pa:
                Ea = pa.enter_context
                tq = Ea(nc.sbuf_tensor("tq", [128, DC, T], BF16))
                qi_t = [Ea(nc.sbuf_tensor(f"qi_t{j}", [128, T], I32))
                        for j in range(2)]
                hst2 = [Ea(nc.sbuf_tensor(f"hst2_{j}", [128, T], BF16))
                        for j in range(2)]
                hred2 = [Ea(nc.sbuf_tensor(f"hred2_{j}", [128, 1], F32))
                         for j in range(2)]

                @blk.scalar
                def _(s):
                    load_act_set(s, _SET_GELU)

                tq_cs = {}
                qit_ss = 0
                for dc in range(DC):
                    j = dc % 2
                    dt = SM[f"d_tfm{j}"]
                    tl_c = dt.n

                    @blk.scalar
                    def _(s, dc=dc, j=j, dt=dt, tl_c=tl_c,
                          tq_c=tq_cs.get(dc - 2, 0), s1c=scales1_c):
                        s.wait_ge(dt.h, tl_c)
                        if dc == 0:
                            s.wait_ge(vs.h, s1c)
                        if tq_c:
                            s.wait_ge(vs.h, tq_c)
                        ss.inc(s.activation(qi_t[j][:], tfm2[j][:], AF.Copy,
                                            bias=0.0, scale=s1_bc[:]))
                    qit_ss = ss.n

                    @blk.vector
                    def _(v, dc=dc, j=j, qit_ss=qit_ss):
                        v.wait_ge(ss.h, qit_ss)
                        vs.inc(v.tensor_scalar(tq[:, dc, :], qi_t[j][:],
                                               inv1_bc[:], None,
                                               op0=ALU.mult))
                    tq_cs[dc] = vs.n
                    tq_c = vs.n

                    if dc + 2 < DC:
                        @blk.sync
                        def _(y, dc=dc, j=j, dt=dt, qit_ss=qit_ss):
                            y.wait_ge(ss.h, qit_ss)
                            dt.inc(y.dma_start_transpose(
                                tfm2[j][:],
                                t_h[:, (dc + 2) * 128:(dc + 3) * 128]), 16)
                tq_all = tq_c

                gelu_done = {}
                hred_done = {}
                for fc in range(FC):
                    jf = fc % 2
                    dh = SM[f"d_hw{jf}"]
                    ms_base = ms.n
                    mm1_done = {tt: ms_base + tt + 1 for tt in range(NTT)}

                    @blk.tensor
                    def _(t, fc=fc, tq_all=tq_all, gd=dict(gelu_done)):
                        if fc == 0:
                            t.wait_ge(vs.h, tq_all)
                        for tt in range(NTT - 1):
                            if gd.get((fc - 1, tt)):
                                t.wait_ge(ss.h, gd[(fc - 1, tt)])
                        for dc in range(DC):
                            for tt in range(NTT - 1):
                                mm = t.matmul(
                                    HP[tt],
                                    Wi_sb[:, dc, fc * 128:(fc + 1) * 128],
                                    tq[:, dc, tt * 512:(tt + 1) * 512],
                                    start=(dc == 0), stop=(dc == DC - 1))
                                if dc == DC - 1:
                                    ms.inc(mm)
                        if gd.get((fc - 1, NTT - 1)):
                            t.wait_ge(ss.h, gd[(fc - 1, NTT - 1)])
                        for dc in range(DC):
                            mm = t.matmul(
                                HP[NTT - 1],
                                Wi_sb[:, dc, fc * 128:(fc + 1) * 128],
                                tq[:, dc, (NTT - 1) * 512:NTT * 512],
                                start=(dc == 0), stop=(dc == DC - 1))
                        ms.inc(mm)
                    assert ms.n == ms_base + NTT

                    @blk.scalar
                    def _(s, fc=fc, jf=jf, dh=dh, mm1_done=mm1_done):
                        for tt in range(NTT):
                            s.wait_ge(ms.h, mm1_done[tt])
                            if fc >= 2 and tt == 0:
                                s.wait_ge(dh.h, 16 * (fc // 2))
                                s.wait_ge(vs.h, hred_done[fc - 2])
                            ss.inc(s.activation(
                                hst2[jf][:, tt * 512:(tt + 1) * 512], HP[tt],
                                AF.Gelu, scale=ginvw_pc[:, fc:fc + 1],
                                bias=binpc[:, fc:fc + 1]))
                            gelu_done[(fc, tt)] = ss.n
                    g_all = ss.n

                    @blk.gpsimd
                    def _(g, fc=fc, jf=jf, dh=dh, g_all=g_all):
                        g.wait_ge(ss.h, g_all)
                        dh.inc(g.dma_start(out=hh3[:, fc, :], in_=hst2[jf][:]), 16)

                    @blk.vector
                    def _(v, fc=fc, jf=jf, g_all=g_all,
                          prev=(hred_done.get(fc - 2, 0))):
                        v.wait_ge(ss.h, g_all)
                        if prev:
                            v.wait_ge(vs.h, prev)
                        vs.inc(v.reduce_max(hred2[jf][:], hst2[jf][:], axis=AX.X,
                                            apply_absolute_value=True))
                        v.wait_ge(vs.h, vs.n)
                        vs.inc(v.tensor_tensor(amax2a[:], amax2a[:], hred2[jf][:],
                                               op=ALU.max))
                    hred_done[fc] = vs.n
                am2_c = vs.n

                barrier(extra=[(SM["d_hw0"], SM["d_hw0"].n),
                               (SM["d_hw1"], SM["d_hw1"].n)])

            tfm_st.close()

            @blk.scalar
            def _(s):
                load_act_set(s, _SET_MAIN)

            ht2 = [Ef(nc.sbuf_tensor(f"ht2_{j}", [128, FC, 512], BF16))
                   for j in range(2)]
            hq_done = {}
            hq_ss = {}
            hload_c = {}
            pe_ht_done = {}

            def emit_hload(tt):
                j = tt % 2
                dhr = SM[f"d_hr{j}"]

                @blk.sync
                def _(y, tt=tt, j=j, dhr=dhr):
                    if tt >= 2:
                        y.wait_ge(ms.h, pe_ht_done[tt - 2])
                        y.wait_ge(vs.h, hq_done[tt - 2])
                        y.wait_ge(ss.h, hq_ss[tt - 2])
                    dhr.inc(y.dma_start(
                        out=ht2[j][:],
                        in_=hh3[:, :, tt * 512:(tt + 1) * 512]), 16)
                hload_c[tt] = dhr.n

            emit_hload(0)
            if NTT > 1:
                emit_hload(1)

            ar2_cc = emit_ar_part1(amax2a, cc2i, cc2o, am2_c, ps_o[1])

            with contextlib.ExitStack() as ar2:
                Ea = ar2.enter_context
                wo2 = [Ea(nc.sbuf_tensor(f"wo2_{j}", [128, D], F32))
                       for j in range(4)]
                qi_o = [Ea(nc.sbuf_tensor(f"qi_o{j}", [128, D], I32))
                        for j in range(4)]
                rmax2 = [Ea(nc.sbuf_tensor(f"rmax2_{j}", [128, 1], F32))
                         for j in range(4)]
                rs2 = [Ea(nc.sbuf_tensor(f"rs2_{j}", [128, 1], F32))
                       for j in range(4)]
                ri2 = [Ea(nc.sbuf_tensor(f"ri2_{j}", [128, 1], F32))
                       for j in range(4)]

                woq = {}
                wo_wl = {}

                def emit_wo_load(fc):
                    if fc >= FC:
                        return
                    j = fc % 4
                    dw = SM[f"d_w{j}"]

                    @blk.sync
                    def _(y, fc=fc, j=j, dw=dw, wq=woq.get(fc - 4)):
                        if wq:
                            y.wait_ge(wq[0].h, wq[1])
                        dw.inc(y.dma_start(
                            out=wo2[j][:],
                            in_=w_out[fc * 128:(fc + 1) * 128, :]), 16)
                    wo_wl[fc] = dw.n

                for fc in range(min(4, FC)):
                    emit_wo_load(fc)

                rsc_all = {}
                qo_all = {}

                def emit_wo_rmax(fc):
                    j = fc % 4
                    dw = SM[f"d_w{j}"]

                    @blk.vector
                    def _(v, fc=fc, j=j, dw=dw, wl_c=wo_wl[fc],
                          wq=woq.get(fc - 4)):
                        v.wait_ge(dw.h, wl_c)
                        if wq:
                            v.wait_ge(wq[0].h, wq[1])
                        vs.inc(v.reduce_max(rmax2[j][:], wo2[j][:], axis=AX.X,
                                            apply_absolute_value=True))
                        v.wait_ge(vs.h, vs.n)
                        vs.inc(v.tensor_scalar_max(rmax2[j][:], rmax2[j][:],
                                                   1e-8))
                        v.wait_ge(vs.h, vs.n)
                        vs.inc(v.reciprocal(rs2[j][:], rmax2[j][:]))
                        vs.inc(v.tensor_scalar(ri2[j][:], rmax2[j][:],
                                               1.0 / 127.0, None,
                                               op0=ALU.mult))
                        v.wait_ge(vs.h, vs.n)
                        vs.inc(v.tensor_scalar(rs2[j][:], rs2[j][:], 127.0,
                                               None, op0=ALU.mult))
                    rsc_all[fc] = vs.n

                def emit_wo_op1(fc):
                    j = fc % 4

                    @blk.scalar
                    def _(s, fc=fc, j=j, rsc_c=rsc_all[fc],
                          pq=woq.get(fc - 4)):
                        s.wait_ge(vs.h, rsc_c)
                        if pq:
                            s.wait_ge(pq[0].h, pq[1])
                        ss.inc(s.activation(qi_o[j][:], wo2[j][:], AF.Copy,
                                            bias=0.0, scale=rs2[j][:]))
                    qo_all[fc] = ss.n

                def emit_wo_op2(fc):
                    j = fc % 4

                    @blk.vector
                    def _(v, fc=fc, j=j, qo_c=qo_all[fc]):
                        v.wait_ge(ss.h, qo_c)
                        vs.inc(v.tensor_scalar(Wo_sb[:, fc, :], qi_o[j][:],
                                               ri2[j][:], None,
                                               op0=ALU.mult))
                    woq[fc] = (vs, vs.n)

                for fc in range(FC):
                    emit_wo_rmax(fc)
                    emit_wo_op1(fc)
                    if fc >= 1:
                        emit_wo_op2(fc - 1)
                        emit_wo_load(fc + 3)
                emit_wo_op2(FC - 1)

                scales2_c = emit_ar_part2(s2_bc, inv2_bc, 32767.0, ar2_cc,
                                          ps_o[1])
                woq_final = woq[FC - 1][1]

            with contextlib.ExitStack() as pb:
                Eb = pb.enter_context
                nqh = 8 if FC >= 8 else (4 if FC >= 4 else
                                         (2 if FC >= 2 else 1))
                qhalf = (FC // nqh) * 512
                qi_h = [Eb(nc.sbuf_tensor(f"qi_h{j}", [128, qhalf], I32))
                        for j in range(2)]
                ot2 = [Eb(nc.sbuf_tensor(f"ot2_{j}", [128, D], F32))
                       for j in range(2)]
                lnf2 = [Eb(nc.sbuf_tensor(f"lnf2_{j}", [128, D], F32))
                        for j in range(2)]
                x2r2 = [Eb(nc.sbuf_tensor(f"x2r2_{j}", [128, D], F32))
                        for j in range(2)]
                outt2 = [Eb(nc.sbuf_tensor(f"outt2_{j}", [128, D], F32))
                         for j in range(2)]
                sqf2 = [Eb(nc.sbuf_tensor(f"sqf2_{j}", [128, D], F32))
                        for j in range(2)]
                fb = make_stats_bufs(Eb, "fb")

                fep = {}
                qih_vs = [0, 0]
                qih_k = [0]

                def emit_quant(tt):
                    j = tt % 2
                    dhr = SM[f"d_hr{j}"]
                    flat = ht2[j][:].rearrange("p c t -> p (c t)")
                    for hh in range(nqh):
                        jq = qih_k[0] % 2

                        @blk.scalar
                        def _(s, tt=tt, hh=hh, jq=jq, dhr=dhr, flat=flat,
                              qv=qih_vs[jq], s2c=scales2_c, hl=hload_c[tt]):
                            s.wait_ge(dhr.h, hl)
                            if qv:
                                s.wait_ge(vs.h, qv)
                            if tt == 0 and hh == 0:
                                s.wait_ge(vs.h, s2c)
                            ss.inc(s.activation(
                                qi_h[jq][:],
                                flat[:, hh * qhalf:(hh + 1) * qhalf],
                                AF.Copy, bias=0.0, scale=s2_bc[:]))
                        op1_c = ss.n

                        @blk.vector
                        def _(v, hh=hh, jq=jq, op1_c=op1_c, flat=flat):
                            v.wait_ge(ss.h, op1_c)
                            vs.inc(v.tensor_scalar(
                                flat[:, hh * qhalf:(hh + 1) * qhalf],
                                qi_h[jq][:],
                                inv2_bc[:], None, op0=ALU.mult))
                        qih_vs[jq] = vs.n
                        qih_k[0] += 1
                    hq_done[tt] = vs.n
                    hq_ss[tt] = ss.n

                emit_quant(0)

                ms_b0 = ms.n
                for tt in range(NTT):
                    pe_ht_done[tt] = ms_b0 + (tt + 1) * TCH

                for tt in range(NTT):
                    j = tt % 2
                    if tt + 1 < NTT:
                        emit_quant(tt + 1)
                    if tt + 2 < NTT:
                        emit_hload(tt + 2)

                    for qc in range(TCH):
                        ob = qc % 2
                        gq2 = tt * TCH + qc
                        j3 = gq2 % 2
                        dxr = SM[f"d_xr{j3}"]

                        @blk.tensor
                        def _(t, tt=tt, j=j, qc=qc, ob=ob, gq2=gq2,
                              hq=hq_done[tt], pb_=fep.get((gq2 - 2, "bias"), 0)):
                            t.wait_ge(vs.h, hq)
                            if gq2 == 0:
                                t.wait_ge(vs.h, woq_final)
                            if pb_:
                                t.wait_ge(vs.h, pb_)
                            for fc in range(FC):
                                lhs = ht2[j][:, fc, qc * 128:(qc + 1) * 128]
                                for dvt in range(NDV):
                                    mm = t.matmul(
                                        ps_o[ob][:, dvt * 512:dvt * 512 + DVW],
                                        lhs,
                                        Wo_sb[:, fc, dvt * 512:dvt * 512 + DVW],
                                        start=(fc == 0), stop=(fc == FC - 1))
                            ms.inc(mm)
                        mm2_c = ms.n
                        assert ms.n == ms_b0 + tt * TCH + qc + 1

                        @blk.gpsimd
                        def _(g, tt=tt, qc=qc, j3=j3, gq2=gq2, dxr=dxr):
                            if gq2 >= 2:
                                g.wait_ge(vs.h, fep[(gq2 - 2, "resid")])
                            else:
                                g.wait_ge(vs.h, scales2_c)
                                g.wait_ge(ss.h, qo_all[FC - 1])
                            r0 = tt * 512 + qc * 128
                            dxr.inc(g.dma_start(out=x2r2[j3][:],
                                                in_=x2h[r0:r0 + 128, :]), 16)
                        xr_c = dxr.n

                        @blk.vector
                        def _(v, gq2=gq2, ob=ob, j3=j3, bv=bio_v, mm2_c=mm2_c):
                            v.wait_ge(ms.h, mm2_c)
                            v.wait_ge(vs.h, bv)
                            if gq2 >= 2:
                                v.wait_ge(ss.h, fep[(gq2 - 2, "lnf")])
                                v.wait_ge(vs.h, fep[(gq2 - 2, "red")])
                            vs.inc(v.tensor_tensor(ot2[j3][:], ps_o[ob][:, 0:D],
                                                   bout_bc[:, 0:D], op=ALU.add))
                        fep[(gq2, "bias")] = vs.n
                        bias_c = vs.n

                        rstd_c, nb_c = emit_ln_stats(
                            ot2[j3][:], sqf2[j3], fb, j3, D,
                            wait_v=[(vs.h, bias_c)], wait_s=[(vs.h, bias_c)])
                        fep[(gq2, "red")] = nb_c

                        @blk.scalar
                        def _(s, j3=j3, nb_c=nb_c, gq2=gq2):
                            s.wait_ge(vs.h, nb_c)
                            if gq2 >= 2:
                                s.wait_ge(vs.h, fep[(gq2 - 2, "resid")])
                            ss.inc(s.activation(lnf2[j3][:], ot2[j3][:],
                                                AF.Identity,
                                                scale=fb["rstd"][j3][:],
                                                bias=fb["nb"][j3][:]))
                        fep[(gq2, "lnf")] = ss.n
                        lnf_c = ss.n

                        @blk.vector
                        def _(v, j3=j3, lnf_c=lnf_c, xr_c=xr_c, dxr=dxr,
                              gq2=gq2):
                            v.wait_ge(ss.h, lnf_c)
                            v.wait_ge(dxr.h, xr_c)
                            if gq2 >= 2:
                                v.wait_ge(SM[f"d_out{j3}"].h, 16 * (gq2 // 2))
                            vs.inc(v.tensor_tensor(outt2[j3][:], lnf2[j3][:],
                                                   x2r2[j3][:], op=ALU.add))
                        fep[(gq2, "resid")] = vs.n
                        res_c = vs.n

                        @blk.gpsimd
                        def _(g, tt=tt, qc=qc, j3=j3, res_c=res_c):
                            g.wait_ge(vs.h, res_c)
                            r0 = tt * 512 + qc * 128
                            SM[f"d_out{j3}"].inc(g.dma_start(
                                out=out[r0:r0 + 128, :], in_=outt2[j3][:]), 16)

                barrier(extra=[(SM["d_out0"], SM["d_out0"].n),
                               (SM["d_out1"], SM["d_out1"].n)])

    return nc


_NC_CACHE = {}


def _get_nc(key, cfg):
    if key not in _NC_CACHE:
        _NC_CACHE[key] = build_nc(cfg)
    return _NC_CACHE[key]


def kernel(L0_attn_x, L0_ffn_W_in, L0_ffn_b_in, L0_ffn_W_out, L0_ffn_b_out):
    cfg = FULL_CFG
    x = np.ascontiguousarray(np.asarray(L0_attn_x))
    w_in_a = np.ascontiguousarray(np.asarray(L0_ffn_W_in))
    b_in_a = np.ascontiguousarray(np.asarray(L0_ffn_b_in)).reshape(1, -1)
    w_out_a = np.ascontiguousarray(np.asarray(L0_ffn_W_out))
    b_out_a = np.ascontiguousarray(np.asarray(L0_ffn_b_out)).reshape(1, -1)

    nc = _get_nc("full", cfg)
    T = cfg["T"]
    in_maps = []
    for c in range(cfg["NCORES"]):
        b, h = divmod(c, 2)
        xb_c = np.ascontiguousarray(np.roll(x[b], -h * T, axis=0))
        in_maps.append({"xb": xb_c, "w_in": w_in_a, "b_in": b_in_a,
                        "w_out": w_out_a, "b_out": b_out_a})
    res = run_bass_kernel_spmd(nc, in_maps, core_ids=list(range(cfg["NCORES"])))
    outs = [np.asarray(r["out"]) for r in res.results]
    full = np.concatenate(outs, axis=0).reshape(4, 2 * T, x.shape[2])
    return full.astype(np.float32, copy=False)


# version 41
# speedup vs baseline: 1.0223x; 1.0223x over previous
import contextlib
import numpy as np
import concourse.bass as bass
import concourse.mybir as mybir
from concourse.bass_utils import run_bass_kernel_spmd

F32 = mybir.dt.float32
BF16 = mybir.dt.bfloat16
I32 = mybir.dt.int32
AF = mybir.ActivationFunctionType
ALU = mybir.AluOpType
AX = mybir.AxisListType

FULL_CFG = dict(NCORES=8, T=4096, S=8192, D=1024, F=4096, SWQ=2048, KBG=512)


class Sem:
    def __init__(self, st, nc, name):
        self.h = st.enter_context(nc.semaphore(name))
        self.n = 0

    def inc(self, inst, k=1):
        inst.then_inc(self.h, k)
        self.n += k
        return self.n


def build_nc(cfg):
    NCORES = cfg["NCORES"]
    T, S, D, F = cfg["T"], cfg["S"], cfg["D"], cfg["F"]
    SWQ, KBG = cfg["SWQ"], cfg["KBG"]
    DC = D // 128
    FC = F // 128
    NTIL = S // 128
    NSW = T // SWQ
    NQT = SWQ // 128
    NQG = SWQ // 512
    NKBG = S // KBG
    KC = KBG // 128
    NDV = (D + 511) // 512
    DVW = min(D, 512)
    NTT = T // 512
    TCH = 4
    assert SWQ % 512 == 0 and T % 512 == 0 and D % 128 == 0 and F % 128 == 0

    nc = bass.Bass("TRN2", target_bir_lowering=False)

    xb = nc.dram_tensor("xb", [S, D], F32, kind="ExternalInput").ap()
    w_in = nc.dram_tensor("w_in", [D, F], F32, kind="ExternalInput").ap()
    b_in = nc.dram_tensor("b_in", [1, F], F32, kind="ExternalInput").ap()
    w_out = nc.dram_tensor("w_out", [F, D], F32, kind="ExternalInput").ap()
    b_out = nc.dram_tensor("b_out", [1, D], F32, kind="ExternalInput").ap()
    out = nc.dram_tensor("out", [T, D], F32, kind="ExternalOutput").ap()

    vtm = nc.dram_tensor("vtm", [S, D], BF16).ap()
    kfm = nc.dram_tensor("kfm", [D, S], BF16).ap()
    x2h = nc.dram_tensor("x2h", [T, D], F32).ap()
    t_h = nc.dram_tensor("t_h", [T, D], BF16).ap()
    h_h = nc.dram_tensor("h_h", [F, T], BF16).ap()
    scr = nc.dram_tensor("scr", [1, F], F32).ap()
    cc1i = nc.dram_tensor("cc1i", [1, 128], F32).ap()
    cc1o = nc.dram_tensor("cc1o", [1, 128], F32).ap()
    cc2i = nc.dram_tensor("cc2i", [1, 128], F32).ap()
    cc2o = nc.dram_tensor("cc2o", [1, 128], F32).ap()

    vtm3 = vtm.rearrange("(c p) d -> p c d", p=128)
    kfm3 = kfm.rearrange("(c p) s -> p c s", p=128)
    hh3 = h_h.rearrange("(c p) t -> p c t", p=128)
    binv = b_in.rearrange("o (c p) -> (o c) p", p=128)
    scrv = scr.rearrange("o (c p) -> (o c) p", p=128)

    st = contextlib.ExitStack()
    E = st.enter_context
    with st:
        c_eps = E(nc.sbuf_tensor("c_eps", [128, 1], F32))
        c_nshift = E(nc.sbuf_tensor("c_nshift", [128, 1], F32))
        c_zero = E(nc.sbuf_tensor("c_zero", [128, 1], F32))
        ones_row = E(nc.sbuf_tensor("ones_row", [1, 128], F32))
        ones_col = E(nc.sbuf_tensor("ones_col", [128, 1], BF16))
        iota_j = E(nc.sbuf_tensor("iota_j", [128, 128], F32))
        iota_p = E(nc.sbuf_tensor("iota_p", [128, 128], F32))
        ident = E(nc.sbuf_tensor("ident", [128, 128], F32))
        ident_bf = E(nc.sbuf_tensor("ident_bf", [128, 128], BF16))
        amax1a = E(nc.sbuf_tensor("amax1a", [128, 1], F32))
        amax2a = E(nc.sbuf_tensor("amax2a", [128, 1], F32))
        mrow = E(nc.sbuf_tensor("mrow", [1, 128], F32))
        mtop = E(nc.sbuf_tensor("mtop", [1, 1], F32))
        screp = E(nc.sbuf_tensor("screp", [1, 128], F32))
        gmax = E(nc.sbuf_tensor("gmax", [1, 1], F32))
        s1_bc = E(nc.sbuf_tensor("s1_bc", [128, 1], F32))
        inv1_bc = E(nc.sbuf_tensor("inv1_bc", [128, 1], F32))
        s2_bc = E(nc.sbuf_tensor("s2_bc", [128, 1], F32))
        inv2_bc = E(nc.sbuf_tensor("inv2_bc", [128, 1], F32))
        ab_bc = E(nc.sbuf_tensor("ab_bc", [128, 1], F32))
        colmax_pc = E(nc.sbuf_tensor("colmax_pc", [128, FC], F32))
        ginvw_pc = E(nc.sbuf_tensor("ginvw_pc", [128, FC], F32))
        binpc = E(nc.sbuf_tensor("binpc", [128, FC], F32))
        bout_bc = E(nc.sbuf_tensor("bout_bc", [128, D], F32))
        l_sb = E(nc.sbuf_tensor("l_sb", [128, max(NQT, 1)], F32))
        linv = E(nc.sbuf_tensor("linv", [128, max(NQT, 1)], F32))

        ps_st = []
        ps_o = []
        HP = []

        sem_names = ["ms", "ss", "vs", "gs", "bar", "ccs",
                     "d_vtm0", "d_vtm1", "d_vtm2", "d_vtm3",
                     "d_kfm0", "d_kfm1", "d_kfm2", "d_kfm3",
                     "d_xin0", "d_xin1", "d_xin2", "d_xin3",
                     "d_xin4", "d_xin5", "d_xin6", "d_xin7",
                     "d_q", "d_k0", "d_k1", "d_v0", "d_v1",
                     "d_xq0", "d_xq1", "d_xq2", "d_xq3", "d_xq4", "d_xq5",
                     "d_x2w0", "d_tw0", "d_tw1", "d_tw2", "d_tw3",
                     "d_w0", "d_w1", "d_w2", "d_w3", "d_scr", "d_bio", "d_tfm0", "d_tfm1",
                     "d_hw0", "d_hw1", "d_hr0", "d_hr1", "d_xr0", "d_xr1", "d_cc",
                     "d_out0", "d_out1"]
        SM = {name: Sem(st, nc, name) for name in sem_names}
        ms, ss, vs, gs, bar = SM["ms"], SM["ss"], SM["vs"], SM["gs"], SM["bar"]

        blk = E(nc.Block())
        nbar = [0]

        def barrier(extra=()):
            @blk.sync
            def _(y):
                for sm, val in extra:
                    y.wait_ge(sm.h, val)
                bar.inc(y.nop())
                nbar[0] += 1
                y.wait_ge(bar.h, 5 * nbar[0])

            for engname, own in [("vector", vs), ("scalar", ss),
                                 ("tensor", ms), ("gpsimd", gs)]:
                @getattr(blk, engname)
                def _(e, own=own):
                    e.wait_ge(own.h, own.n)
                    bar.inc(e.nop())
                    e.wait_ge(bar.h, 5 * nbar[0])

        NCH = (D + 511) // 512

        def make_stats_bufs(Ep, prefix, n=2):
            d = {k: [Ep(nc.sbuf_tensor(f"{prefix}_{k}{j}", [128, 1], F32))
                     for j in range(n)]
                 for k in ["lnv", "rstd", "nb"]}
            d["s12"] = [Ep(nc.sbuf_tensor(f"{prefix}_s12{j}", [128, 6 * NCH],
                                          F32)) for j in range(n)]
            d["mv"] = [Ep(nc.sbuf_tensor(f"{prefix}_mv{j}", [128, 2], F32))
                       for j in range(n)]
            return d

        def emit_ln_stats(src_ap, sq_scr, bufs, j, W, wait_v=(), wait_s=(),
                          sq_on=None, prev_use=None):
            nch = (W + 511) // 512
            s12 = bufs["s12"][j]
            mv = bufs["mv"][j]

            @blk.vector
            def _(v):
                for smh, val in wait_v:
                    v.wait_ge(smh, val)
                if prev_use is not None:
                    v.wait_ge(prev_use[0].h, prev_use[1])
                for ch in range(nch):
                    w0 = ch * 512
                    w1 = min(W, w0 + 512)
                    vs.inc(v.bn_stats(s12[:, 6 * ch:6 * ch + 6],
                                      src_ap[:, w0:w1]))
                v.wait_ge(vs.h, vs.n)
                vs.inc(v.bn_aggr(mv[:], s12[:, 0:6 * nch]))
            mv_c = vs.n

            @blk.scalar
            def _(s):
                s.wait_ge(vs.h, mv_c)
                ss.inc(s.activation(bufs["lnv"][j][:], mv[:, 1:2], AF.Ln,
                                    scale=1.0, bias=c_eps[:]))
                s.wait_ge(ss.h, ss.n)
                ss.inc(s.activation(bufs["rstd"][j][:], bufs["lnv"][j][:],
                                    AF.Exp, scale=-0.5, bias=c_zero[:]))
            rstd_c = ss.n

            @blk.vector
            def _(v):
                v.wait_ge(ss.h, rstd_c)
                vs.inc(v.tensor_scalar(bufs["nb"][j][:], mv[:, 0:1],
                                       bufs["rstd"][j][:], -1.0,
                                       op0=ALU.mult, op1=ALU.mult))
            nb_c = vs.n
            return rstd_c, nb_c

        def emit_ar_part1(acc_tile, cci, cco, last_vs_for_acc, pbank):
            @blk.tensor
            def _(t):
                t.wait_ge(vs.h, last_vs_for_acc)
                ms.inc(t.transpose(pbank[0:1, 0:128], acc_tile[:], ident[:]))
            tr_c = ms.n

            @blk.vector
            def _(v):
                v.wait_ge(ms.h, tr_c)
                vs.inc(v.tensor_copy(mrow[:], pbank[0:1, 0:128]))
                v.wait_ge(vs.h, vs.n)
                vs.inc(v.reduce_max(mtop[:], mrow[:], axis=AX.X))
                v.wait_ge(vs.h, vs.n)
                vs.inc(v.tensor_scalar(screp[:], mrow[:], 0.0, mtop[0:1, 0:1],
                                       op0=ALU.mult, op1=ALU.add))
            rep_c = vs.n

            @blk.gpsimd
            def _(g):
                g.wait_ge(vs.h, rep_c)
                SM["d_cc"].inc(g.dma_start(out=cci, in_=screp[:]), 16)
                g.wait_ge(SM["d_cc"].h, SM["d_cc"].n)
                SM["ccs"].inc(g.collective_compute(
                    "AllReduce", ALU.max, replica_groups=[list(range(NCORES))],
                    ins=[cci], outs=[cco]))
                g.wait_ge(SM["ccs"].h, SM["ccs"].n)
                SM["d_cc"].inc(g.dma_start(out=gmax[:], in_=cco[0:1, 0:1]), 16)
            return (SM["d_cc"].n, rep_c)

        def emit_ar_part2(s_out, inv_out, qconst, cc_in, pbank):
            cc_c, vsafe = cc_in

            @blk.tensor
            def _(t):
                t.wait_ge(SM["d_cc"].h, cc_c)
                t.wait_ge(vs.h, vsafe)
                ms.inc(t.matmul(pbank[:, 0:1], ones_row[0:1, :], gmax[0:1, :],
                                start=True, stop=True))
            bc_c = ms.n

            @blk.vector
            def _(v):
                v.wait_ge(ms.h, bc_c)
                vs.inc(v.tensor_copy(ab_bc[:], pbank[:, 0:1]))
                v.wait_ge(vs.h, vs.n)
                vs.inc(v.reciprocal(s_out[:], ab_bc[:]))
                v.wait_ge(vs.h, vs.n)
                vs.inc(v.tensor_scalar(s_out[:], s_out[:], float(qconst), None,
                                       op0=ALU.mult))
                vs.inc(v.tensor_scalar(inv_out[:], ab_bc[:], 1.0 / float(qconst),
                                       None, op0=ALU.mult))
            return vs.n

        from concourse.hw_specs import get_activation_tables
        _tab_names = list(get_activation_tables(nc.m.arch).keys())
        _SET_MAIN = _tab_names.index("natural_log_exp_and_others")
        _SET_GELU = _tab_names.index("gelu_and_others")

        def load_act_set(s, set_id):
            s.add_instruction(mybir.InstLoadActFuncSet(
                name=nc.get_next_instruction_name(),
                act_func_set_id=set_id, ins=[], outs=[]))

        @blk.scalar
        def _(s):
            load_act_set(s, _SET_MAIN)

        @blk.gpsimd
        def _(g):
            g.memset(c_eps[:], 1e-5)
            g.memset(c_nshift[:], -float(D))
            g.memset(c_zero[:], 0.0)
            g.memset(ones_row[:], 1.0)
            g.memset(ones_col[:], 1.0)
            g.memset(amax1a[:], 1e-8)
            g.memset(amax2a[:], 1e-8)
            g.iota(iota_j[:], [[1, 128]], channel_multiplier=0,
                   allow_small_or_imprecise_dtypes=True)
            gs.inc(g.iota(iota_p[:], [[0, 128]], channel_multiplier=1,
                          allow_small_or_imprecise_dtypes=True))
        init_g = gs.n

        @blk.vector
        def _(v):
            v.wait_ge(gs.h, init_g)
            vs.inc(v.tensor_tensor(ident[:], iota_j[:], iota_p[:], op=ALU.is_equal))
            v.wait_ge(vs.h, vs.n)
            vs.inc(v.tensor_copy(ident_bf[:], ident[:]))

        barrier()

        with contextlib.ExitStack() as p0:
            Ep = p0.enter_context
            xin2 = [Ep(nc.sbuf_tensor(f"xin2_{j}", [128, D], F32))
                    for j in range(8)]
            vt2 = [Ep(nc.sbuf_tensor(f"vt2_{j}", [128, D], BF16))
                   for j in range(4)]
            kst2 = [Ep(nc.sbuf_tensor(f"kst2_{j}", [128, D], BF16))
                    for j in range(4)]
            p0b = make_stats_bufs(Ep, "p0", n=16)

            ps_tp = [Ep(nc.psum_tensor(f"ps_tp{j}", [128, D], BF16))
                     for j in range(2)]
            norm_done = {}
            rstd_done = {}
            nb_done = {}
            mv_done = {}
            ld_done = {}
            tpose_done = {}
            kcopy_done = {}

            G0 = 8
            for g0 in range(0, NTIL, G0):
                grp = range(g0, min(NTIL, g0 + G0))

                for i in grp:
                    dxin = SM[f"d_xin{i % 8}"]

                    @blk.sync
                    def _(y, i=i, dxin=dxin, nd=norm_done.get(i - 8)):
                        if nd:
                            y.wait_ge(nd[0].h, nd[1])
                        dxin.inc(y.dma_start(
                            out=xin2[i % 8][:],
                            in_=xb[i * 128:(i + 1) * 128, :]), 16)
                    ld_done[i] = (dxin, dxin.n)

                for i in grp:
                    e = i % 16

                    @blk.vector
                    def _(v, i=i, e=e, ld=ld_done[i],
                          pr=rstd_done.get(i - 16)):
                        v.wait_ge(ld[0].h, ld[1])
                        if pr:
                            v.wait_ge(ss.h, pr)
                        for ch in range(NCH):
                            vs.inc(v.bn_stats(
                                p0b["s12"][e][:, 6 * ch:6 * ch + 6],
                                xin2[i % 8][:, ch * 512:min(D, (ch + 1) * 512)]))
                        v.wait_ge(vs.h, vs.n)
                        vs.inc(v.bn_aggr(p0b["mv"][e][:],
                                         p0b["s12"][e][:, 0:6 * NCH]))
                    mv_done[i] = vs.n

                for i in grp:
                    e = i % 16

                    @blk.scalar
                    def _(s, i=i, e=e):
                        s.wait_ge(vs.h, mv_done[i])
                        ss.inc(s.activation(p0b["lnv"][e][:],
                                            p0b["mv"][e][:, 1:2], AF.Ln,
                                            scale=1.0, bias=c_eps[:]))
                        s.wait_ge(ss.h, ss.n)
                        ss.inc(s.activation(p0b["rstd"][e][:],
                                            p0b["lnv"][e][:], AF.Exp,
                                            scale=-0.5, bias=c_zero[:]))
                    rstd_done[i] = ss.n

                for i in grp:
                    e = i % 16

                    @blk.vector
                    def _(v, i=i, e=e):
                        v.wait_ge(ss.h, rstd_done[i])
                        vs.inc(v.tensor_scalar(p0b["nb"][e][:],
                                               p0b["mv"][e][:, 0:1],
                                               p0b["rstd"][e][:], -1.0,
                                               op0=ALU.mult, op1=ALU.mult))
                    nb_done[i] = vs.n

                for i in grp:
                    e = i % 16
                    dvtm = SM[f"d_vtm{i % 4}"]
                    dkf = SM[f"d_kfm{i % 4}"]

                    if i % 2 == 0:
                        @blk.scalar
                        def _(s, i=i, e=e, dvtm=dvtm, dv_c=dvtm.n,
                              td=tpose_done.get(i - 4)):
                            s.wait_ge(vs.h, nb_done[i])
                            if td:
                                s.wait_ge(dvtm.h, dv_c)
                                s.wait_ge(ms.h, td)
                            ss.inc(s.activation(vt2[i % 4][:], xin2[i % 8][:],
                                                AF.Identity,
                                                scale=p0b["rstd"][e][:],
                                                bias=p0b["nb"][e][:]))
                        norm_done[i] = (ss, ss.n)
                    else:
                        @blk.vector
                        def _(v, i=i, e=e, dvtm=dvtm, dv_c=dvtm.n,
                              td=tpose_done.get(i - 4)):
                            v.wait_ge(vs.h, nb_done[i])
                            if td:
                                v.wait_ge(dvtm.h, dv_c)
                                v.wait_ge(ms.h, td)
                            vs.inc(v.tensor_scalar(vt2[i % 4][:],
                                                   xin2[i % 8][:],
                                                   p0b["rstd"][e][:],
                                                   p0b["nb"][e][:],
                                                   op0=ALU.mult, op1=ALU.add))
                        norm_done[i] = (vs, vs.n)

                    @blk.gpsimd
                    def _(g, i=i, dvtm=dvtm, nd=norm_done[i]):
                        g.wait_ge(nd[0].h, nd[1])
                        dvtm.inc(g.dma_start(out=vtm3[:, i, :],
                                             in_=vt2[i % 4][:]), 16)

                    @blk.tensor
                    def _(t, i=i, nd=norm_done[i], kd=kcopy_done.get(i - 2)):
                        t.wait_ge(nd[0].h, nd[1])
                        if kd:
                            t.wait_ge(kd[0].h, kd[1])
                        for dc in range(DC):
                            mm = t.transpose(
                                ps_tp[i % 2][:, dc * 128:(dc + 1) * 128],
                                vt2[i % 4][:, dc * 128:(dc + 1) * 128],
                                ident_bf[:])
                        ms.inc(mm)
                    tpose_done[i] = ms.n

                    if i % 2 == 1:
                        @blk.scalar
                        def _(s, i=i, dkf=dkf, dk_c=dkf.n):
                            s.wait_ge(ms.h, tpose_done[i])
                            if i >= 4:
                                s.wait_ge(dkf.h, dk_c)
                            ss.inc(s.activation(kst2[i % 4][:],
                                                ps_tp[i % 2][:], AF.Copy,
                                                bias=0.0, scale=1.0))
                        kcopy_done[i] = (ss, ss.n)
                    else:
                        @blk.vector
                        def _(v, i=i, dkf=dkf, dk_c=dkf.n):
                            v.wait_ge(ms.h, tpose_done[i])
                            if i >= 4:
                                v.wait_ge(dkf.h, dk_c)
                            vs.inc(v.tensor_copy(kst2[i % 4][:],
                                                 ps_tp[i % 2][:]))
                        kcopy_done[i] = (vs, vs.n)

                    @blk.gpsimd
                    def _(g, i=i, dkf=dkf, kd=kcopy_done[i]):
                        g.wait_ge(kd[0].h, kd[1])
                        dkf.inc(g.dma_start(
                            out=kfm3[:, :, i * 128:(i + 1) * 128],
                            in_=kst2[i % 4][:].rearrange("p (c k) -> p c k",
                                                         c=DC)), 16)

            barrier(extra=[(SM[f"d_kfm{j}"], SM[f"d_kfm{j}"].n)
                           for j in range(4)] +
                          [(SM[f"d_vtm{j}"], SM[f"d_vtm{j}"].n)
                           for j in range(4)])

        ps_st.extend(E(nc.psum_tensor(f"ps_st{i}", [128, 512], F32))
                     for i in range(4))
        ps_o.extend(E(nc.psum_tensor(f"ps_o{i}", [128, 1024], F32))
                    for i in range(2))
        HP.extend([ps_st[0][:], ps_st[1][:], ps_st[2][:], ps_st[3][:],
                   ps_o[0][:, 0:512], ps_o[0][:, 512:1024],
                   ps_o[1][:, 0:512], ps_o[1][:, 512:1024]])

        with contextlib.ExitStack() as p2:
            Ep = p2.enter_context
            Qt = Ep(nc.sbuf_tensor("Qt", [128, DC, SWQ], BF16))
            K2 = [Ep(nc.sbuf_tensor(f"K2_{j}", [128, DC, KBG], BF16)) for j in range(2)]
            V2 = [Ep(nc.sbuf_tensor(f"V2_{j}", [128, KC, D], BF16)) for j in range(2)]
            PT = Ep(nc.sbuf_tensor("PT", [128, KC, NQG, 512], BF16))
            O_sb = [Ep(nc.sbuf_tensor(f"O_sb{q}", [128, D], F32)) for q in range(NQT)]
            xr = [Ep(nc.sbuf_tensor(f"xr_{j}", [128, D], F32)) for j in range(6)]
            ln12 = [Ep(nc.sbuf_tensor(f"ln12_{j}", [128, D], F32)) for j in range(4)]
            tt2 = [Ep(nc.sbuf_tensor(f"tt2_{j}", [128, D], BF16)) for j in range(4)]
            tmax2 = [Ep(nc.sbuf_tensor(f"tmax2_{j}", [128, 1], F32)) for j in range(16)]
            linv2 = Ep(nc.sbuf_tensor("linv2", [128, max(NQT, 1)], F32))
            eb1 = make_stats_bufs(Ep, "eb1", n=16)
            eb2 = make_stats_bufs(Ep, "eb2", n=16)

            npp = (NQG + 1) // 2
            st_seq = []
            for pp in range(npp):
                qgs = [g for g in (2 * pp, 2 * pp + 1) if g < NQG]
                for kc in range(KC):
                    st_seq.append((pp, kc, qgs,
                                   [2 * gi + kc % 2 for gi in range(len(qgs))]))
            exps_per_kbg = sum(len(e[2]) for e in st_seq)

            bank_exp = {0: 0, 1: 0, 2: 0, 3: 0}
            bank_oev = {0: 0, 1: 0}
            l_evac_c = 0
            kv_used = {}
            att_prev = {}
            att_prev_dma = 0
            ep_counts = {}
            xq_map = {}
            amax_upd = 0

            for sw in range(NSW):
                q0 = sw * SWQ

                @blk.sync
                def _(y, sw=sw, q0=q0):
                    if sw > 0:
                        y.wait_ge(ms.h, kv_used[(sw - 1, NKBG - 1)])
                    SM["d_q"].inc(y.dma_start(out=Qt[:],
                                              in_=kfm3[:, :, q0:q0 + SWQ]), 16)
                q_c = SM["d_q"].n

                oev = {}
                for kbg in range(NKBG):
                    j = kbg % 2
                    k0 = kbg * KBG
                    dk, dv = SM[f"d_k{j}"], SM[f"d_v{j}"]

                    @blk.sync
                    def _(y, sw=sw, kbg=kbg, j=j, k0=k0, dk=dk, dv=dv):
                        key = (sw, kbg - 2) if kbg >= 2 else \
                              (sw - 1, NKBG - 2 + kbg)
                        if key in kv_used:
                            y.wait_ge(ms.h, kv_used[key])
                        dk.inc(y.dma_start(out=K2[j][:],
                                           in_=kfm3[:, :, k0:k0 + KBG]), 16)
                        dv.inc(y.dma_start(
                            out=V2[j][:],
                            in_=vtm[k0:k0 + KBG, :].rearrange(
                                "(c p) d -> p c d", p=128)), 16)
                    k_c, v_c = dk.n, dv.n

                    ss_base = ss.n
                    ms_base = ms.n
                    exp_cnt = {}
                    st_done = {}
                    run = 0
                    for idx, (pp, kc, qgs, banks) in enumerate(st_seq):
                        st_done[(pp, kc)] = ms_base + idx + 1
                        for gi in range(len(qgs)):
                            run += 1
                            exp_cnt[(pp, kc, gi)] = ss_base + run
                    pv_base = ms_base + len(st_seq)
                    vs_pv_base = vs.n

                    bank_last = dict(bank_exp)

                    @blk.tensor
                    def _(t, j=j, k_c=k_c, q_c=q_c, bank_last=bank_last,
                          lec=l_evac_c):
                        t.wait_ge(SM["d_q"].h, q_c)
                        t.wait_ge(SM[f"d_k{j}"].h, k_c)
                        if lec:
                            t.wait_ge(vs.h, lec)
                        for (pp, kc, qgs, banks) in st_seq:
                            for bk in banks:
                                if bank_last[bk]:
                                    t.wait_ge(ss.h, bank_last[bk])
                            for dc in range(DC):
                                lhs = K2[j][:, dc, kc * 128:(kc + 1) * 128]
                                for gi, qg in enumerate(qgs):
                                    mm = t.matmul(
                                        ps_st[banks[gi]][:], lhs,
                                        Qt[:, dc, qg * 512:(qg + 1) * 512],
                                        start=(dc == 0), stop=(dc == DC - 1))
                            ms.inc(mm)
                            for gi, bk in enumerate(banks):
                                bank_last[bk] = exp_cnt[(pp, kc, gi)]
                    assert ms.n == pv_base

                    @blk.scalar
                    def _(s):
                        for (pp, kc, qgs, banks) in st_seq:
                            s.wait_ge(ms.h, st_done[(pp, kc)])
                            for gi, qg in enumerate(qgs):
                                ss.inc(s.activation(
                                    PT[:, kc, qg, :], ps_st[banks[gi]][:],
                                    AF.Exp, bias=c_nshift[:], scale=1.0))
                    assert ss.n == ss_base + exps_per_kbg
                    exp_all = ss.n
                    for (pp, kc, qgs, banks) in st_seq:
                        for gi, bk in enumerate(banks):
                            bank_exp[bk] = exp_cnt[(pp, kc, gi)]

                    @blk.tensor
                    def _(t, sw=sw, kbg=kbg, j=j, v_c=v_c, exp_all=exp_all,
                          vs_pv_base=vs_pv_base, boev=dict(bank_oev)):
                        t.wait_ge(ss.h, exp_all)
                        t.wait_ge(SM[f"d_v{j}"].h, v_c)
                        for qt in range(NQT):
                            qg, qc = divmod(qt, 4)
                            ob = qt % 2
                            if qt >= 2:
                                t.wait_ge(vs.h, vs_pv_base + (qt - 2) + 1)
                            elif boev[ob]:
                                t.wait_ge(vs.h, boev[ob])
                            for kc in range(KC):
                                lhs = PT[:, kc, qg, qc * 128:(qc + 1) * 128]
                                for dvt in range(NDV):
                                    t.matmul(
                                        ps_o[ob][:, dvt * 512:dvt * 512 + DVW],
                                        lhs,
                                        V2[j][:, kc, dvt * 512:dvt * 512 + DVW],
                                        start=(kc == 0), stop=(kc == KC - 1))
                                mm = t.matmul(ps_st[0][:, qt:qt + 1], lhs,
                                              ones_col[:], start=(kc == 0),
                                              stop=(kc == KC - 1))
                            ms.inc(mm)
                    assert ms.n == pv_base + NQT
                    kv_used[(sw, kbg)] = ms.n

                    @blk.vector
                    def _(v, sw=sw, kbg=kbg, pv_base=pv_base, lec=l_evac_c):
                        for qt in range(NQT):
                            ob = qt % 2
                            v.wait_ge(ms.h, pv_base + qt + 1)
                            if kbg > 0:
                                v.wait_ge(vs.h, oev[(kbg - 1, qt)])
                            elif sw > 0:
                                apm, apc = att_prev[qt]
                                v.wait_ge(apm.h, apc)
                                v.wait_ge(SM["d_x2w0"].h, att_prev_dma)
                            if kbg == 0:
                                vs.inc(v.tensor_copy(O_sb[qt][:],
                                                     ps_o[ob][:, 0:D]))
                            else:
                                vs.inc(v.tensor_tensor(O_sb[qt][:], O_sb[qt][:],
                                                       ps_o[ob][:, 0:D],
                                                       op=ALU.add))
                            oev[(kbg, qt)] = vs.n
                            bank_oev[ob] = vs.n
                        if lec:
                            v.wait_ge(vs.h, lec)
                        if kbg == 0:
                            vs.inc(v.tensor_copy(l_sb[:, 0:NQT],
                                                 ps_st[0][:, 0:NQT]))
                        else:
                            vs.inc(v.tensor_tensor(l_sb[:, 0:NQT], l_sb[:, 0:NQT],
                                                   ps_st[0][:, 0:NQT],
                                                   op=ALU.add))
                    l_evac_c = vs.n
                    assert vs.n == vs_pv_base + NQT + 1

                @blk.vector
                def _(v, lec=l_evac_c):
                    v.wait_ge(vs.h, lec)
                    vs.inc(v.reciprocal(linv[:, 0:NQT], l_sb[:, 0:NQT]))
                    v.wait_ge(vs.h, vs.n)
                    vs.inc(v.tensor_tensor(linv2[:, 0:NQT], linv[:, 0:NQT],
                                           linv[:, 0:NQT], op=ALU.mult))
                linv2_c = vs.n

                gqs = [sw * NQT + qt for qt in range(NQT)]
                mv1_c = {}; rstd_c = {}; bq_c = {}; ln1_c = {}
                x2_c = {}; nb2_c = {}; tn_c = {}

                def emit_xq(qt_l):
                    gql = gqs[qt_l]
                    dxql = SM[f"d_xq{gql % 6}"]

                    @blk.gpsimd
                    def _(g, qt_l=qt_l, gql=gql, dxql=dxql):
                        if (gql - 6, "x2add") in ep_counts:
                            sm2, c2 = ep_counts[(gql - 6, "x2add")]
                            g.wait_ge(sm2.h, c2)
                        dxql.inc(g.dma_start(
                            out=xr[gql % 6][:],
                            in_=xb[q0 + qt_l * 128:
                                   q0 + (qt_l + 1) * 128, :]), 16)
                    xq_map[gql] = (dxql, dxql.n)

                for qt in range(min(6, NQT)):
                    emit_xq(qt)

                for qt in range(NQT):
                    gq = gqs[qt]
                    e = gq % 16

                    @blk.vector
                    def _(v, qt=qt, gq=gq, e=e):
                        if (gq - 16, "rstd1") in ep_counts:
                            sm2, c2 = ep_counts[(gq - 16, "rstd1")]
                            v.wait_ge(sm2.h, c2)
                        for ch in range(NCH):
                            vs.inc(v.bn_stats(
                                eb1["s12"][e][:, 6 * ch:6 * ch + 6],
                                O_sb[qt][:, ch * 512:min(D, (ch + 1) * 512)]))
                        v.wait_ge(vs.h, vs.n)
                        vs.inc(v.bn_aggr(eb1["mv"][e][:],
                                         eb1["s12"][e][:, 0:6 * NCH]))
                    mv1_c[gq] = vs.n

                for qt in range(NQT):
                    gq = gqs[qt]
                    e = gq % 16

                    @blk.scalar
                    def _(s, qt=qt, gq=gq, e=e, l2c=linv2_c):
                        s.wait_ge(vs.h, max(mv1_c[gq], l2c))
                        ss.inc(s.activation(eb1["lnv"][e][:],
                                            eb1["mv"][e][:, 1:2],
                                            AF.Ln, scale=linv2[:, qt:qt + 1],
                                            bias=c_eps[:]))
                        s.wait_ge(ss.h, ss.n)
                        ss.inc(s.activation(eb1["rstd"][e][:],
                                            eb1["lnv"][e][:], AF.Exp,
                                            scale=-0.5, bias=c_zero[:]))
                    rstd_c[gq] = ss.n
                    ep_counts[(gq, "rstd1")] = (ss, ss.n)

                for qt in range(NQT):
                    gq = gqs[qt]
                    e = gq % 16

                    @blk.vector
                    def _(v, qt=qt, gq=gq, e=e):
                        v.wait_ge(ss.h, rstd_c[gq])
                        vs.inc(v.tensor_scalar(eb1["rstd"][e][:],
                                               eb1["rstd"][e][:],
                                               linv[:, qt:qt + 1], 1.0,
                                               op0=ALU.mult, op1=ALU.mult))
                        v.wait_ge(vs.h, vs.n)
                        vs.inc(v.tensor_scalar(eb1["nb"][e][:],
                                               eb1["mv"][e][:, 0:1],
                                               eb1["rstd"][e][:], -1.0,
                                               op0=ALU.mult, op1=ALU.mult))
                    bq_c[gq] = vs.n

                for qt in range(NQT):
                    gq = gqs[qt]
                    e = gq % 16

                    @blk.scalar
                    def _(s, qt=qt, gq=gq, e=e):
                        s.wait_ge(vs.h, bq_c[gq])
                        if (gq - 4, "x2add") in ep_counts:
                            sm2, c2 = ep_counts[(gq - 4, "x2add")]
                            s.wait_ge(sm2.h, c2)
                        ss.inc(s.activation(ln12[gq % 4][:], O_sb[qt][:],
                                            AF.Identity,
                                            scale=eb1["rstd"][e][:],
                                            bias=eb1["nb"][e][:]))
                    ln1_c[gq] = ss.n

                    dxq, xq_cc = xq_map[gq]

                    @blk.vector
                    def _(v, qt=qt, gq=gq, dxq=dxq, xq_cc=xq_cc):
                        v.wait_ge(dxq.h, xq_cc)
                        v.wait_ge(ss.h, ln1_c[gq])
                        vs.inc(v.tensor_tensor(O_sb[qt][:], ln12[gq % 4][:],
                                               xr[gq % 6][:], op=ALU.add))
                    ep_counts[(gq, "x2add")] = (vs, vs.n)
                    x2_c[gq] = vs.n

                    if qt + 6 < NQT:
                        emit_xq(qt + 6)

                for qt in range(NQT):
                    gq = gqs[qt]

                    @blk.gpsimd
                    def _(g, qt=qt, gq=gq):
                        g.wait_ge(vs.h, x2_c[gq])
                        SM["d_x2w0"].inc(g.dma_start(
                            out=x2h[q0 + qt * 128:q0 + (qt + 1) * 128, :],
                            in_=O_sb[qt][:]), 16)

                for qt in range(NQT):
                    gq = gqs[qt]
                    e = gq % 16

                    @blk.vector
                    def _(v, qt=qt, gq=gq, e=e):
                        v.wait_ge(vs.h, x2_c[gq])
                        if (gq - 16, "rstd2") in ep_counts:
                            sm2, c2 = ep_counts[(gq - 16, "rstd2")]
                            v.wait_ge(sm2.h, c2)
                        for ch in range(NCH):
                            vs.inc(v.bn_stats(
                                eb2["s12"][e][:, 6 * ch:6 * ch + 6],
                                O_sb[qt][:, ch * 512:min(D, (ch + 1) * 512)]))
                        v.wait_ge(vs.h, vs.n)
                        vs.inc(v.bn_aggr(eb2["mv"][e][:],
                                         eb2["s12"][e][:, 0:6 * NCH]))
                    mv1_c[(gq, 2)] = vs.n

                for qt in range(NQT):
                    gq = gqs[qt]
                    e = gq % 16

                    @blk.scalar
                    def _(s, qt=qt, gq=gq, e=e):
                        s.wait_ge(vs.h, mv1_c[(gq, 2)])
                        ss.inc(s.activation(eb2["lnv"][e][:],
                                            eb2["mv"][e][:, 1:2],
                                            AF.Ln, scale=1.0, bias=c_eps[:]))
                        s.wait_ge(ss.h, ss.n)
                        ss.inc(s.activation(eb2["rstd"][e][:],
                                            eb2["lnv"][e][:], AF.Exp,
                                            scale=-0.5, bias=c_zero[:]))
                    rstd_c[(gq, 2)] = ss.n
                    ep_counts[(gq, "rstd2")] = (ss, ss.n)

                for qt in range(NQT):
                    gq = gqs[qt]
                    e = gq % 16

                    @blk.vector
                    def _(v, qt=qt, gq=gq, e=e):
                        v.wait_ge(ss.h, rstd_c[(gq, 2)])
                        vs.inc(v.tensor_scalar(eb2["nb"][e][:],
                                               eb2["mv"][e][:, 0:1],
                                               eb2["rstd"][e][:], -1.0,
                                               op0=ALU.mult, op1=ALU.mult))
                    nb2_c[gq] = vs.n

                for qt in range(NQT):
                    gq = gqs[qt]
                    e = gq % 16
                    dtw = SM[f"d_tw{gq % 4}"]

                    @blk.scalar
                    def _(s, qt=qt, gq=gq, e=e, dtw=dtw, dtw_c=dtw.n):
                        s.wait_ge(vs.h, nb2_c[gq])
                        if (gq - 4, "amax") in ep_counts:
                            s.wait_ge(dtw.h, dtw_c)
                            sm2, c2 = ep_counts[(gq - 4, "amax")]
                            s.wait_ge(sm2.h, c2)
                        ss.inc(s.activation(tt2[gq % 4][:], O_sb[qt][:],
                                            AF.Identity,
                                            scale=eb2["rstd"][e][:],
                                            bias=eb2["nb"][e][:]))
                    tn_c[gq] = ss.n
                    ep_counts[(gq, "tnorm")] = (ss, ss.n)

                    @blk.vector
                    def _(v, qt=qt, gq=gq, au=amax_upd):
                        v.wait_ge(ss.h, tn_c[gq])
                        if au:
                            v.wait_ge(vs.h, au)
                        vs.inc(v.reduce_max(tmax2[gq % 16][:], tt2[gq % 4][:],
                                            axis=AX.X,
                                            apply_absolute_value=True))
                        v.wait_ge(vs.h, vs.n)
                        vs.inc(v.tensor_tensor(amax1a[:], amax1a[:],
                                               tmax2[gq % 16][:], op=ALU.max))
                    amax_upd = vs.n
                    ep_counts[(gq, "amax")] = (vs, amax_upd)

                    @blk.gpsimd
                    def _(g, qt=qt, gq=gq):
                        g.wait_ge(ss.h, tn_c[gq])
                        SM[f"d_tw{gq % 4}"].inc(g.dma_start(
                            out=t_h[q0 + qt * 128:q0 + (qt + 1) * 128, :],
                            in_=tt2[gq % 4][:]), 16)

                att_prev = {qt: (ss, tn_c[gqs[qt]]) for qt in range(NQT)}
                att_prev_dma = SM["d_x2w0"].n

            barrier(extra=[(SM["d_x2w0"], SM["d_x2w0"].n)] +
                          [(SM[f"d_tw{j}"], SM[f"d_tw{j}"].n)
                           for j in range(4)])

        with contextlib.ExitStack() as pf:
            Ef = pf.enter_context
            W_sb = Ef(nc.sbuf_tensor("W_sb", [128, DC * F], BF16))
            Wi_sb = W_sb[:].rearrange("p (c f) -> p c f", c=DC)
            Wo_sb = W_sb[:].rearrange("p (c d) -> p c d", c=FC)

            tfm_st = contextlib.ExitStack()
            tfm2 = [tfm_st.enter_context(nc.sbuf_tensor(f"tfm2_{j}", [128, T],
                                                        BF16))
                    for j in range(2)]
            ar1_cc = emit_ar_part1(amax1a, cc1i, cc1o, amax_upd, ps_o[1])

            with contextlib.ExitStack() as ar1:
                Ea = ar1.enter_context
                wt2 = [Ea(nc.sbuf_tensor(f"wt2_{j}", [128, F], F32))
                       for j in range(2)]

                @blk.sync
                def _(y):
                    for dc0 in range(min(2, DC)):
                        SM[f"d_tfm{dc0 % 2}"].inc(y.dma_start_transpose(
                            tfm2[dc0 % 2][:],
                            t_h[:, dc0 * 128:(dc0 + 1) * 128]), 16)
                macc = Ea(nc.sbuf_tensor("macc", [128, F], F32))
                prodw = Ea(nc.sbuf_tensor("prodw", [128, F], F32))
                qi_w = Ea(nc.sbuf_tensor("qi_w", [128, F], I32))
                s_pc = Ea(nc.sbuf_tensor("s_pc", [128, FC], F32))
                s32 = Ea(nc.sbuf_tensor("s32", [FC, 128], F32))
                srow = Ea(nc.sbuf_tensor("srow", [1, F], F32))
                s_bcast = Ea(nc.sbuf_tensor("s_bcast", [128, F], F32))
                binfc = Ea(nc.sbuf_tensor("binfc", [FC, 128], F32))
                boutrow = Ea(nc.sbuf_tensor("boutrow", [1, D], F32))

                @blk.vector
                def _(v):
                    vs.inc(v.memset(macc[:], 1e-8))
                macc_c = vs.n

                w_rd = {}
                for dc in range(DC):
                    j = dc % 2
                    dw = SM[f"d_w{j}"]

                    @blk.sync
                    def _(y, dc=dc, j=j, dw=dw):
                        if dc >= 2:
                            y.wait_ge(ss.h, w_rd[dc - 2])
                        dw.inc(y.dma_start(
                            out=wt2[j][:],
                            in_=w_in[dc * 128:(dc + 1) * 128, :]), 16)
                    wl_c = dw.n

                    @blk.scalar
                    def _(s, j=j, dw=dw, wl_c=wl_c, mc=macc_c):
                        s.wait_ge(dw.h, wl_c)
                        if isinstance(mc, tuple):
                            s.wait_ge(mc[0].h, mc[1])
                        else:
                            s.wait_ge(vs.h, mc)
                        ss.inc(s.activation(prodw[:], wt2[j][:], AF.Abs,
                                            bias=c_zero[:], scale=1.0))
                    abs_c = ss.n
                    w_rd[dc] = abs_c

                    @blk.vector
                    def _(v, abs_c=abs_c, mc=macc_c):
                        v.wait_ge(ss.h, abs_c)
                        if isinstance(mc, tuple):
                            v.wait_ge(mc[0].h, mc[1])
                        else:
                            v.wait_ge(vs.h, mc)
                        vs.inc(v.tensor_tensor(macc[:], prodw[:], macc[:],
                                               op=ALU.max))
                    macc_c = (vs, vs.n)

                cm_c = macc_c
                for c in range(FC):
                    pb = 2 + (c % 2)

                    @blk.tensor
                    def _(t, c=c, pb=pb, cm_c=cm_c):
                        if isinstance(cm_c, tuple):
                            t.wait_ge(cm_c[0].h, cm_c[1])
                        else:
                            t.wait_ge(vs.h, cm_c)
                        ms.inc(t.transpose(ps_st[pb][0:128, 0:128],
                                           macc[:, c * 128:(c + 1) * 128],
                                           ident[:]))
                    tr_c = ms.n

                    @blk.vector
                    def _(v, c=c, pb=pb, tr_c=tr_c):
                        v.wait_ge(ms.h, tr_c)
                        vs.inc(v.reduce_max(colmax_pc[:, c:c + 1],
                                            ps_st[pb][0:128, 0:128], axis=AX.X))
                    cm_c = vs.n

                @blk.vector
                def _(v, cm_c=cm_c):
                    v.wait_ge(vs.h, cm_c)
                    vs.inc(v.tensor_scalar(ginvw_pc[:], colmax_pc[:],
                                           1.0 / 127.0, None, op0=ALU.mult))
                    vs.inc(v.reciprocal(s_pc[:], colmax_pc[:]))
                    v.wait_ge(vs.h, vs.n)
                    vs.inc(v.tensor_scalar(s_pc[:], s_pc[:], 127.0, None,
                                           op0=ALU.mult))
                spc_c = vs.n

                @blk.tensor
                def _(t, spc_c=spc_c):
                    t.wait_ge(vs.h, spc_c)
                    ms.inc(t.transpose(ps_st[2][0:FC, 0:128], s_pc[:], ident[:]))
                trs_c = ms.n

                @blk.vector
                def _(v, trs_c=trs_c):
                    v.wait_ge(ms.h, trs_c)
                    vs.inc(v.tensor_copy(s32[:], ps_st[2][0:FC, 0:128]))
                s32_c = vs.n

                @blk.sync
                def _(y, s32_c=s32_c):
                    y.wait_ge(vs.h, s32_c)
                    SM["d_scr"].inc(y.dma_start(out=scrv, in_=s32[:]), 16)
                    y.wait_ge(SM["d_scr"].h, SM["d_scr"].n)
                    SM["d_scr"].inc(y.dma_start(out=srow[:], in_=scr), 16)
                    SM["d_bio"].inc(y.dma_start(out=binfc[:], in_=binv), 16)
                    SM["d_bio"].inc(y.dma_start(out=boutrow[:], in_=b_out), 16)
                scr_c = SM["d_scr"].n
                bio_c = SM["d_bio"].n

                sb_c = 0
                for jj in range((F + 511) // 512):
                    w = min(512, F - jj * 512)
                    pb = 2 + (jj % 2)

                    @blk.tensor
                    def _(t, jj=jj, w=w, pb=pb, scr_c=scr_c, sb_c=sb_c):
                        t.wait_ge(SM["d_scr"].h, scr_c)
                        if sb_c:
                            t.wait_ge(vs.h, sb_c)
                        ms.inc(t.matmul(ps_st[pb][:, 0:w], ones_row[0:1, :],
                                        srow[0:1, jj * 512:jj * 512 + w],
                                        start=True, stop=True))
                    mmc = ms.n

                    @blk.vector
                    def _(v, jj=jj, w=w, pb=pb, mmc=mmc):
                        v.wait_ge(ms.h, mmc)
                        vs.inc(v.tensor_copy(s_bcast[:, jj * 512:jj * 512 + w],
                                             ps_st[pb][:, 0:w]))
                    sb_c = vs.n

                @blk.tensor
                def _(t, bio_c=bio_c, sb_c=sb_c):
                    t.wait_ge(SM["d_bio"].h, bio_c)
                    t.wait_ge(vs.h, sb_c)
                    ms.inc(t.transpose(ps_st[0][0:128, 0:FC], binfc[0:FC, :],
                                       ident[0:FC, 0:FC]))
                    for jj in range((D + 511) // 512):
                        w = min(512, D - jj * 512)
                        tgt = ps_st[1][:, 0:w] if jj == 0 else ps_o[0][:, 0:w]
                        ms.inc(t.matmul(tgt, ones_row[0:1, :],
                                        boutrow[0:1, jj * 512:jj * 512 + w],
                                        start=True, stop=True))
                bio_mm = ms.n

                @blk.vector
                def _(v, bio_mm=bio_mm):
                    v.wait_ge(ms.h, bio_mm)
                    vs.inc(v.tensor_copy(binpc[:, 0:FC], ps_st[0][0:128, 0:FC]))
                    vs.inc(v.tensor_copy(bout_bc[:, 0:min(512, D)],
                                         ps_st[1][:, 0:min(512, D)]))
                    if D > 512:
                        vs.inc(v.tensor_copy(bout_bc[:, 512:D],
                                             ps_o[0][:, 0:D - 512]))
                bio_v = vs.n

                wq_rd = {}
                cp1_cs = {}
                gw_cs = {}
                for dc in range(DC):
                    j = dc % 2
                    dw = SM[f"d_w{j}"]

                    @blk.sync
                    def _(y, dc=dc, j=j, dw=dw):
                        if dc - 2 in gw_cs:
                            y.wait_ge(gs.h, gw_cs[dc - 2])
                        else:
                            y.wait_ge(ss.h, w_rd.get(DC - 2 + dc, w_rd[DC - 1]))
                        dw.inc(y.dma_start(
                            out=wt2[j][:],
                            in_=w_in[dc * 128:(dc + 1) * 128, :]), 16)
                    wl_c = dw.n

                    @blk.gpsimd
                    def _(g, j=j, dw=dw, wl_c=wl_c, sb_c=sb_c,
                          pw=cp1_cs.get(dc - 1, 0)):
                        g.wait_ge(dw.h, wl_c)
                        g.wait_ge(vs.h, max(sb_c, pw))
                        gs.inc(g.tensor_tensor(prodw[:], wt2[j][:], s_bcast[:],
                                               op=ALU.mult))
                    gw_cs[dc] = gs.n

                    @blk.vector
                    def _(v, dc=dc, gw_c=gw_cs[dc], pq=wq_rd.get(dc - 1, 0)):
                        v.wait_ge(gs.h, gw_c)
                        if pq:
                            v.wait_ge(ss.h, pq)
                        vs.inc(v.tensor_copy(qi_w[:], prodw[:]))
                    cp1_cs[dc] = vs.n

                    @blk.scalar
                    def _(s, dc=dc, cp1_c=cp1_cs[dc]):
                        s.wait_ge(vs.h, cp1_c)
                        ss.inc(s.activation(Wi_sb[:, dc, :], qi_w[:],
                                            AF.Copy, bias=0.0, scale=1.0))
                    wq_rd[dc] = ss.n

                scales1_c = emit_ar_part2(s1_bc, inv1_bc, 32767.0, ar1_cc,
                                          ps_o[1])
                barrier(extra=[(SM["d_scr"], SM["d_scr"].n)])

            with contextlib.ExitStack() as pa:
                Ea = pa.enter_context
                tq = Ea(nc.sbuf_tensor("tq", [128, DC, T], BF16))
                qi_t = [Ea(nc.sbuf_tensor(f"qi_t{j}", [128, T], I32))
                        for j in range(2)]
                hst2 = [Ea(nc.sbuf_tensor(f"hst2_{j}", [128, T], BF16))
                        for j in range(2)]
                hred2 = [Ea(nc.sbuf_tensor(f"hred2_{j}", [128, 1], F32))
                         for j in range(2)]

                @blk.scalar
                def _(s):
                    load_act_set(s, _SET_GELU)

                tq_cs = {}
                qit_ss = 0
                for dc in range(DC):
                    j = dc % 2
                    dt = SM[f"d_tfm{j}"]
                    tl_c = dt.n

                    @blk.scalar
                    def _(s, dc=dc, j=j, dt=dt, tl_c=tl_c,
                          tq_c=tq_cs.get(dc - 2, 0), s1c=scales1_c):
                        s.wait_ge(dt.h, tl_c)
                        if dc == 0:
                            s.wait_ge(vs.h, s1c)
                        if tq_c:
                            s.wait_ge(vs.h, tq_c)
                        ss.inc(s.activation(qi_t[j][:], tfm2[j][:], AF.Copy,
                                            bias=0.0, scale=s1_bc[:]))
                    qit_ss = ss.n

                    @blk.vector
                    def _(v, dc=dc, j=j, qit_ss=qit_ss):
                        v.wait_ge(ss.h, qit_ss)
                        vs.inc(v.tensor_scalar(tq[:, dc, :], qi_t[j][:],
                                               inv1_bc[:], None,
                                               op0=ALU.mult))
                    tq_cs[dc] = vs.n
                    tq_c = vs.n

                    if dc + 2 < DC:
                        @blk.sync
                        def _(y, dc=dc, j=j, dt=dt, qit_ss=qit_ss):
                            y.wait_ge(ss.h, qit_ss)
                            dt.inc(y.dma_start_transpose(
                                tfm2[j][:],
                                t_h[:, (dc + 2) * 128:(dc + 3) * 128]), 16)
                tq_all = tq_c

                gelu_done = {}
                hred_done = {}
                for fc in range(FC):
                    jf = fc % 2
                    dh = SM[f"d_hw{jf}"]
                    ms_base = ms.n
                    mm1_done = {tt: ms_base + tt + 1 for tt in range(NTT)}

                    @blk.tensor
                    def _(t, fc=fc, tq_all=tq_all, gd=dict(gelu_done)):
                        if fc == 0:
                            t.wait_ge(vs.h, tq_all)
                        for tt in range(NTT - 1):
                            if gd.get((fc - 1, tt)):
                                t.wait_ge(ss.h, gd[(fc - 1, tt)])
                        for dc in range(DC):
                            for tt in range(NTT - 1):
                                mm = t.matmul(
                                    HP[tt],
                                    Wi_sb[:, dc, fc * 128:(fc + 1) * 128],
                                    tq[:, dc, tt * 512:(tt + 1) * 512],
                                    start=(dc == 0), stop=(dc == DC - 1))
                                if dc == DC - 1:
                                    ms.inc(mm)
                        if gd.get((fc - 1, NTT - 1)):
                            t.wait_ge(ss.h, gd[(fc - 1, NTT - 1)])
                        for dc in range(DC):
                            mm = t.matmul(
                                HP[NTT - 1],
                                Wi_sb[:, dc, fc * 128:(fc + 1) * 128],
                                tq[:, dc, (NTT - 1) * 512:NTT * 512],
                                start=(dc == 0), stop=(dc == DC - 1))
                        ms.inc(mm)
                    assert ms.n == ms_base + NTT

                    @blk.scalar
                    def _(s, fc=fc, jf=jf, dh=dh, mm1_done=mm1_done):
                        for tt in range(NTT):
                            s.wait_ge(ms.h, mm1_done[tt])
                            if fc >= 2 and tt == 0:
                                s.wait_ge(dh.h, 16 * (fc // 2))
                                s.wait_ge(vs.h, hred_done[fc - 2])
                            ss.inc(s.activation(
                                hst2[jf][:, tt * 512:(tt + 1) * 512], HP[tt],
                                AF.Gelu, scale=ginvw_pc[:, fc:fc + 1],
                                bias=binpc[:, fc:fc + 1]))
                            gelu_done[(fc, tt)] = ss.n
                    g_all = ss.n

                    @blk.gpsimd
                    def _(g, fc=fc, jf=jf, dh=dh, g_all=g_all):
                        g.wait_ge(ss.h, g_all)
                        dh.inc(g.dma_start(out=hh3[:, fc, :], in_=hst2[jf][:]), 16)

                    @blk.vector
                    def _(v, fc=fc, jf=jf, g_all=g_all,
                          prev=(hred_done.get(fc - 2, 0))):
                        v.wait_ge(ss.h, g_all)
                        if prev:
                            v.wait_ge(vs.h, prev)
                        vs.inc(v.reduce_max(hred2[jf][:], hst2[jf][:], axis=AX.X,
                                            apply_absolute_value=True))
                        v.wait_ge(vs.h, vs.n)
                        vs.inc(v.tensor_tensor(amax2a[:], amax2a[:], hred2[jf][:],
                                               op=ALU.max))
                    hred_done[fc] = vs.n
                am2_c = vs.n

                barrier(extra=[(SM["d_hw0"], SM["d_hw0"].n),
                               (SM["d_hw1"], SM["d_hw1"].n)])

            tfm_st.close()

            @blk.scalar
            def _(s):
                load_act_set(s, _SET_MAIN)

            ht2 = [Ef(nc.sbuf_tensor(f"ht2_{j}", [128, FC, 512], BF16))
                   for j in range(2)]
            hq_done = {}
            hq_ss = {}
            hload_c = {}
            pe_ht_done = {}

            def emit_hload(tt):
                j = tt % 2
                dhr = SM[f"d_hr{j}"]

                @blk.sync
                def _(y, tt=tt, j=j, dhr=dhr):
                    if tt >= 2:
                        y.wait_ge(ms.h, pe_ht_done[tt - 2])
                        y.wait_ge(vs.h, hq_done[tt - 2])
                        y.wait_ge(ss.h, hq_ss[tt - 2])
                    dhr.inc(y.dma_start(
                        out=ht2[j][:],
                        in_=hh3[:, :, tt * 512:(tt + 1) * 512]), 16)
                hload_c[tt] = dhr.n

            emit_hload(0)
            if NTT > 1:
                emit_hload(1)

            ar2_cc = emit_ar_part1(amax2a, cc2i, cc2o, am2_c, ps_o[1])

            with contextlib.ExitStack() as ar2:
                Ea = ar2.enter_context
                wo2 = [Ea(nc.sbuf_tensor(f"wo2_{j}", [128, D], F32))
                       for j in range(4)]
                qi_o = [Ea(nc.sbuf_tensor(f"qi_o{j}", [128, D], I32))
                        for j in range(4)]
                rmax2 = [Ea(nc.sbuf_tensor(f"rmax2_{j}", [128, 1], F32))
                         for j in range(4)]
                rs2 = [Ea(nc.sbuf_tensor(f"rs2_{j}", [128, 1], F32))
                       for j in range(4)]
                ri2 = [Ea(nc.sbuf_tensor(f"ri2_{j}", [128, 1], F32))
                       for j in range(4)]

                woq = {}
                wo_wl = {}

                def emit_wo_load(fc):
                    if fc >= FC:
                        return
                    j = fc % 4
                    dw = SM[f"d_w{j}"]

                    @blk.sync
                    def _(y, fc=fc, j=j, dw=dw, wq=woq.get(fc - 4)):
                        if wq:
                            y.wait_ge(wq[0].h, wq[1])
                        dw.inc(y.dma_start(
                            out=wo2[j][:],
                            in_=w_out[fc * 128:(fc + 1) * 128, :]), 16)
                    wo_wl[fc] = dw.n

                for fc in range(min(4, FC)):
                    emit_wo_load(fc)

                rsc_all = {}
                qo_all = {}

                def emit_wo_rmax(fc):
                    j = fc % 4
                    dw = SM[f"d_w{j}"]

                    @blk.vector
                    def _(v, fc=fc, j=j, dw=dw, wl_c=wo_wl[fc],
                          wq=woq.get(fc - 4)):
                        v.wait_ge(dw.h, wl_c)
                        if wq:
                            v.wait_ge(wq[0].h, wq[1])
                        vs.inc(v.reduce_max(rmax2[j][:], wo2[j][:], axis=AX.X,
                                            apply_absolute_value=True))
                        v.wait_ge(vs.h, vs.n)
                        vs.inc(v.tensor_scalar_max(rmax2[j][:], rmax2[j][:],
                                                   1e-8))
                        v.wait_ge(vs.h, vs.n)
                        vs.inc(v.reciprocal(rs2[j][:], rmax2[j][:]))
                        vs.inc(v.tensor_scalar(ri2[j][:], rmax2[j][:],
                                               1.0 / 127.0, None,
                                               op0=ALU.mult))
                        v.wait_ge(vs.h, vs.n)
                        vs.inc(v.tensor_scalar(rs2[j][:], rs2[j][:], 127.0,
                                               None, op0=ALU.mult))
                    rsc_all[fc] = vs.n

                def emit_wo_op1(fc):
                    j = fc % 4

                    @blk.scalar
                    def _(s, fc=fc, j=j, rsc_c=rsc_all[fc],
                          pq=woq.get(fc - 4)):
                        s.wait_ge(vs.h, rsc_c)
                        if pq:
                            s.wait_ge(pq[0].h, pq[1])
                        ss.inc(s.activation(qi_o[j][:], wo2[j][:], AF.Copy,
                                            bias=0.0, scale=rs2[j][:]))
                    qo_all[fc] = ss.n

                def emit_wo_op2(fc):
                    j = fc % 4

                    @blk.vector
                    def _(v, fc=fc, j=j, qo_c=qo_all[fc]):
                        v.wait_ge(ss.h, qo_c)
                        vs.inc(v.tensor_scalar(Wo_sb[:, fc, :], qi_o[j][:],
                                               ri2[j][:], None,
                                               op0=ALU.mult))
                    woq[fc] = (vs, vs.n)

                for fc in range(FC):
                    emit_wo_rmax(fc)
                    emit_wo_op1(fc)
                    if fc >= 1:
                        emit_wo_op2(fc - 1)
                        emit_wo_load(fc + 3)
                emit_wo_op2(FC - 1)

                scales2_c = emit_ar_part2(s2_bc, inv2_bc, 32767.0, ar2_cc,
                                          ps_o[1])
                woq_final = woq[FC - 1][1]

            with contextlib.ExitStack() as pb:
                Eb = pb.enter_context
                nqh = 8 if FC >= 8 else (4 if FC >= 4 else
                                         (2 if FC >= 2 else 1))
                qhalf = (FC // nqh) * 512
                qi_h = [Eb(nc.sbuf_tensor(f"qi_h{j}", [128, qhalf], I32))
                        for j in range(2)]
                ot2 = [Eb(nc.sbuf_tensor(f"ot2_{j}", [128, D], F32))
                       for j in range(2)]
                lnf2 = [Eb(nc.sbuf_tensor(f"lnf2_{j}", [128, D], F32))
                        for j in range(2)]
                x2r2 = [Eb(nc.sbuf_tensor(f"x2r2_{j}", [128, D], F32))
                        for j in range(2)]
                outt2 = [Eb(nc.sbuf_tensor(f"outt2_{j}", [128, D], F32))
                         for j in range(2)]
                sqf2 = [Eb(nc.sbuf_tensor(f"sqf2_{j}", [128, D], F32))
                        for j in range(2)]
                fb = make_stats_bufs(Eb, "fb")

                fep = {}
                qih_vs = [0, 0]
                qih_k = [0]

                def emit_quant(tt):
                    j = tt % 2
                    dhr = SM[f"d_hr{j}"]
                    flat = ht2[j][:].rearrange("p c t -> p (c t)")
                    for hh in range(nqh):
                        jq = qih_k[0] % 2

                        @blk.scalar
                        def _(s, tt=tt, hh=hh, jq=jq, dhr=dhr, flat=flat,
                              qv=qih_vs[jq], s2c=scales2_c, hl=hload_c[tt]):
                            s.wait_ge(dhr.h, hl)
                            if qv:
                                s.wait_ge(vs.h, qv)
                            if tt == 0 and hh == 0:
                                s.wait_ge(vs.h, s2c)
                            ss.inc(s.activation(
                                qi_h[jq][:],
                                flat[:, hh * qhalf:(hh + 1) * qhalf],
                                AF.Copy, bias=0.0, scale=s2_bc[:]))
                        op1_c = ss.n

                        @blk.vector
                        def _(v, hh=hh, jq=jq, op1_c=op1_c, flat=flat):
                            v.wait_ge(ss.h, op1_c)
                            vs.inc(v.tensor_scalar(
                                flat[:, hh * qhalf:(hh + 1) * qhalf],
                                qi_h[jq][:],
                                inv2_bc[:], None, op0=ALU.mult))
                        qih_vs[jq] = vs.n
                        qih_k[0] += 1
                    hq_done[tt] = vs.n
                    hq_ss[tt] = ss.n

                emit_quant(0)

                ms_b0 = ms.n
                for tt in range(NTT):
                    pe_ht_done[tt] = ms_b0 + (tt + 1) * TCH

                for tt in range(NTT):
                    j = tt % 2
                    if tt + 1 < NTT:
                        emit_quant(tt + 1)
                    if tt + 2 < NTT:
                        emit_hload(tt + 2)

                    for qc in range(TCH):
                        ob = qc % 2
                        gq2 = tt * TCH + qc
                        j3 = gq2 % 2
                        dxr = SM[f"d_xr{j3}"]

                        @blk.tensor
                        def _(t, tt=tt, j=j, qc=qc, ob=ob, gq2=gq2,
                              hq=hq_done[tt], pb_=fep.get((gq2 - 2, "bias"), 0)):
                            t.wait_ge(vs.h, hq)
                            if gq2 == 0:
                                t.wait_ge(vs.h, woq_final)
                            if pb_:
                                t.wait_ge(vs.h, pb_)
                            for fc in range(FC):
                                lhs = ht2[j][:, fc, qc * 128:(qc + 1) * 128]
                                for dvt in range(NDV):
                                    mm = t.matmul(
                                        ps_o[ob][:, dvt * 512:dvt * 512 + DVW],
                                        lhs,
                                        Wo_sb[:, fc, dvt * 512:dvt * 512 + DVW],
                                        start=(fc == 0), stop=(fc == FC - 1))
                            ms.inc(mm)
                        mm2_c = ms.n
                        assert ms.n == ms_b0 + tt * TCH + qc + 1

                        @blk.gpsimd
                        def _(g, tt=tt, qc=qc, j3=j3, gq2=gq2, dxr=dxr):
                            if gq2 >= 2:
                                g.wait_ge(vs.h, fep[(gq2 - 2, "resid")])
                            else:
                                g.wait_ge(vs.h, scales2_c)
                                g.wait_ge(ss.h, qo_all[FC - 1])
                            r0 = tt * 512 + qc * 128
                            dxr.inc(g.dma_start(out=x2r2[j3][:],
                                                in_=x2h[r0:r0 + 128, :]), 16)
                        xr_c = dxr.n

                        @blk.vector
                        def _(v, gq2=gq2, ob=ob, j3=j3, bv=bio_v, mm2_c=mm2_c):
                            v.wait_ge(ms.h, mm2_c)
                            v.wait_ge(vs.h, bv)
                            if gq2 >= 2:
                                v.wait_ge(ss.h, fep[(gq2 - 2, "lnf")])
                                v.wait_ge(vs.h, fep[(gq2 - 2, "red")])
                            vs.inc(v.tensor_tensor(ot2[j3][:], ps_o[ob][:, 0:D],
                                                   bout_bc[:, 0:D], op=ALU.add))
                        fep[(gq2, "bias")] = vs.n
                        bias_c = vs.n

                        rstd_c, nb_c = emit_ln_stats(
                            ot2[j3][:], sqf2[j3], fb, j3, D,
                            wait_v=[(vs.h, bias_c)], wait_s=[(vs.h, bias_c)])
                        fep[(gq2, "red")] = nb_c

                        @blk.scalar
                        def _(s, j3=j3, nb_c=nb_c, gq2=gq2):
                            s.wait_ge(vs.h, nb_c)
                            if gq2 >= 2:
                                s.wait_ge(vs.h, fep[(gq2 - 2, "resid")])
                            ss.inc(s.activation(lnf2[j3][:], ot2[j3][:],
                                                AF.Identity,
                                                scale=fb["rstd"][j3][:],
                                                bias=fb["nb"][j3][:]))
                        fep[(gq2, "lnf")] = ss.n
                        lnf_c = ss.n

                        @blk.vector
                        def _(v, j3=j3, lnf_c=lnf_c, xr_c=xr_c, dxr=dxr,
                              gq2=gq2):
                            v.wait_ge(ss.h, lnf_c)
                            v.wait_ge(dxr.h, xr_c)
                            if gq2 >= 2:
                                v.wait_ge(SM[f"d_out{j3}"].h, 16 * (gq2 // 2))
                            vs.inc(v.tensor_tensor(outt2[j3][:], lnf2[j3][:],
                                                   x2r2[j3][:], op=ALU.add))
                        fep[(gq2, "resid")] = vs.n
                        res_c = vs.n

                        @blk.gpsimd
                        def _(g, tt=tt, qc=qc, j3=j3, res_c=res_c):
                            g.wait_ge(vs.h, res_c)
                            r0 = tt * 512 + qc * 128
                            SM[f"d_out{j3}"].inc(g.dma_start(
                                out=out[r0:r0 + 128, :], in_=outt2[j3][:]), 16)

                barrier(extra=[(SM["d_out0"], SM["d_out0"].n),
                               (SM["d_out1"], SM["d_out1"].n)])

    return nc


_NC_CACHE = {}


def _get_nc(key, cfg):
    if key not in _NC_CACHE:
        _NC_CACHE[key] = build_nc(cfg)
    return _NC_CACHE[key]


def kernel(L0_attn_x, L0_ffn_W_in, L0_ffn_b_in, L0_ffn_W_out, L0_ffn_b_out):
    cfg = FULL_CFG
    x = np.ascontiguousarray(np.asarray(L0_attn_x))
    w_in_a = np.ascontiguousarray(np.asarray(L0_ffn_W_in))
    b_in_a = np.ascontiguousarray(np.asarray(L0_ffn_b_in)).reshape(1, -1)
    w_out_a = np.ascontiguousarray(np.asarray(L0_ffn_W_out))
    b_out_a = np.ascontiguousarray(np.asarray(L0_ffn_b_out)).reshape(1, -1)

    nc = _get_nc("full", cfg)
    T = cfg["T"]
    in_maps = []
    for c in range(cfg["NCORES"]):
        b, h = divmod(c, 2)
        xb_c = np.ascontiguousarray(np.roll(x[b], -h * T, axis=0))
        in_maps.append({"xb": xb_c, "w_in": w_in_a, "b_in": b_in_a,
                        "w_out": w_out_a, "b_out": b_out_a})
    res = run_bass_kernel_spmd(nc, in_maps, core_ids=list(range(cfg["NCORES"])))
    outs = [np.asarray(r["out"]) for r in res.results]
    full = np.concatenate(outs, axis=0).reshape(4, 2 * T, x.shape[2])
    return full.astype(np.float32, copy=False)


# version 42
# speedup vs baseline: 1.0364x; 1.0138x over previous
import contextlib
import numpy as np
import concourse.bass as bass
import concourse.mybir as mybir
from concourse.bass_utils import run_bass_kernel_spmd

F32 = mybir.dt.float32
BF16 = mybir.dt.bfloat16
I32 = mybir.dt.int32
AF = mybir.ActivationFunctionType
ALU = mybir.AluOpType
AX = mybir.AxisListType

FULL_CFG = dict(NCORES=8, T=4096, S=8192, D=1024, F=4096, SWQ=2048, KBG=512)


class Sem:
    def __init__(self, st, nc, name):
        self.h = st.enter_context(nc.semaphore(name))
        self.n = 0

    def inc(self, inst, k=1):
        inst.then_inc(self.h, k)
        self.n += k
        return self.n


def build_nc(cfg):
    NCORES = cfg["NCORES"]
    T, S, D, F = cfg["T"], cfg["S"], cfg["D"], cfg["F"]
    SWQ, KBG = cfg["SWQ"], cfg["KBG"]
    DC = D // 128
    FC = F // 128
    NTIL = S // 128
    NSW = T // SWQ
    NQT = SWQ // 128
    NQG = SWQ // 512
    NKBG = S // KBG
    KC = KBG // 128
    NDV = (D + 511) // 512
    DVW = min(D, 512)
    NTT = T // 512
    TCH = 4
    assert SWQ % 512 == 0 and T % 512 == 0 and D % 128 == 0 and F % 128 == 0

    nc = bass.Bass("TRN2", target_bir_lowering=False)

    xb = nc.dram_tensor("xb", [S, D], F32, kind="ExternalInput").ap()
    w_in = nc.dram_tensor("w_in", [D, F], F32, kind="ExternalInput").ap()
    b_in = nc.dram_tensor("b_in", [1, F], F32, kind="ExternalInput").ap()
    w_out = nc.dram_tensor("w_out", [F, D], F32, kind="ExternalInput").ap()
    b_out = nc.dram_tensor("b_out", [1, D], F32, kind="ExternalInput").ap()
    out = nc.dram_tensor("out", [T, D], F32, kind="ExternalOutput").ap()

    vtm = nc.dram_tensor("vtm", [S, D], BF16).ap()
    kfm = nc.dram_tensor("kfm", [D, S], BF16).ap()
    x2h = nc.dram_tensor("x2h", [T, D], F32).ap()
    t_h = nc.dram_tensor("t_h", [T, D], BF16).ap()
    h_h = nc.dram_tensor("h_h", [F, T], BF16).ap()
    scr = nc.dram_tensor("scr", [1, F], F32).ap()
    cc1i = nc.dram_tensor("cc1i", [1, 128], F32).ap()
    cc1o = nc.dram_tensor("cc1o", [1, 128], F32).ap()
    cc2i = nc.dram_tensor("cc2i", [1, 128], F32).ap()
    cc2o = nc.dram_tensor("cc2o", [1, 128], F32).ap()

    vtm3 = vtm.rearrange("(c p) d -> p c d", p=128)
    kfm3 = kfm.rearrange("(c p) s -> p c s", p=128)
    hh3 = h_h.rearrange("(c p) t -> p c t", p=128)
    binv = b_in.rearrange("o (c p) -> (o c) p", p=128)
    scrv = scr.rearrange("o (c p) -> (o c) p", p=128)

    st = contextlib.ExitStack()
    E = st.enter_context
    with st:
        c_eps = E(nc.sbuf_tensor("c_eps", [128, 1], F32))
        c_nshift = E(nc.sbuf_tensor("c_nshift", [128, 1], F32))
        c_zero = E(nc.sbuf_tensor("c_zero", [128, 1], F32))
        ones_row = E(nc.sbuf_tensor("ones_row", [1, 128], F32))
        ones_col = E(nc.sbuf_tensor("ones_col", [128, 1], BF16))
        iota_j = E(nc.sbuf_tensor("iota_j", [128, 128], F32))
        iota_p = E(nc.sbuf_tensor("iota_p", [128, 128], F32))
        ident = E(nc.sbuf_tensor("ident", [128, 128], F32))
        ident_bf = E(nc.sbuf_tensor("ident_bf", [128, 128], BF16))
        amax1a = E(nc.sbuf_tensor("amax1a", [128, 1], F32))
        amax2a = E(nc.sbuf_tensor("amax2a", [128, 1], F32))
        mrow = E(nc.sbuf_tensor("mrow", [1, 128], F32))
        mtop = E(nc.sbuf_tensor("mtop", [1, 1], F32))
        screp = E(nc.sbuf_tensor("screp", [1, 128], F32))
        gmax = E(nc.sbuf_tensor("gmax", [1, 1], F32))
        s1_bc = E(nc.sbuf_tensor("s1_bc", [128, 1], F32))
        inv1_bc = E(nc.sbuf_tensor("inv1_bc", [128, 1], F32))
        s2_bc = E(nc.sbuf_tensor("s2_bc", [128, 1], F32))
        inv2_bc = E(nc.sbuf_tensor("inv2_bc", [128, 1], F32))
        ab_bc = E(nc.sbuf_tensor("ab_bc", [128, 1], F32))
        colmax_pc = E(nc.sbuf_tensor("colmax_pc", [128, FC], F32))
        ginvw_pc = E(nc.sbuf_tensor("ginvw_pc", [128, FC], F32))
        binpc = E(nc.sbuf_tensor("binpc", [128, FC], F32))
        bout_bc = E(nc.sbuf_tensor("bout_bc", [128, D], F32))
        l_sb = E(nc.sbuf_tensor("l_sb", [128, max(NQT, 1)], F32))
        linv = E(nc.sbuf_tensor("linv", [128, max(NQT, 1)], F32))

        ps_st = []
        ps_o = []
        HP = []

        sem_names = ["ms", "ss", "vs", "gs", "bar", "ccs",
                     "d_vtm0", "d_vtm1", "d_vtm2", "d_vtm3",
                     "d_kfm0", "d_kfm1", "d_kfm2", "d_kfm3",
                     "d_xin0", "d_xin1", "d_xin2", "d_xin3",
                     "d_xin4", "d_xin5", "d_xin6", "d_xin7",
                     "d_q", "d_q2", "d_k0", "d_k1", "d_v0", "d_v1",
                     "d_xq0", "d_xq1", "d_xq2", "d_xq3", "d_xq4", "d_xq5",
                     "d_x2w0", "d_tw0", "d_tw1", "d_tw2", "d_tw3",
                     "d_w0", "d_w1", "d_w2", "d_w3", "d_scr", "d_bio", "d_tfm0", "d_tfm1",
                     "d_hw0", "d_hw1", "d_hr0", "d_hr1", "d_xr0", "d_xr1", "d_cc",
                     "d_out0", "d_out1"]
        SM = {name: Sem(st, nc, name) for name in sem_names}
        ms, ss, vs, gs, bar = SM["ms"], SM["ss"], SM["vs"], SM["gs"], SM["bar"]

        blk = E(nc.Block())
        nbar = [0]

        def barrier(extra=()):
            @blk.sync
            def _(y):
                for sm, val in extra:
                    y.wait_ge(sm.h, val)
                bar.inc(y.nop())
                nbar[0] += 1
                y.wait_ge(bar.h, 5 * nbar[0])

            for engname, own in [("vector", vs), ("scalar", ss),
                                 ("tensor", ms), ("gpsimd", gs)]:
                @getattr(blk, engname)
                def _(e, own=own):
                    e.wait_ge(own.h, own.n)
                    bar.inc(e.nop())
                    e.wait_ge(bar.h, 5 * nbar[0])

        NCH = (D + 511) // 512

        def make_stats_bufs(Ep, prefix, n=2):
            d = {k: [Ep(nc.sbuf_tensor(f"{prefix}_{k}{j}", [128, 1], F32))
                     for j in range(n)]
                 for k in ["lnv", "rstd", "nb"]}
            d["s12"] = [Ep(nc.sbuf_tensor(f"{prefix}_s12{j}", [128, 6 * NCH],
                                          F32)) for j in range(n)]
            d["mv"] = [Ep(nc.sbuf_tensor(f"{prefix}_mv{j}", [128, 2], F32))
                       for j in range(n)]
            return d

        def emit_ln_stats(src_ap, sq_scr, bufs, j, W, wait_v=(), wait_s=(),
                          sq_on=None, prev_use=None):
            nch = (W + 511) // 512
            s12 = bufs["s12"][j]
            mv = bufs["mv"][j]

            @blk.vector
            def _(v):
                for smh, val in wait_v:
                    v.wait_ge(smh, val)
                if prev_use is not None:
                    v.wait_ge(prev_use[0].h, prev_use[1])
                for ch in range(nch):
                    w0 = ch * 512
                    w1 = min(W, w0 + 512)
                    vs.inc(v.bn_stats(s12[:, 6 * ch:6 * ch + 6],
                                      src_ap[:, w0:w1]))
                v.wait_ge(vs.h, vs.n)
                vs.inc(v.bn_aggr(mv[:], s12[:, 0:6 * nch]))
            mv_c = vs.n

            @blk.scalar
            def _(s):
                s.wait_ge(vs.h, mv_c)
                ss.inc(s.activation(bufs["lnv"][j][:], mv[:, 1:2], AF.Ln,
                                    scale=1.0, bias=c_eps[:]))
                s.wait_ge(ss.h, ss.n)
                ss.inc(s.activation(bufs["rstd"][j][:], bufs["lnv"][j][:],
                                    AF.Exp, scale=-0.5, bias=c_zero[:]))
            rstd_c = ss.n

            @blk.vector
            def _(v):
                v.wait_ge(ss.h, rstd_c)
                vs.inc(v.tensor_scalar(bufs["nb"][j][:], mv[:, 0:1],
                                       bufs["rstd"][j][:], -1.0,
                                       op0=ALU.mult, op1=ALU.mult))
            nb_c = vs.n
            return rstd_c, nb_c

        def emit_ar_part1(acc_tile, cci, cco, last_vs_for_acc, pbank):
            @blk.tensor
            def _(t):
                t.wait_ge(vs.h, last_vs_for_acc)
                ms.inc(t.transpose(pbank[0:1, 0:128], acc_tile[:], ident[:]))
            tr_c = ms.n

            @blk.vector
            def _(v):
                v.wait_ge(ms.h, tr_c)
                vs.inc(v.tensor_copy(mrow[:], pbank[0:1, 0:128]))
                v.wait_ge(vs.h, vs.n)
                vs.inc(v.reduce_max(mtop[:], mrow[:], axis=AX.X))
                v.wait_ge(vs.h, vs.n)
                vs.inc(v.tensor_scalar(screp[:], mrow[:], 0.0, mtop[0:1, 0:1],
                                       op0=ALU.mult, op1=ALU.add))
            rep_c = vs.n

            @blk.gpsimd
            def _(g):
                g.wait_ge(vs.h, rep_c)
                SM["d_cc"].inc(g.dma_start(out=cci, in_=screp[:]), 16)
                g.wait_ge(SM["d_cc"].h, SM["d_cc"].n)
                SM["ccs"].inc(g.collective_compute(
                    "AllReduce", ALU.max, replica_groups=[list(range(NCORES))],
                    ins=[cci], outs=[cco]))
                g.wait_ge(SM["ccs"].h, SM["ccs"].n)
                SM["d_cc"].inc(g.dma_start(out=gmax[:], in_=cco[0:1, 0:1]), 16)
            return (SM["d_cc"].n, rep_c)

        def emit_ar_part2(s_out, inv_out, qconst, cc_in, pbank):
            cc_c, vsafe = cc_in

            @blk.tensor
            def _(t):
                t.wait_ge(SM["d_cc"].h, cc_c)
                t.wait_ge(vs.h, vsafe)
                ms.inc(t.matmul(pbank[:, 0:1], ones_row[0:1, :], gmax[0:1, :],
                                start=True, stop=True))
            bc_c = ms.n

            @blk.vector
            def _(v):
                v.wait_ge(ms.h, bc_c)
                vs.inc(v.tensor_copy(ab_bc[:], pbank[:, 0:1]))
                v.wait_ge(vs.h, vs.n)
                vs.inc(v.reciprocal(s_out[:], ab_bc[:]))
                v.wait_ge(vs.h, vs.n)
                vs.inc(v.tensor_scalar(s_out[:], s_out[:], float(qconst), None,
                                       op0=ALU.mult))
                vs.inc(v.tensor_scalar(inv_out[:], ab_bc[:], 1.0 / float(qconst),
                                       None, op0=ALU.mult))
            return vs.n

        from concourse.hw_specs import get_activation_tables
        _tab_names = list(get_activation_tables(nc.m.arch).keys())
        _SET_MAIN = _tab_names.index("natural_log_exp_and_others")
        _SET_GELU = _tab_names.index("gelu_and_others")

        def load_act_set(s, set_id):
            s.add_instruction(mybir.InstLoadActFuncSet(
                name=nc.get_next_instruction_name(),
                act_func_set_id=set_id, ins=[], outs=[]))

        @blk.scalar
        def _(s):
            load_act_set(s, _SET_MAIN)

        @blk.gpsimd
        def _(g):
            g.memset(c_eps[:], 1e-5)
            g.memset(c_nshift[:], -float(D))
            g.memset(c_zero[:], 0.0)
            g.memset(ones_row[:], 1.0)
            g.memset(ones_col[:], 1.0)
            g.memset(amax1a[:], 1e-8)
            g.memset(amax2a[:], 1e-8)
            g.iota(iota_j[:], [[1, 128]], channel_multiplier=0,
                   allow_small_or_imprecise_dtypes=True)
            gs.inc(g.iota(iota_p[:], [[0, 128]], channel_multiplier=1,
                          allow_small_or_imprecise_dtypes=True))
        init_g = gs.n

        @blk.vector
        def _(v):
            v.wait_ge(gs.h, init_g)
            vs.inc(v.tensor_tensor(ident[:], iota_j[:], iota_p[:], op=ALU.is_equal))
            v.wait_ge(vs.h, vs.n)
            vs.inc(v.tensor_copy(ident_bf[:], ident[:]))

        barrier()

        with contextlib.ExitStack() as p0:
            Ep = p0.enter_context
            xin2 = [Ep(nc.sbuf_tensor(f"xin2_{j}", [128, D], F32))
                    for j in range(8)]
            vt2 = [Ep(nc.sbuf_tensor(f"vt2_{j}", [128, D], BF16))
                   for j in range(4)]
            kst2 = [Ep(nc.sbuf_tensor(f"kst2_{j}", [128, D], BF16))
                    for j in range(4)]
            p0b = make_stats_bufs(Ep, "p0", n=16)

            ps_tp = [Ep(nc.psum_tensor(f"ps_tp{j}", [128, D], BF16))
                     for j in range(2)]
            norm_done = {}
            rstd_done = {}
            nb_done = {}
            mv_done = {}
            ld_done = {}
            tpose_done = {}
            kcopy_done = {}

            G0 = 8
            for g0 in range(0, NTIL, G0):
                grp = range(g0, min(NTIL, g0 + G0))

                for i in grp:
                    dxin = SM[f"d_xin{i % 8}"]

                    @blk.sync
                    def _(y, i=i, dxin=dxin, nd=norm_done.get(i - 8)):
                        if nd:
                            y.wait_ge(nd[0].h, nd[1])
                        dxin.inc(y.dma_start(
                            out=xin2[i % 8][:],
                            in_=xb[i * 128:(i + 1) * 128, :]), 16)
                    ld_done[i] = (dxin, dxin.n)

                for i in grp:
                    e = i % 16

                    @blk.vector
                    def _(v, i=i, e=e, ld=ld_done[i],
                          pr=rstd_done.get(i - 16)):
                        v.wait_ge(ld[0].h, ld[1])
                        if pr:
                            v.wait_ge(ss.h, pr)
                        for ch in range(NCH):
                            vs.inc(v.bn_stats(
                                p0b["s12"][e][:, 6 * ch:6 * ch + 6],
                                xin2[i % 8][:, ch * 512:min(D, (ch + 1) * 512)]))
                        v.wait_ge(vs.h, vs.n)
                        vs.inc(v.bn_aggr(p0b["mv"][e][:],
                                         p0b["s12"][e][:, 0:6 * NCH]))
                    mv_done[i] = vs.n

                for i in grp:
                    e = i % 16

                    @blk.scalar
                    def _(s, i=i, e=e):
                        s.wait_ge(vs.h, mv_done[i])
                        ss.inc(s.activation(p0b["lnv"][e][:],
                                            p0b["mv"][e][:, 1:2], AF.Ln,
                                            scale=1.0, bias=c_eps[:]))
                        s.wait_ge(ss.h, ss.n)
                        ss.inc(s.activation(p0b["rstd"][e][:],
                                            p0b["lnv"][e][:], AF.Exp,
                                            scale=-0.5, bias=c_zero[:]))
                    rstd_done[i] = ss.n

                for i in grp:
                    e = i % 16

                    @blk.vector
                    def _(v, i=i, e=e):
                        v.wait_ge(ss.h, rstd_done[i])
                        vs.inc(v.tensor_scalar(p0b["nb"][e][:],
                                               p0b["mv"][e][:, 0:1],
                                               p0b["rstd"][e][:], -1.0,
                                               op0=ALU.mult, op1=ALU.mult))
                    nb_done[i] = vs.n

                for i in grp:
                    e = i % 16
                    dvtm = SM[f"d_vtm{i % 4}"]
                    dkf = SM[f"d_kfm{i % 4}"]

                    if i % 2 == 0:
                        @blk.scalar
                        def _(s, i=i, e=e, dvtm=dvtm, dv_c=dvtm.n,
                              td=tpose_done.get(i - 4)):
                            s.wait_ge(vs.h, nb_done[i])
                            if td:
                                s.wait_ge(dvtm.h, dv_c)
                                s.wait_ge(ms.h, td)
                            ss.inc(s.activation(vt2[i % 4][:], xin2[i % 8][:],
                                                AF.Identity,
                                                scale=p0b["rstd"][e][:],
                                                bias=p0b["nb"][e][:]))
                        norm_done[i] = (ss, ss.n)
                    else:
                        @blk.vector
                        def _(v, i=i, e=e, dvtm=dvtm, dv_c=dvtm.n,
                              td=tpose_done.get(i - 4)):
                            v.wait_ge(vs.h, nb_done[i])
                            if td:
                                v.wait_ge(dvtm.h, dv_c)
                                v.wait_ge(ms.h, td)
                            vs.inc(v.tensor_scalar(vt2[i % 4][:],
                                                   xin2[i % 8][:],
                                                   p0b["rstd"][e][:],
                                                   p0b["nb"][e][:],
                                                   op0=ALU.mult, op1=ALU.add))
                        norm_done[i] = (vs, vs.n)

                    @blk.gpsimd
                    def _(g, i=i, dvtm=dvtm, nd=norm_done[i]):
                        g.wait_ge(nd[0].h, nd[1])
                        dvtm.inc(g.dma_start(out=vtm3[:, i, :],
                                             in_=vt2[i % 4][:]), 16)

                    @blk.tensor
                    def _(t, i=i, nd=norm_done[i], kd=kcopy_done.get(i - 2)):
                        t.wait_ge(nd[0].h, nd[1])
                        if kd:
                            t.wait_ge(kd[0].h, kd[1])
                        for dc in range(DC):
                            mm = t.transpose(
                                ps_tp[i % 2][:, dc * 128:(dc + 1) * 128],
                                vt2[i % 4][:, dc * 128:(dc + 1) * 128],
                                ident_bf[:])
                        ms.inc(mm)
                    tpose_done[i] = ms.n

                    if i % 2 == 1:
                        @blk.scalar
                        def _(s, i=i, dkf=dkf, dk_c=dkf.n):
                            s.wait_ge(ms.h, tpose_done[i])
                            if i >= 4:
                                s.wait_ge(dkf.h, dk_c)
                            ss.inc(s.activation(kst2[i % 4][:],
                                                ps_tp[i % 2][:], AF.Copy,
                                                bias=0.0, scale=1.0))
                        kcopy_done[i] = (ss, ss.n)
                    else:
                        @blk.vector
                        def _(v, i=i, dkf=dkf, dk_c=dkf.n):
                            v.wait_ge(ms.h, tpose_done[i])
                            if i >= 4:
                                v.wait_ge(dkf.h, dk_c)
                            vs.inc(v.tensor_copy(kst2[i % 4][:],
                                                 ps_tp[i % 2][:]))
                        kcopy_done[i] = (vs, vs.n)

                    @blk.gpsimd
                    def _(g, i=i, dkf=dkf, kd=kcopy_done[i]):
                        g.wait_ge(kd[0].h, kd[1])
                        dkf.inc(g.dma_start(
                            out=kfm3[:, :, i * 128:(i + 1) * 128],
                            in_=kst2[i % 4][:].rearrange("p (c k) -> p c k",
                                                         c=DC)), 16)

            barrier(extra=[(SM[f"d_kfm{j}"], SM[f"d_kfm{j}"].n)
                           for j in range(4)] +
                          [(SM[f"d_vtm{j}"], SM[f"d_vtm{j}"].n)
                           for j in range(4)])

        ps_st.extend(E(nc.psum_tensor(f"ps_st{i}", [128, 512], F32))
                     for i in range(4))
        ps_o.extend(E(nc.psum_tensor(f"ps_o{i}", [128, 1024], F32))
                    for i in range(2))
        HP.extend([ps_st[0][:], ps_st[1][:], ps_st[2][:], ps_st[3][:],
                   ps_o[0][:, 0:512], ps_o[0][:, 512:1024],
                   ps_o[1][:, 0:512], ps_o[1][:, 512:1024]])

        with contextlib.ExitStack() as p2:
            Ep = p2.enter_context
            Qt = Ep(nc.sbuf_tensor("Qt", [128, DC, SWQ], BF16))
            K2 = [Ep(nc.sbuf_tensor(f"K2_{j}", [128, DC, KBG], BF16)) for j in range(2)]
            V2 = [Ep(nc.sbuf_tensor(f"V2_{j}", [128, KC, D], BF16)) for j in range(2)]
            PT = Ep(nc.sbuf_tensor("PT", [128, KC, NQG, 512], BF16))
            O_sb = [Ep(nc.sbuf_tensor(f"O_sb{q}", [128, D], F32)) for q in range(NQT)]
            xr = [Ep(nc.sbuf_tensor(f"xr_{j}", [128, D], F32)) for j in range(6)]
            ln12 = [Ep(nc.sbuf_tensor(f"ln12_{j}", [128, D], F32)) for j in range(4)]
            tt2 = [Ep(nc.sbuf_tensor(f"tt2_{j}", [128, D], BF16)) for j in range(4)]
            tmax2 = [Ep(nc.sbuf_tensor(f"tmax2_{j}", [128, 1], F32)) for j in range(16)]
            linv2 = Ep(nc.sbuf_tensor("linv2", [128, max(NQT, 1)], F32))
            eb1 = make_stats_bufs(Ep, "eb1", n=16)
            eb2 = make_stats_bufs(Ep, "eb2", n=16)

            npp = (NQG + 1) // 2
            st_seq = []
            for pp in range(npp):
                qgs = [g for g in (2 * pp, 2 * pp + 1) if g < NQG]
                for kc in range(KC):
                    st_seq.append((pp, kc, qgs,
                                   [2 * gi + kc % 2 for gi in range(len(qgs))]))
            exps_per_kbg = sum(len(e[2]) for e in st_seq)

            bank_exp = {0: 0, 1: 0, 2: 0, 3: 0}
            bank_oev = {0: 0, 1: 0}
            l_evac_c = 0
            kv_used = {}
            att_prev = {}
            att_prev_dma = 0
            ep_counts = {}
            xq_map = {}
            amax_upd = 0

            for sw in range(NSW):
                q0 = sw * SWQ

                @blk.sync
                def _(y, sw=sw, q0=q0):
                    if sw > 0:
                        y.wait_ge(ms.h, kv_used[(sw - 1, NKBG - 1)])
                    if NQG >= 2:
                        hw = SWQ // 2
                        SM["d_q"].inc(y.dma_start(
                            out=Qt[:, :, 0:hw],
                            in_=kfm3[:, :, q0:q0 + hw]), 16)
                        SM["d_q2"].inc(y.dma_start(
                            out=Qt[:, :, hw:SWQ],
                            in_=kfm3[:, :, q0 + hw:q0 + SWQ]), 16)
                    else:
                        SM["d_q"].inc(y.dma_start(
                            out=Qt[:],
                            in_=kfm3[:, :, q0:q0 + SWQ]), 16)
                q_c = SM["d_q"].n
                q2_c = SM["d_q2"].n

                oev = {}
                for kbg in range(NKBG):
                    j = kbg % 2
                    k0 = kbg * KBG
                    dk, dv = SM[f"d_k{j}"], SM[f"d_v{j}"]

                    @blk.sync
                    def _(y, sw=sw, kbg=kbg, j=j, k0=k0, dk=dk, dv=dv):
                        key = (sw, kbg - 2) if kbg >= 2 else \
                              (sw - 1, NKBG - 2 + kbg)
                        if key in kv_used:
                            y.wait_ge(ms.h, kv_used[key])
                        dk.inc(y.dma_start(out=K2[j][:],
                                           in_=kfm3[:, :, k0:k0 + KBG]), 16)
                        dv.inc(y.dma_start(
                            out=V2[j][:],
                            in_=vtm[k0:k0 + KBG, :].rearrange(
                                "(c p) d -> p c d", p=128)), 16)
                    k_c, v_c = dk.n, dv.n

                    ss_base = ss.n
                    ms_base = ms.n
                    exp_cnt = {}
                    st_done = {}
                    run = 0
                    for idx, (pp, kc, qgs, banks) in enumerate(st_seq):
                        st_done[(pp, kc)] = ms_base + idx + 1
                        for gi in range(len(qgs)):
                            run += 1
                            exp_cnt[(pp, kc, gi)] = ss_base + run
                    pv_base = ms_base + len(st_seq)
                    vs_pv_base = vs.n

                    bank_last = dict(bank_exp)

                    @blk.tensor
                    def _(t, j=j, k_c=k_c, q_c=q_c, q2_c=q2_c,
                          bank_last=bank_last, lec=l_evac_c):
                        t.wait_ge(SM["d_q"].h, q_c)
                        t.wait_ge(SM[f"d_k{j}"].h, k_c)
                        if lec:
                            t.wait_ge(vs.h, lec)
                        q2_waited = NQG < 2
                        for (pp, kc, qgs, banks) in st_seq:
                            if not q2_waited and pp >= 1:
                                t.wait_ge(SM["d_q2"].h, q2_c)
                                q2_waited = True
                            for bk in banks:
                                if bank_last[bk]:
                                    t.wait_ge(ss.h, bank_last[bk])
                            for dc in range(DC):
                                lhs = K2[j][:, dc, kc * 128:(kc + 1) * 128]
                                for gi, qg in enumerate(qgs):
                                    mm = t.matmul(
                                        ps_st[banks[gi]][:], lhs,
                                        Qt[:, dc, qg * 512:(qg + 1) * 512],
                                        start=(dc == 0), stop=(dc == DC - 1))
                            ms.inc(mm)
                            for gi, bk in enumerate(banks):
                                bank_last[bk] = exp_cnt[(pp, kc, gi)]
                    assert ms.n == pv_base

                    @blk.scalar
                    def _(s):
                        for (pp, kc, qgs, banks) in st_seq:
                            s.wait_ge(ms.h, st_done[(pp, kc)])
                            for gi, qg in enumerate(qgs):
                                ss.inc(s.activation(
                                    PT[:, kc, qg, :], ps_st[banks[gi]][:],
                                    AF.Exp, bias=c_nshift[:], scale=1.0))
                    assert ss.n == ss_base + exps_per_kbg
                    exp_all = ss.n
                    for (pp, kc, qgs, banks) in st_seq:
                        for gi, bk in enumerate(banks):
                            bank_exp[bk] = exp_cnt[(pp, kc, gi)]

                    @blk.tensor
                    def _(t, sw=sw, kbg=kbg, j=j, v_c=v_c, exp_all=exp_all,
                          vs_pv_base=vs_pv_base, boev=dict(bank_oev)):
                        t.wait_ge(ss.h, exp_all)
                        t.wait_ge(SM[f"d_v{j}"].h, v_c)
                        for qt in range(NQT):
                            qg, qc = divmod(qt, 4)
                            ob = qt % 2
                            if qt >= 2:
                                t.wait_ge(vs.h, vs_pv_base + (qt - 2) + 1)
                            elif boev[ob]:
                                t.wait_ge(vs.h, boev[ob])
                            for kc in range(KC):
                                lhs = PT[:, kc, qg, qc * 128:(qc + 1) * 128]
                                for dvt in range(NDV):
                                    t.matmul(
                                        ps_o[ob][:, dvt * 512:dvt * 512 + DVW],
                                        lhs,
                                        V2[j][:, kc, dvt * 512:dvt * 512 + DVW],
                                        start=(kc == 0), stop=(kc == KC - 1))
                                mm = t.matmul(ps_st[0][:, qt:qt + 1], lhs,
                                              ones_col[:], start=(kc == 0),
                                              stop=(kc == KC - 1))
                            ms.inc(mm)
                    assert ms.n == pv_base + NQT
                    kv_used[(sw, kbg)] = ms.n

                    @blk.vector
                    def _(v, sw=sw, kbg=kbg, pv_base=pv_base, lec=l_evac_c):
                        for qt in range(NQT):
                            ob = qt % 2
                            v.wait_ge(ms.h, pv_base + qt + 1)
                            if kbg > 0:
                                v.wait_ge(vs.h, oev[(kbg - 1, qt)])
                            elif sw > 0:
                                apm, apc = att_prev[qt]
                                v.wait_ge(apm.h, apc)
                                v.wait_ge(SM["d_x2w0"].h, att_prev_dma)
                            if kbg == 0:
                                vs.inc(v.tensor_copy(O_sb[qt][:],
                                                     ps_o[ob][:, 0:D]))
                            else:
                                vs.inc(v.tensor_tensor(O_sb[qt][:], O_sb[qt][:],
                                                       ps_o[ob][:, 0:D],
                                                       op=ALU.add))
                            oev[(kbg, qt)] = vs.n
                            bank_oev[ob] = vs.n
                        if lec:
                            v.wait_ge(vs.h, lec)
                        if kbg == 0:
                            vs.inc(v.tensor_copy(l_sb[:, 0:NQT],
                                                 ps_st[0][:, 0:NQT]))
                        else:
                            vs.inc(v.tensor_tensor(l_sb[:, 0:NQT], l_sb[:, 0:NQT],
                                                   ps_st[0][:, 0:NQT],
                                                   op=ALU.add))
                    l_evac_c = vs.n
                    assert vs.n == vs_pv_base + NQT + 1

                @blk.vector
                def _(v, lec=l_evac_c):
                    v.wait_ge(vs.h, lec)
                    vs.inc(v.reciprocal(linv[:, 0:NQT], l_sb[:, 0:NQT]))
                    v.wait_ge(vs.h, vs.n)
                    vs.inc(v.tensor_tensor(linv2[:, 0:NQT], linv[:, 0:NQT],
                                           linv[:, 0:NQT], op=ALU.mult))
                linv2_c = vs.n

                gqs = [sw * NQT + qt for qt in range(NQT)]
                mv1_c = {}; rstd_c = {}; bq_c = {}; ln1_c = {}
                x2_c = {}; nb2_c = {}; tn_c = {}

                def emit_xq(qt_l):
                    gql = gqs[qt_l]
                    dxql = SM[f"d_xq{gql % 6}"]

                    @blk.gpsimd
                    def _(g, qt_l=qt_l, gql=gql, dxql=dxql):
                        if (gql - 6, "x2add") in ep_counts:
                            sm2, c2 = ep_counts[(gql - 6, "x2add")]
                            g.wait_ge(sm2.h, c2)
                        dxql.inc(g.dma_start(
                            out=xr[gql % 6][:],
                            in_=xb[q0 + qt_l * 128:
                                   q0 + (qt_l + 1) * 128, :]), 16)
                    xq_map[gql] = (dxql, dxql.n)

                for qt in range(min(6, NQT)):
                    emit_xq(qt)

                for qt in range(NQT):
                    gq = gqs[qt]
                    e = gq % 16

                    @blk.vector
                    def _(v, qt=qt, gq=gq, e=e):
                        if (gq - 16, "rstd1") in ep_counts:
                            sm2, c2 = ep_counts[(gq - 16, "rstd1")]
                            v.wait_ge(sm2.h, c2)
                        for ch in range(NCH):
                            vs.inc(v.bn_stats(
                                eb1["s12"][e][:, 6 * ch:6 * ch + 6],
                                O_sb[qt][:, ch * 512:min(D, (ch + 1) * 512)]))
                        v.wait_ge(vs.h, vs.n)
                        vs.inc(v.bn_aggr(eb1["mv"][e][:],
                                         eb1["s12"][e][:, 0:6 * NCH]))
                    mv1_c[gq] = vs.n

                for qt in range(NQT):
                    gq = gqs[qt]
                    e = gq % 16

                    @blk.scalar
                    def _(s, qt=qt, gq=gq, e=e, l2c=linv2_c):
                        s.wait_ge(vs.h, max(mv1_c[gq], l2c))
                        ss.inc(s.activation(eb1["lnv"][e][:],
                                            eb1["mv"][e][:, 1:2],
                                            AF.Ln, scale=linv2[:, qt:qt + 1],
                                            bias=c_eps[:]))
                        s.wait_ge(ss.h, ss.n)
                        ss.inc(s.activation(eb1["rstd"][e][:],
                                            eb1["lnv"][e][:], AF.Exp,
                                            scale=-0.5, bias=c_zero[:]))
                    rstd_c[gq] = ss.n
                    ep_counts[(gq, "rstd1")] = (ss, ss.n)

                for qt in range(NQT):
                    gq = gqs[qt]
                    e = gq % 16

                    @blk.vector
                    def _(v, qt=qt, gq=gq, e=e):
                        v.wait_ge(ss.h, rstd_c[gq])
                        vs.inc(v.tensor_scalar(eb1["rstd"][e][:],
                                               eb1["rstd"][e][:],
                                               linv[:, qt:qt + 1], 1.0,
                                               op0=ALU.mult, op1=ALU.mult))
                        v.wait_ge(vs.h, vs.n)
                        vs.inc(v.tensor_scalar(eb1["nb"][e][:],
                                               eb1["mv"][e][:, 0:1],
                                               eb1["rstd"][e][:], -1.0,
                                               op0=ALU.mult, op1=ALU.mult))
                    bq_c[gq] = vs.n

                for qt in range(NQT):
                    gq = gqs[qt]
                    e = gq % 16

                    @blk.scalar
                    def _(s, qt=qt, gq=gq, e=e):
                        s.wait_ge(vs.h, bq_c[gq])
                        if (gq - 4, "x2add") in ep_counts:
                            sm2, c2 = ep_counts[(gq - 4, "x2add")]
                            s.wait_ge(sm2.h, c2)
                        ss.inc(s.activation(ln12[gq % 4][:], O_sb[qt][:],
                                            AF.Identity,
                                            scale=eb1["rstd"][e][:],
                                            bias=eb1["nb"][e][:]))
                    ln1_c[gq] = ss.n

                    dxq, xq_cc = xq_map[gq]

                    @blk.vector
                    def _(v, qt=qt, gq=gq, dxq=dxq, xq_cc=xq_cc):
                        v.wait_ge(dxq.h, xq_cc)
                        v.wait_ge(ss.h, ln1_c[gq])
                        vs.inc(v.tensor_tensor(O_sb[qt][:], ln12[gq % 4][:],
                                               xr[gq % 6][:], op=ALU.add))
                    ep_counts[(gq, "x2add")] = (vs, vs.n)
                    x2_c[gq] = vs.n

                    if qt + 6 < NQT:
                        emit_xq(qt + 6)

                for qt in range(NQT):
                    gq = gqs[qt]

                    @blk.gpsimd
                    def _(g, qt=qt, gq=gq):
                        g.wait_ge(vs.h, x2_c[gq])
                        SM["d_x2w0"].inc(g.dma_start(
                            out=x2h[q0 + qt * 128:q0 + (qt + 1) * 128, :],
                            in_=O_sb[qt][:]), 16)

                for qt in range(NQT):
                    gq = gqs[qt]
                    e = gq % 16

                    @blk.vector
                    def _(v, qt=qt, gq=gq, e=e):
                        v.wait_ge(vs.h, x2_c[gq])
                        if (gq - 16, "rstd2") in ep_counts:
                            sm2, c2 = ep_counts[(gq - 16, "rstd2")]
                            v.wait_ge(sm2.h, c2)
                        for ch in range(NCH):
                            vs.inc(v.bn_stats(
                                eb2["s12"][e][:, 6 * ch:6 * ch + 6],
                                O_sb[qt][:, ch * 512:min(D, (ch + 1) * 512)]))
                        v.wait_ge(vs.h, vs.n)
                        vs.inc(v.bn_aggr(eb2["mv"][e][:],
                                         eb2["s12"][e][:, 0:6 * NCH]))
                    mv1_c[(gq, 2)] = vs.n

                for qt in range(NQT):
                    gq = gqs[qt]
                    e = gq % 16

                    @blk.scalar
                    def _(s, qt=qt, gq=gq, e=e):
                        s.wait_ge(vs.h, mv1_c[(gq, 2)])
                        ss.inc(s.activation(eb2["lnv"][e][:],
                                            eb2["mv"][e][:, 1:2],
                                            AF.Ln, scale=1.0, bias=c_eps[:]))
                        s.wait_ge(ss.h, ss.n)
                        ss.inc(s.activation(eb2["rstd"][e][:],
                                            eb2["lnv"][e][:], AF.Exp,
                                            scale=-0.5, bias=c_zero[:]))
                    rstd_c[(gq, 2)] = ss.n
                    ep_counts[(gq, "rstd2")] = (ss, ss.n)

                for qt in range(NQT):
                    gq = gqs[qt]
                    e = gq % 16

                    @blk.vector
                    def _(v, qt=qt, gq=gq, e=e):
                        v.wait_ge(ss.h, rstd_c[(gq, 2)])
                        vs.inc(v.tensor_scalar(eb2["nb"][e][:],
                                               eb2["mv"][e][:, 0:1],
                                               eb2["rstd"][e][:], -1.0,
                                               op0=ALU.mult, op1=ALU.mult))
                    nb2_c[gq] = vs.n

                for qt in range(NQT):
                    gq = gqs[qt]
                    e = gq % 16
                    dtw = SM[f"d_tw{gq % 4}"]

                    @blk.scalar
                    def _(s, qt=qt, gq=gq, e=e, dtw=dtw, dtw_c=dtw.n):
                        s.wait_ge(vs.h, nb2_c[gq])
                        if (gq - 4, "amax") in ep_counts:
                            s.wait_ge(dtw.h, dtw_c)
                            sm2, c2 = ep_counts[(gq - 4, "amax")]
                            s.wait_ge(sm2.h, c2)
                        ss.inc(s.activation(tt2[gq % 4][:], O_sb[qt][:],
                                            AF.Identity,
                                            scale=eb2["rstd"][e][:],
                                            bias=eb2["nb"][e][:]))
                    tn_c[gq] = ss.n
                    ep_counts[(gq, "tnorm")] = (ss, ss.n)

                    @blk.vector
                    def _(v, qt=qt, gq=gq, au=amax_upd):
                        v.wait_ge(ss.h, tn_c[gq])
                        if au:
                            v.wait_ge(vs.h, au)
                        vs.inc(v.reduce_max(tmax2[gq % 16][:], tt2[gq % 4][:],
                                            axis=AX.X,
                                            apply_absolute_value=True))
                        v.wait_ge(vs.h, vs.n)
                        vs.inc(v.tensor_tensor(amax1a[:], amax1a[:],
                                               tmax2[gq % 16][:], op=ALU.max))
                    amax_upd = vs.n
                    ep_counts[(gq, "amax")] = (vs, amax_upd)

                    @blk.gpsimd
                    def _(g, qt=qt, gq=gq):
                        g.wait_ge(ss.h, tn_c[gq])
                        SM[f"d_tw{gq % 4}"].inc(g.dma_start(
                            out=t_h[q0 + qt * 128:q0 + (qt + 1) * 128, :],
                            in_=tt2[gq % 4][:]), 16)

                att_prev = {qt: (ss, tn_c[gqs[qt]]) for qt in range(NQT)}
                att_prev_dma = SM["d_x2w0"].n

            barrier(extra=[(SM["d_x2w0"], SM["d_x2w0"].n)] +
                          [(SM[f"d_tw{j}"], SM[f"d_tw{j}"].n)
                           for j in range(4)])

        with contextlib.ExitStack() as pf:
            Ef = pf.enter_context
            W_sb = Ef(nc.sbuf_tensor("W_sb", [128, DC * F], BF16))
            Wi_sb = W_sb[:].rearrange("p (c f) -> p c f", c=DC)
            Wo_sb = W_sb[:].rearrange("p (c d) -> p c d", c=FC)

            tfm_st = contextlib.ExitStack()
            tfm2 = [tfm_st.enter_context(nc.sbuf_tensor(f"tfm2_{j}", [128, T],
                                                        BF16))
                    for j in range(2)]
            ar1_cc = emit_ar_part1(amax1a, cc1i, cc1o, amax_upd, ps_o[1])

            with contextlib.ExitStack() as ar1:
                Ea = ar1.enter_context
                wt2 = [Ea(nc.sbuf_tensor(f"wt2_{j}", [128, F], F32))
                       for j in range(2)]

                @blk.sync
                def _(y):
                    for dc0 in range(min(2, DC)):
                        SM[f"d_tfm{dc0 % 2}"].inc(y.dma_start_transpose(
                            tfm2[dc0 % 2][:],
                            t_h[:, dc0 * 128:(dc0 + 1) * 128]), 16)
                macc = Ea(nc.sbuf_tensor("macc", [128, F], F32))
                prodw = Ea(nc.sbuf_tensor("prodw", [128, F], F32))
                qi_w = Ea(nc.sbuf_tensor("qi_w", [128, F], I32))
                s_pc = Ea(nc.sbuf_tensor("s_pc", [128, FC], F32))
                s32 = Ea(nc.sbuf_tensor("s32", [FC, 128], F32))
                srow = Ea(nc.sbuf_tensor("srow", [1, F], F32))
                s_bcast = Ea(nc.sbuf_tensor("s_bcast", [128, F], F32))
                binfc = Ea(nc.sbuf_tensor("binfc", [FC, 128], F32))
                boutrow = Ea(nc.sbuf_tensor("boutrow", [1, D], F32))

                @blk.vector
                def _(v):
                    vs.inc(v.memset(macc[:], 1e-8))
                macc_c = vs.n

                w_rd = {}
                for dc in range(DC):
                    j = dc % 2
                    dw = SM[f"d_w{j}"]

                    @blk.sync
                    def _(y, dc=dc, j=j, dw=dw):
                        if dc >= 2:
                            y.wait_ge(ss.h, w_rd[dc - 2])
                        dw.inc(y.dma_start(
                            out=wt2[j][:],
                            in_=w_in[dc * 128:(dc + 1) * 128, :]), 16)
                    wl_c = dw.n

                    @blk.scalar
                    def _(s, j=j, dw=dw, wl_c=wl_c, mc=macc_c):
                        s.wait_ge(dw.h, wl_c)
                        if isinstance(mc, tuple):
                            s.wait_ge(mc[0].h, mc[1])
                        else:
                            s.wait_ge(vs.h, mc)
                        ss.inc(s.activation(prodw[:], wt2[j][:], AF.Abs,
                                            bias=c_zero[:], scale=1.0))
                    abs_c = ss.n
                    w_rd[dc] = abs_c

                    @blk.vector
                    def _(v, abs_c=abs_c, mc=macc_c):
                        v.wait_ge(ss.h, abs_c)
                        if isinstance(mc, tuple):
                            v.wait_ge(mc[0].h, mc[1])
                        else:
                            v.wait_ge(vs.h, mc)
                        vs.inc(v.tensor_tensor(macc[:], prodw[:], macc[:],
                                               op=ALU.max))
                    macc_c = (vs, vs.n)

                cm_c = macc_c
                for c in range(FC):
                    pb = 2 + (c % 2)

                    @blk.tensor
                    def _(t, c=c, pb=pb, cm_c=cm_c):
                        if isinstance(cm_c, tuple):
                            t.wait_ge(cm_c[0].h, cm_c[1])
                        else:
                            t.wait_ge(vs.h, cm_c)
                        ms.inc(t.transpose(ps_st[pb][0:128, 0:128],
                                           macc[:, c * 128:(c + 1) * 128],
                                           ident[:]))
                    tr_c = ms.n

                    @blk.vector
                    def _(v, c=c, pb=pb, tr_c=tr_c):
                        v.wait_ge(ms.h, tr_c)
                        vs.inc(v.reduce_max(colmax_pc[:, c:c + 1],
                                            ps_st[pb][0:128, 0:128], axis=AX.X))
                    cm_c = vs.n

                @blk.vector
                def _(v, cm_c=cm_c):
                    v.wait_ge(vs.h, cm_c)
                    vs.inc(v.tensor_scalar(ginvw_pc[:], colmax_pc[:],
                                           1.0 / 127.0, None, op0=ALU.mult))
                    vs.inc(v.reciprocal(s_pc[:], colmax_pc[:]))
                    v.wait_ge(vs.h, vs.n)
                    vs.inc(v.tensor_scalar(s_pc[:], s_pc[:], 127.0, None,
                                           op0=ALU.mult))
                spc_c = vs.n

                @blk.tensor
                def _(t, spc_c=spc_c):
                    t.wait_ge(vs.h, spc_c)
                    ms.inc(t.transpose(ps_st[2][0:FC, 0:128], s_pc[:], ident[:]))
                trs_c = ms.n

                @blk.vector
                def _(v, trs_c=trs_c):
                    v.wait_ge(ms.h, trs_c)
                    vs.inc(v.tensor_copy(s32[:], ps_st[2][0:FC, 0:128]))
                s32_c = vs.n

                @blk.sync
                def _(y, s32_c=s32_c):
                    y.wait_ge(vs.h, s32_c)
                    SM["d_scr"].inc(y.dma_start(out=scrv, in_=s32[:]), 16)
                    y.wait_ge(SM["d_scr"].h, SM["d_scr"].n)
                    SM["d_scr"].inc(y.dma_start(out=srow[:], in_=scr), 16)
                    SM["d_bio"].inc(y.dma_start(out=binfc[:], in_=binv), 16)
                    SM["d_bio"].inc(y.dma_start(out=boutrow[:], in_=b_out), 16)
                scr_c = SM["d_scr"].n
                bio_c = SM["d_bio"].n

                sb_c = 0
                for jj in range((F + 511) // 512):
                    w = min(512, F - jj * 512)
                    pb = 2 + (jj % 2)

                    @blk.tensor
                    def _(t, jj=jj, w=w, pb=pb, scr_c=scr_c, sb_c=sb_c):
                        t.wait_ge(SM["d_scr"].h, scr_c)
                        if sb_c:
                            t.wait_ge(vs.h, sb_c)
                        ms.inc(t.matmul(ps_st[pb][:, 0:w], ones_row[0:1, :],
                                        srow[0:1, jj * 512:jj * 512 + w],
                                        start=True, stop=True))
                    mmc = ms.n

                    @blk.vector
                    def _(v, jj=jj, w=w, pb=pb, mmc=mmc):
                        v.wait_ge(ms.h, mmc)
                        vs.inc(v.tensor_copy(s_bcast[:, jj * 512:jj * 512 + w],
                                             ps_st[pb][:, 0:w]))
                    sb_c = vs.n

                @blk.tensor
                def _(t, bio_c=bio_c, sb_c=sb_c):
                    t.wait_ge(SM["d_bio"].h, bio_c)
                    t.wait_ge(vs.h, sb_c)
                    ms.inc(t.transpose(ps_st[0][0:128, 0:FC], binfc[0:FC, :],
                                       ident[0:FC, 0:FC]))
                    for jj in range((D + 511) // 512):
                        w = min(512, D - jj * 512)
                        tgt = ps_st[1][:, 0:w] if jj == 0 else ps_o[0][:, 0:w]
                        ms.inc(t.matmul(tgt, ones_row[0:1, :],
                                        boutrow[0:1, jj * 512:jj * 512 + w],
                                        start=True, stop=True))
                bio_mm = ms.n

                @blk.vector
                def _(v, bio_mm=bio_mm):
                    v.wait_ge(ms.h, bio_mm)
                    vs.inc(v.tensor_copy(binpc[:, 0:FC], ps_st[0][0:128, 0:FC]))
                    vs.inc(v.tensor_copy(bout_bc[:, 0:min(512, D)],
                                         ps_st[1][:, 0:min(512, D)]))
                    if D > 512:
                        vs.inc(v.tensor_copy(bout_bc[:, 512:D],
                                             ps_o[0][:, 0:D - 512]))
                bio_v = vs.n

                wq_rd = {}
                for dc in range(DC):
                    j = dc % 2
                    dw = SM[f"d_w{j}"]

                    @blk.sync
                    def _(y, dc=dc, j=j, dw=dw):
                        if dc - 2 in wq_rd:
                            y.wait_ge(vs.h, wq_rd[dc - 2])
                        else:
                            y.wait_ge(ss.h, w_rd.get(DC - 2 + dc, w_rd[DC - 1]))
                        dw.inc(y.dma_start(
                            out=wt2[j][:],
                            in_=w_in[dc * 128:(dc + 1) * 128, :]), 16)
                    wl_c = dw.n

                    @blk.gpsimd
                    def _(g, j=j, dw=dw, wl_c=wl_c, sb_c=sb_c,
                          pw=wq_rd.get(dc - 1, 0)):
                        g.wait_ge(dw.h, wl_c)
                        g.wait_ge(vs.h, max(sb_c, pw))
                        gs.inc(g.tensor_tensor(prodw[:], wt2[j][:], s_bcast[:],
                                               op=ALU.mult))
                    gw_c = gs.n

                    @blk.vector
                    def _(v, dc=dc, gw_c=gw_c, pw=wq_rd.get(dc - 2, 0)):
                        v.wait_ge(gs.h, gw_c)
                        if pw:
                            v.wait_ge(vs.h, pw)
                        vs.inc(v.tensor_copy(qi_w[:], prodw[:]))
                        v.wait_ge(vs.h, vs.n)
                        vs.inc(v.tensor_copy(Wi_sb[:, dc, :], qi_w[:]))
                    wq_rd[dc] = vs.n

                scales1_c = emit_ar_part2(s1_bc, inv1_bc, 32767.0, ar1_cc,
                                          ps_o[1])
                barrier(extra=[(SM["d_scr"], SM["d_scr"].n)])

            with contextlib.ExitStack() as pa:
                Ea = pa.enter_context
                tq = Ea(nc.sbuf_tensor("tq", [128, DC, T], BF16))
                qi_t = [Ea(nc.sbuf_tensor(f"qi_t{j}", [128, T], I32))
                        for j in range(2)]
                hst2 = [Ea(nc.sbuf_tensor(f"hst2_{j}", [128, T], BF16))
                        for j in range(2)]
                hred2 = [Ea(nc.sbuf_tensor(f"hred2_{j}", [128, 1], F32))
                         for j in range(2)]

                @blk.scalar
                def _(s):
                    load_act_set(s, _SET_GELU)

                tq_cs = {}
                qit_ss = 0
                for dc in range(DC):
                    j = dc % 2
                    dt = SM[f"d_tfm{j}"]
                    tl_c = dt.n

                    @blk.scalar
                    def _(s, dc=dc, j=j, dt=dt, tl_c=tl_c,
                          tq_c=tq_cs.get(dc - 2, 0), s1c=scales1_c):
                        s.wait_ge(dt.h, tl_c)
                        if dc == 0:
                            s.wait_ge(vs.h, s1c)
                        if tq_c:
                            s.wait_ge(vs.h, tq_c)
                        ss.inc(s.activation(qi_t[j][:], tfm2[j][:], AF.Copy,
                                            bias=0.0, scale=s1_bc[:]))
                    qit_ss = ss.n

                    @blk.vector
                    def _(v, dc=dc, j=j, qit_ss=qit_ss):
                        v.wait_ge(ss.h, qit_ss)
                        vs.inc(v.tensor_scalar(tq[:, dc, :], qi_t[j][:],
                                               inv1_bc[:], None,
                                               op0=ALU.mult))
                    tq_cs[dc] = vs.n
                    tq_c = vs.n

                    if dc + 2 < DC:
                        @blk.sync
                        def _(y, dc=dc, j=j, dt=dt, qit_ss=qit_ss):
                            y.wait_ge(ss.h, qit_ss)
                            dt.inc(y.dma_start_transpose(
                                tfm2[j][:],
                                t_h[:, (dc + 2) * 128:(dc + 3) * 128]), 16)
                tq_all = tq_c

                gelu_done = {}
                hred_done = {}
                for fc in range(FC):
                    jf = fc % 2
                    dh = SM[f"d_hw{jf}"]
                    ms_base = ms.n
                    mm1_done = {tt: ms_base + tt + 1 for tt in range(NTT)}

                    @blk.tensor
                    def _(t, fc=fc, tq_all=tq_all, gd=dict(gelu_done)):
                        if fc == 0:
                            t.wait_ge(vs.h, tq_all)
                        for tt in range(NTT - 1):
                            if gd.get((fc - 1, tt)):
                                t.wait_ge(ss.h, gd[(fc - 1, tt)])
                        for dc in range(DC):
                            for tt in range(NTT - 1):
                                mm = t.matmul(
                                    HP[tt],
                                    Wi_sb[:, dc, fc * 128:(fc + 1) * 128],
                                    tq[:, dc, tt * 512:(tt + 1) * 512],
                                    start=(dc == 0), stop=(dc == DC - 1))
                                if dc == DC - 1:
                                    ms.inc(mm)
                        if gd.get((fc - 1, NTT - 1)):
                            t.wait_ge(ss.h, gd[(fc - 1, NTT - 1)])
                        for dc in range(DC):
                            mm = t.matmul(
                                HP[NTT - 1],
                                Wi_sb[:, dc, fc * 128:(fc + 1) * 128],
                                tq[:, dc, (NTT - 1) * 512:NTT * 512],
                                start=(dc == 0), stop=(dc == DC - 1))
                        ms.inc(mm)
                    assert ms.n == ms_base + NTT

                    @blk.scalar
                    def _(s, fc=fc, jf=jf, dh=dh, mm1_done=mm1_done):
                        for tt in range(NTT):
                            s.wait_ge(ms.h, mm1_done[tt])
                            if fc >= 2 and tt == 0:
                                s.wait_ge(dh.h, 16 * (fc // 2))
                                s.wait_ge(vs.h, hred_done[fc - 2])
                            ss.inc(s.activation(
                                hst2[jf][:, tt * 512:(tt + 1) * 512], HP[tt],
                                AF.Gelu, scale=ginvw_pc[:, fc:fc + 1],
                                bias=binpc[:, fc:fc + 1]))
                            gelu_done[(fc, tt)] = ss.n
                    g_all = ss.n

                    @blk.gpsimd
                    def _(g, fc=fc, jf=jf, dh=dh, g_all=g_all):
                        g.wait_ge(ss.h, g_all)
                        dh.inc(g.dma_start(out=hh3[:, fc, :], in_=hst2[jf][:]), 16)

                    @blk.vector
                    def _(v, fc=fc, jf=jf, g_all=g_all,
                          prev=(hred_done.get(fc - 2, 0))):
                        v.wait_ge(ss.h, g_all)
                        if prev:
                            v.wait_ge(vs.h, prev)
                        vs.inc(v.reduce_max(hred2[jf][:], hst2[jf][:], axis=AX.X,
                                            apply_absolute_value=True))
                        v.wait_ge(vs.h, vs.n)
                        vs.inc(v.tensor_tensor(amax2a[:], amax2a[:], hred2[jf][:],
                                               op=ALU.max))
                    hred_done[fc] = vs.n
                am2_c = vs.n

                barrier(extra=[(SM["d_hw0"], SM["d_hw0"].n),
                               (SM["d_hw1"], SM["d_hw1"].n)])

            tfm_st.close()

            @blk.scalar
            def _(s):
                load_act_set(s, _SET_MAIN)

            ht2 = [Ef(nc.sbuf_tensor(f"ht2_{j}", [128, FC, 512], BF16))
                   for j in range(2)]
            hq_done = {}
            hq_ss = {}
            hload_c = {}
            pe_ht_done = {}

            def emit_hload(tt):
                j = tt % 2
                dhr = SM[f"d_hr{j}"]

                @blk.sync
                def _(y, tt=tt, j=j, dhr=dhr):
                    if tt >= 2:
                        y.wait_ge(ms.h, pe_ht_done[tt - 2])
                        y.wait_ge(vs.h, hq_done[tt - 2])
                        y.wait_ge(ss.h, hq_ss[tt - 2])
                    dhr.inc(y.dma_start(
                        out=ht2[j][:],
                        in_=hh3[:, :, tt * 512:(tt + 1) * 512]), 16)
                hload_c[tt] = dhr.n

            emit_hload(0)
            if NTT > 1:
                emit_hload(1)

            ar2_cc = emit_ar_part1(amax2a, cc2i, cc2o, am2_c, ps_o[1])

            with contextlib.ExitStack() as ar2:
                Ea = ar2.enter_context
                wo2 = [Ea(nc.sbuf_tensor(f"wo2_{j}", [128, D], F32))
                       for j in range(4)]
                qi_o = [Ea(nc.sbuf_tensor(f"qi_o{j}", [128, D], I32))
                        for j in range(4)]
                rmax2 = [Ea(nc.sbuf_tensor(f"rmax2_{j}", [128, 1], F32))
                         for j in range(4)]
                rs2 = [Ea(nc.sbuf_tensor(f"rs2_{j}", [128, 1], F32))
                       for j in range(4)]
                ri2 = [Ea(nc.sbuf_tensor(f"ri2_{j}", [128, 1], F32))
                       for j in range(4)]

                woq = {}
                wo_wl = {}

                def emit_wo_load(fc):
                    if fc >= FC:
                        return
                    j = fc % 4
                    dw = SM[f"d_w{j}"]

                    @blk.sync
                    def _(y, fc=fc, j=j, dw=dw, wq=woq.get(fc - 4)):
                        if wq:
                            y.wait_ge(wq[0].h, wq[1])
                        dw.inc(y.dma_start(
                            out=wo2[j][:],
                            in_=w_out[fc * 128:(fc + 1) * 128, :]), 16)
                    wo_wl[fc] = dw.n

                for fc in range(min(4, FC)):
                    emit_wo_load(fc)

                rsc_all = {}
                qo_all = {}

                def emit_wo_rmax(fc):
                    j = fc % 4
                    dw = SM[f"d_w{j}"]

                    @blk.vector
                    def _(v, fc=fc, j=j, dw=dw, wl_c=wo_wl[fc],
                          wq=woq.get(fc - 4)):
                        v.wait_ge(dw.h, wl_c)
                        if wq:
                            v.wait_ge(wq[0].h, wq[1])
                        vs.inc(v.reduce_max(rmax2[j][:], wo2[j][:], axis=AX.X,
                                            apply_absolute_value=True))
                        v.wait_ge(vs.h, vs.n)
                        vs.inc(v.tensor_scalar_max(rmax2[j][:], rmax2[j][:],
                                                   1e-8))
                        v.wait_ge(vs.h, vs.n)
                        vs.inc(v.reciprocal(rs2[j][:], rmax2[j][:]))
                        vs.inc(v.tensor_scalar(ri2[j][:], rmax2[j][:],
                                               1.0 / 127.0, None,
                                               op0=ALU.mult))
                        v.wait_ge(vs.h, vs.n)
                        vs.inc(v.tensor_scalar(rs2[j][:], rs2[j][:], 127.0,
                                               None, op0=ALU.mult))
                    rsc_all[fc] = vs.n

                def emit_wo_op1(fc):
                    j = fc % 4

                    @blk.scalar
                    def _(s, fc=fc, j=j, rsc_c=rsc_all[fc],
                          pq=woq.get(fc - 4)):
                        s.wait_ge(vs.h, rsc_c)
                        if pq:
                            s.wait_ge(pq[0].h, pq[1])
                        ss.inc(s.activation(qi_o[j][:], wo2[j][:], AF.Copy,
                                            bias=0.0, scale=rs2[j][:]))
                    qo_all[fc] = ss.n

                def emit_wo_op2(fc):
                    j = fc % 4

                    @blk.vector
                    def _(v, fc=fc, j=j, qo_c=qo_all[fc]):
                        v.wait_ge(ss.h, qo_c)
                        vs.inc(v.tensor_scalar(Wo_sb[:, fc, :], qi_o[j][:],
                                               ri2[j][:], None,
                                               op0=ALU.mult))
                    woq[fc] = (vs, vs.n)

                for fc in range(FC):
                    emit_wo_rmax(fc)
                    emit_wo_op1(fc)
                    if fc >= 1:
                        emit_wo_op2(fc - 1)
                        emit_wo_load(fc + 3)
                emit_wo_op2(FC - 1)

                scales2_c = emit_ar_part2(s2_bc, inv2_bc, 32767.0, ar2_cc,
                                          ps_o[1])
                woq_final = woq[FC - 1][1]

            with contextlib.ExitStack() as pb:
                Eb = pb.enter_context
                nqh = 8 if FC >= 8 else (4 if FC >= 4 else
                                         (2 if FC >= 2 else 1))
                qhalf = (FC // nqh) * 512
                qi_h = [Eb(nc.sbuf_tensor(f"qi_h{j}", [128, qhalf], I32))
                        for j in range(2)]
                ot2 = [Eb(nc.sbuf_tensor(f"ot2_{j}", [128, D], F32))
                       for j in range(2)]
                lnf2 = [Eb(nc.sbuf_tensor(f"lnf2_{j}", [128, D], F32))
                        for j in range(2)]
                x2r2 = [Eb(nc.sbuf_tensor(f"x2r2_{j}", [128, D], F32))
                        for j in range(2)]
                outt2 = [Eb(nc.sbuf_tensor(f"outt2_{j}", [128, D], F32))
                         for j in range(2)]
                sqf2 = [Eb(nc.sbuf_tensor(f"sqf2_{j}", [128, D], F32))
                        for j in range(2)]
                fb = make_stats_bufs(Eb, "fb")

                fep = {}
                qih_vs = [0, 0]
                qih_k = [0]

                def emit_quant(tt):
                    j = tt % 2
                    dhr = SM[f"d_hr{j}"]
                    flat = ht2[j][:].rearrange("p c t -> p (c t)")
                    for hh in range(nqh):
                        jq = qih_k[0] % 2

                        @blk.scalar
                        def _(s, tt=tt, hh=hh, jq=jq, dhr=dhr, flat=flat,
                              qv=qih_vs[jq], s2c=scales2_c, hl=hload_c[tt]):
                            s.wait_ge(dhr.h, hl)
                            if qv:
                                s.wait_ge(vs.h, qv)
                            if tt == 0 and hh == 0:
                                s.wait_ge(vs.h, s2c)
                            ss.inc(s.activation(
                                qi_h[jq][:],
                                flat[:, hh * qhalf:(hh + 1) * qhalf],
                                AF.Copy, bias=0.0, scale=s2_bc[:]))
                        op1_c = ss.n

                        @blk.vector
                        def _(v, hh=hh, jq=jq, op1_c=op1_c, flat=flat):
                            v.wait_ge(ss.h, op1_c)
                            vs.inc(v.tensor_scalar(
                                flat[:, hh * qhalf:(hh + 1) * qhalf],
                                qi_h[jq][:],
                                inv2_bc[:], None, op0=ALU.mult))
                        qih_vs[jq] = vs.n
                        qih_k[0] += 1
                    hq_done[tt] = vs.n
                    hq_ss[tt] = ss.n

                emit_quant(0)

                ms_b0 = ms.n
                for tt in range(NTT):
                    pe_ht_done[tt] = ms_b0 + (tt + 1) * TCH

                for tt in range(NTT):
                    j = tt % 2
                    if tt + 1 < NTT:
                        emit_quant(tt + 1)
                    if tt + 2 < NTT:
                        emit_hload(tt + 2)

                    for qc in range(TCH):
                        ob = qc % 2
                        gq2 = tt * TCH + qc
                        j3 = gq2 % 2
                        dxr = SM[f"d_xr{j3}"]

                        @blk.tensor
                        def _(t, tt=tt, j=j, qc=qc, ob=ob, gq2=gq2,
                              hq=hq_done[tt], pb_=fep.get((gq2 - 2, "bias"), 0)):
                            t.wait_ge(vs.h, hq)
                            if gq2 == 0:
                                t.wait_ge(vs.h, woq_final)
                            if pb_:
                                t.wait_ge(vs.h, pb_)
                            for fc in range(FC):
                                lhs = ht2[j][:, fc, qc * 128:(qc + 1) * 128]
                                for dvt in range(NDV):
                                    mm = t.matmul(
                                        ps_o[ob][:, dvt * 512:dvt * 512 + DVW],
                                        lhs,
                                        Wo_sb[:, fc, dvt * 512:dvt * 512 + DVW],
                                        start=(fc == 0), stop=(fc == FC - 1))
                            ms.inc(mm)
                        mm2_c = ms.n
                        assert ms.n == ms_b0 + tt * TCH + qc + 1

                        @blk.gpsimd
                        def _(g, tt=tt, qc=qc, j3=j3, gq2=gq2, dxr=dxr):
                            if gq2 >= 2:
                                g.wait_ge(vs.h, fep[(gq2 - 2, "resid")])
                            else:
                                g.wait_ge(vs.h, scales2_c)
                                g.wait_ge(ss.h, qo_all[FC - 1])
                            r0 = tt * 512 + qc * 128
                            dxr.inc(g.dma_start(out=x2r2[j3][:],
                                                in_=x2h[r0:r0 + 128, :]), 16)
                        xr_c = dxr.n

                        @blk.vector
                        def _(v, gq2=gq2, ob=ob, j3=j3, bv=bio_v, mm2_c=mm2_c):
                            v.wait_ge(ms.h, mm2_c)
                            v.wait_ge(vs.h, bv)
                            if gq2 >= 2:
                                v.wait_ge(ss.h, fep[(gq2 - 2, "lnf")])
                                v.wait_ge(vs.h, fep[(gq2 - 2, "red")])
                            vs.inc(v.tensor_tensor(ot2[j3][:], ps_o[ob][:, 0:D],
                                                   bout_bc[:, 0:D], op=ALU.add))
                        fep[(gq2, "bias")] = vs.n
                        bias_c = vs.n

                        rstd_c, nb_c = emit_ln_stats(
                            ot2[j3][:], sqf2[j3], fb, j3, D,
                            wait_v=[(vs.h, bias_c)], wait_s=[(vs.h, bias_c)])
                        fep[(gq2, "red")] = nb_c

                        @blk.scalar
                        def _(s, j3=j3, nb_c=nb_c, gq2=gq2):
                            s.wait_ge(vs.h, nb_c)
                            if gq2 >= 2:
                                s.wait_ge(vs.h, fep[(gq2 - 2, "resid")])
                            ss.inc(s.activation(lnf2[j3][:], ot2[j3][:],
                                                AF.Identity,
                                                scale=fb["rstd"][j3][:],
                                                bias=fb["nb"][j3][:]))
                        fep[(gq2, "lnf")] = ss.n
                        lnf_c = ss.n

                        @blk.vector
                        def _(v, j3=j3, lnf_c=lnf_c, xr_c=xr_c, dxr=dxr,
                              gq2=gq2):
                            v.wait_ge(ss.h, lnf_c)
                            v.wait_ge(dxr.h, xr_c)
                            if gq2 >= 2:
                                v.wait_ge(SM[f"d_out{j3}"].h, 16 * (gq2 // 2))
                            vs.inc(v.tensor_tensor(outt2[j3][:], lnf2[j3][:],
                                                   x2r2[j3][:], op=ALU.add))
                        fep[(gq2, "resid")] = vs.n
                        res_c = vs.n

                        @blk.gpsimd
                        def _(g, tt=tt, qc=qc, j3=j3, res_c=res_c):
                            g.wait_ge(vs.h, res_c)
                            r0 = tt * 512 + qc * 128
                            SM[f"d_out{j3}"].inc(g.dma_start(
                                out=out[r0:r0 + 128, :], in_=outt2[j3][:]), 16)

                barrier(extra=[(SM["d_out0"], SM["d_out0"].n),
                               (SM["d_out1"], SM["d_out1"].n)])

    return nc


_NC_CACHE = {}


def _get_nc(key, cfg):
    if key not in _NC_CACHE:
        _NC_CACHE[key] = build_nc(cfg)
    return _NC_CACHE[key]


def kernel(L0_attn_x, L0_ffn_W_in, L0_ffn_b_in, L0_ffn_W_out, L0_ffn_b_out):
    cfg = FULL_CFG
    x = np.ascontiguousarray(np.asarray(L0_attn_x))
    w_in_a = np.ascontiguousarray(np.asarray(L0_ffn_W_in))
    b_in_a = np.ascontiguousarray(np.asarray(L0_ffn_b_in)).reshape(1, -1)
    w_out_a = np.ascontiguousarray(np.asarray(L0_ffn_W_out))
    b_out_a = np.ascontiguousarray(np.asarray(L0_ffn_b_out)).reshape(1, -1)

    nc = _get_nc("full", cfg)
    T = cfg["T"]
    in_maps = []
    for c in range(cfg["NCORES"]):
        b, h = divmod(c, 2)
        xb_c = np.ascontiguousarray(np.roll(x[b], -h * T, axis=0))
        in_maps.append({"xb": xb_c, "w_in": w_in_a, "b_in": b_in_a,
                        "w_out": w_out_a, "b_out": b_out_a})
    res = run_bass_kernel_spmd(nc, in_maps, core_ids=list(range(cfg["NCORES"])))
    outs = [np.asarray(r["out"]) for r in res.results]
    full = np.concatenate(outs, axis=0).reshape(4, 2 * T, x.shape[2])
    return full.astype(np.float32, copy=False)
